# version 07 (final)
# baseline (speedup 1.0000x reference)
import contextlib
import numpy as np
import ml_dtypes
import concourse.bass as bass
import concourse.mybir as mybir
from concourse.bass_utils import run_bass_kernel_spmd

F32 = mybir.dt.float32
BF16 = mybir.dt.bfloat16
AF = mybir.ActivationFunctionType
ALU = mybir.AluOpType
AX = mybir.AxisListType

NCORES = 8
B, S, D = 2, 8192, 1024
T = B * S
TPC = T // NCORES
NCH = D // 128
FF = 4096
NFC = FF // 128
NH, HD = 16, 64
NE = 8
CAP = 2560
TT = 512
NTT = TPC // TT
TH = 1024
NTH = TPC // TH
TTH = TH // TT
LN_EPS = 1e-5
LN0_EPS = 1e-12
RG = [list(range(NCORES))]
BF = ml_dtypes.bfloat16


class Sem:
    def __init__(self, stack, nc, name):
        self.h = stack.enter_context(nc.semaphore(name))
        self.n = 0

    def inc(self, inst, k=1):
        inst.then_inc(self.h, k)
        self.n += k
        return self.n


class Prog:
    def __init__(self):
        self.nc = bass.Bass("TRN2", target_bir_lowering=False, debug=False)
        self.stack = contextlib.ExitStack()
        self.sems = {}
        self._consts = {}
        self.ps = [self.stack.enter_context(self.nc.psum_tensor(f"psf{i}", [128, TT], F32))
                   for i in range(7)]
        vals = [LN_EPS, LN0_EPS, -0.5, 0.5]
        arr = np.tile(np.asarray(vals, np.float32)[None, :], (128, 1))
        cst_d = self.nc.inline_tensor(np.ascontiguousarray(arr), "c_scalarpool").ap()
        self.cst = self.sbuf("cstpool", [128, len(vals)], F32)
        for i, v in enumerate(vals):
            self.nc.const_aps.register(F32, float(v), self.cst[:, i:i + 1])
        self.onesb = self.sbuf("c_onesb_sb", [128, 1], BF16)
        self.onesf = self.sbuf("c_onesf_sb", [1, 128], F32)
        self.onesbr = self.sbuf("c_onesbr_sb", [1, 128], BF16)
        self.identf = self.sbuf("c_identf_sb", [128, 128], F32)
        self.identb = self.sbuf("c_identb_sb", [128, 128], BF16)
        onesb_d = self.nc.inline_tensor(np.ones((128, 1), BF), "c_ones_128x1_b").ap()
        onesf_d = self.nc.inline_tensor(np.ones((1, 128), np.float32), "c_ones_1x128_f").ap()
        onesbr_d = self.nc.inline_tensor(np.ones((1, 128), BF), "c_ones_1x128_b").ap()
        identf_d = self.nc.inline_tensor(np.eye(128, dtype=np.float32), "c_ident_f").ap()
        identb_d = self.nc.inline_tensor(np.eye(128, dtype=BF), "c_ident_b").ap()
        self.s_const = self.sem("constld")
        with self.nc.Block() as blk:
            @blk.sync
            def _(e):
                self.s_const.inc(e.dma_start(out=self.cst[:], in_=cst_d[:, :]), 16)
                self.s_const.inc(e.dma_start(out=self.onesb[:], in_=onesb_d[:, :]), 16)
                self.s_const.inc(e.dma_start(out=self.onesf[:], in_=onesf_d[:, :]), 16)
                self.s_const.inc(e.dma_start(out=self.onesbr[:], in_=onesbr_d[:, :]), 16)
                self.s_const.inc(e.dma_start(out=self.identf[:], in_=identf_d[:, :]), 16)
                self.s_const.inc(e.dma_start(out=self.identb[:], in_=identb_d[:, :]), 16)

    def sem(self, name):
        if name not in self.sems:
            self.sems[name] = Sem(self.stack, self.nc, name)
        return self.sems[name]

    def sem0(self, *names):
        return tuple(self.sem(n).n for n in names)

    def sbuf(self, name, shape, dtype):
        return self.stack.enter_context(self.nc.sbuf_tensor(name, shape, dtype, side="right"))

    def const(self, name, arr):
        if name not in self._consts:
            self._consts[name] = self.nc.inline_tensor(np.ascontiguousarray(arr), name).ap()
        return self._consts[name]


def chained(eng, sem):
    def step(inst):
        sem.inc(inst)
        eng.wait_ge(sem.h, sem.n)
    return step


def emit_ln(P, tag, x_dram, xb, xln, sq, rows, tmp, eps=LN_EPS, gate=None, affine=None, vec_wait=None):
    nc = P.nc
    r_sx, r_sq, r_mean, r_var, r_a, r_b = rows
    c_onesb = P.onesb
    c_onesf = P.onesf

    s_ld = [P.sem(f"ln_ld{c}") for c in range(NCH)]
    s_sq = P.sem("ln_sq")
    s_mm = P.sem("ln_mm")
    s_ab = P.sem("ln_ab")
    s_act = P.sem("ln_act")
    s_bc = P.sem("ln_bc")
    s_out = P.sem("ln_out")
    ld0 = [P.sems[f"ln_ld{c}"].n for c in range(NCH)]
    sq0, mm0, ab0, act0, bc0, out0 = P.sem0(
        "ln_sq", "ln_mm", "ln_ab", "ln_act", "ln_bc", "ln_out")

    ps_sx = [P.ps[0][0:1, :], P.ps[2][0:1, :]]
    ps_sq = [P.ps[1][0:1, :], P.ps[3][0:1, :]]
    ps_a = P.ps[4]
    ps_b = P.ps[5]

    with nc.Block() as blk:
        @blk.sync
        def _(e):
            pass

        @blk.gpsimd
        def _(e):
            if gate is not None:
                for gs, gv in gate:
                    e.wait_ge(gs.h, gv)
            for c in range(NCH):
                s_ld[c].inc(e.dma_start(out=xb[c][:], in_=x_dram[128 * c:128 * (c + 1), :]), 16)

        @blk.vector
        def _(e):
            if vec_wait is not None:
                for gs, gv in vec_wait:
                    e.wait_ge(gs.h, gv)
            st = chained(e, s_ab)

            def evac(t):
                e.wait_ge(s_mm.h, mm0 + NCH * (t + 1))
                st(e.tensor_copy(out=r_sx[0:1, TT * t:TT * (t + 1)], in_=ps_sx[t % 2]))
                st(e.tensor_copy(out=r_sq[0:1, TT * t:TT * (t + 1)], in_=ps_sq[t % 2]))

            k = 0
            for t in range(NTT):
                for c in range(NCH):
                    e.wait_ge(s_ld[c].h, ld0[c] + 16)
                    if k >= 2:
                        e.wait_ge(s_mm.h, mm0 + k - 1)
                    s_sq.inc(e.tensor_tensor(out=sq[:, TT * (k % 2):TT * (k % 2 + 1)],
                                             in0=xb[c][:, TT * t:TT * (t + 1)],
                                             in1=xb[c][:, TT * t:TT * (t + 1)], op=ALU.mult))
                    e.wait_ge(s_sq.h, s_sq.n)
                    k += 1
                if t >= 1:
                    evac(t - 1)
            evac(NTT - 1)
            st(e.tensor_scalar_mul(out=r_mean[:], in0=r_sx[:], scalar1=1.0 / D))
            st(e.tensor_scalar_mul(out=r_var[:], in0=r_sq[:], scalar1=1.0 / D))
            st(e.tensor_tensor(out=r_sx[:], in0=r_mean[:], in1=r_mean[:], op=ALU.mult))
            st(e.tensor_tensor(out=r_var[:], in0=r_var[:], in1=r_sx[:], op=ALU.subtract))
            e.wait_ge(s_act.h, act0 + 2)
            st(e.scalar_tensor_tensor(out=r_b[:], in0=r_mean[:], scalar=-1.0, in1=r_a[:],
                                      op0=ALU.mult, op1=ALU.mult))
            for t in range(NTT):
                e.wait_ge(s_bc.h, bc0 + 2 * (t + 1))
                for c in range(NCH):
                    s_out.inc(e.tensor_tensor(out=tmp[:, 0:TT], in0=xb[c][:, TT * t:TT * (t + 1)],
                                              in1=ps_a[:, :], op=ALU.mult))
                    e.wait_ge(s_out.h, s_out.n)
                    if affine is None:
                        s_out.inc(e.tensor_tensor(out=xln[c][:, TT * t:TT * (t + 1)], in0=tmp[:, 0:TT],
                                                  in1=ps_b[:, :], op=ALU.add))
                        e.wait_ge(s_out.h, s_out.n)
                    else:
                        g_t, be_t = affine
                        s_out.inc(e.tensor_tensor(out=tmp[:, TT:2 * TT], in0=tmp[:, 0:TT],
                                                  in1=ps_b[:, :], op=ALU.add))
                        e.wait_ge(s_out.h, s_out.n)
                        s_out.inc(e.tensor_scalar(out=xln[c][:, TT * t:TT * (t + 1)], in0=tmp[:, TT:2 * TT],
                                                  scalar1=g_t[:, c:c + 1], scalar2=be_t[:, c:c + 1],
                                                  op0=ALU.mult, op1=ALU.add))
                        e.wait_ge(s_out.h, s_out.n)

        @blk.scalar
        def _(e):
            e.wait_ge(P.s_const.h, 96)
            e.wait_ge(s_ab.h, ab0 + 2 * NTT + 4)
            s_act.inc(e.activation(out=r_sq[:], in_=r_var[:], func=AF.Ln, bias=float(eps)))
            e.wait_ge(s_act.h, act0 + 1)
            s_act.inc(e.activation(out=r_a[:], in_=r_sq[:], func=AF.Exp, scale=-0.5))

        @blk.tensor
        def _(e):
            e.wait_ge(P.s_const.h, 96)
            k = 0
            for t in range(NTT):
                if t >= 2:
                    e.wait_ge(s_ab.h, ab0 + 2 * (t - 1))
                for c in range(NCH):
                    k += 1
                    e.wait_ge(s_sq.h, sq0 + k)
                    last = c == NCH - 1
                    e.matmul(ps_sx[t % 2], c_onesb[:], xb[c][:, TT * t:TT * (t + 1)],
                             start=(c == 0), stop=last)
                    s_mm.inc(e.matmul(ps_sq[t % 2], c_onesb[:],
                                      sq[:, TT * ((k - 1) % 2):TT * ((k - 1) % 2 + 1)],
                                      start=(c == 0), stop=last))
            for t in range(NTT):
                if t == 0:
                    e.wait_ge(s_ab.h, s_ab.n)
                else:
                    e.wait_ge(s_out.h, out0 + (3 if affine is not None else 2) * NCH * t)
                s_bc.inc(e.matmul(ps_a[:, :], c_onesf[:], r_a[0:1, TT * t:TT * (t + 1)],
                                  start=True, stop=True))
                s_bc.inc(e.matmul(ps_b[:, :], c_onesf[:], r_b[0:1, TT * t:TT * (t + 1)],
                                  start=True, stop=True))

    return s_out


def emit_moe2(P, tag, x_dram, y_dram, rwt_d, wg_d, wu_d, wd_d):
    nc = P.nc
    with contextlib.ExitStack() as sc_main:
        sbm = lambda n, s, d: sc_main.enter_context(nc.sbuf_tensor(f"{tag}_{n}", s, d))
        xln = [sbm(f"xln{c}", [128, TPC], BF16) for c in range(NCH)]
        acc = [sbm(f"acc{c}", [128, TPC], F32) for c in range(NCH)]
        ps8 = sc_main.enter_context(nc.psum_tensor(f"{tag}_ps8", [128, TT], F32))

        with contextlib.ExitStack() as sc_ln:
            sbl = lambda n, s, d: sc_ln.enter_context(nc.sbuf_tensor(f"{tag}_{n}", s, d))
            xb = [sbl(f"xb{c}", [128, TPC], BF16) for c in range(NCH)]
            sq = sbl("sq", [128, 2 * TT], BF16)
            rows = [sbl(f"r{i}", [1, TPC], F32) for i in range(6)]
            tmpf = sbl("tmpf", [128, 2 * TT], F32)
            s_ln = emit_ln(P, tag + "_ln", x_dram, xb, xln, sq, rows, tmpf)
            ln_done = s_ln.n

        with contextlib.ExitStack() as sc:
            sb = lambda n, s, d: sc.enter_context(nc.sbuf_tensor(f"{tag}_{n}", s, d))
            rwt_sb = sb("rwt", [128, NCH * NE], BF16)
            lg = sb("lg", [128, NE], F32)
            negm = sb("negm", [128, 1], F32)
            e_ = sb("e_", [128, NE], F32)
            ge1 = sb("ge1", [128, NE], F32)
            em = sb("em", [128, NE], F32)
            m2 = sb("m2", [128, 1], F32)
            den = sb("den", [128, 1], F32)
            rden = sb("rden", [128, 1], F32)
            gem = sb("gem", [128, NE], F32)
            cwf = sb("cwf", [128, NE], F32)
            cwt = sb("cwt", [128, NE], F32)
            cwT = sb("cwT", [8, TPC], BF16)
            cwrow = sb("cwrow", [1, TPC], BF16)

            s_rld = P.sem("rt_ld")
            s_rmm = P.sem("rt_mm")
            s_rv = P.sem("rt_v")
            s_ra = P.sem("rt_a")
            s_rt = P.sem("rt_t")
            s_rc = P.sem("rt_c")
            s_row = P.sem("rt_row")
            rld0, rmm0, rv0, ra0, rt0, rc0, row0 = P.sem0(
                "rt_ld", "rt_mm", "rt_v", "rt_a", "rt_t", "rt_c", "rt_row")

            NTI = TPC // 128
            ps_r = P.ps[6]
            ps_t = ps8
            VC = 10

            NGF = 2
            NG = NFC // NGF
            wgu_b = [sb(f"wgu{p}", [128, 2 * NCH * NGF * 128], BF16) for p in range(2)]
            wd_b = [sb(f"wdq{p}", [128, 8 * 1024], BF16) for p in range(2)]
            h_sb = sb("h", [128, 16 * TH], BF16)
            silu_b = sb("silu", [128, 2 * TT], BF16)
            tsl_b = sb("tsl", [128, 2 * TT], BF16)
            cwb = sb("cwb", [128, TPC], BF16)

            s_w = [P.sem("m2_wA"), P.sem("m2_wB")]
            s_wd = [P.sem("m2_wdA"), P.sem("m2_wdB")]
            s_cbm = P.sem("m2_cbm")
            s_cbe = P.sem("m2_cbe")
            s_g = P.sem("m2_g")
            s_u = P.sem("m2_u")
            s_sl = P.sem("m2_sl")
            s_h = P.sem("m2_h")
            s_o = P.sem("m2_o")
            s_oc = P.sem("m2_oc")
            w0 = [P.sems["m2_wA"].n, P.sems["m2_wB"].n]
            wdd0 = [P.sems["m2_wdA"].n, P.sems["m2_wdB"].n]
            cbm0, cbe0, g0, u0, sl0, h0, o0, oc0 = P.sem0(
                "m2_cbm", "m2_cbe", "m2_g", "m2_u", "m2_sl", "m2_h", "m2_o", "m2_oc")

            psG = [P.ps[0], P.ps[1]]
            psU = [P.ps[2], P.ps[3]]
            PSO = P.ps[:7] + [ps8]

            ETH = [(ee, th) for ee in range(NE) for th in range(NTH)]

            with nc.Block() as blk:
                @blk.gpsimd
                def _(e):
                    e.wait_ge(s_ln.h, ln_done)
                    s_rld.inc(e.dma_start(out=rwt_sb[:].rearrange("p (c ee) -> p c ee", c=NCH),
                                          in_=rwt_d.rearrange("(c p) ee -> p c ee", p=128)), 16)

                @blk.tensor
                def _(e):
                    e.wait_ge(s_rld.h, rld0 + 16)
                    e.wait_ge(P.s_const.h, 96)
                    def logits(i):
                        if i >= 1:
                            e.wait_ge(s_rv.h, rv0 + VC * (i - 1) + 1)
                        for c in range(NCH):
                            ii = e.matmul(ps_r[:, 0:NE], xln[c][:, 128 * i:128 * (i + 1)],
                                          rwt_sb[:, NE * c:NE * (c + 1)],
                                          start=(c == 0), stop=(c == NCH - 1))
                            if c == NCH - 1:
                                s_rmm.inc(ii)

                    def transp(i):
                        e.wait_ge(s_rv.h, rv0 + VC * (i + 1))
                        if i >= 1:
                            e.wait_ge(s_rc.h, rc0 + i)
                        s_rt.inc(e.transpose(ps_t[0:8, 0:128], cwt[:], P.identf[:]))

                    logits(0)
                    for i in range(1, NTI):
                        logits(i)
                        transp(i - 1)
                    transp(NTI - 1)

                @blk.vector
                def _(e):
                    st = chained(e, s_rv)
                    for i in range(NTI):
                        e.wait_ge(s_rmm.h, rmm0 + i + 1)
                        st(e.tensor_copy(out=lg[:], in_=ps_r[:, 0:NE]))
                        st(e.reduce_max(out=negm[:], in_=lg[:], axis=AX.X, negate=True))
                        e.wait_ge(s_ra.h, ra0 + i + 1)
                        st(e.tensor_single_scalar(out=ge1[:], in_=e_[:], scalar=1.0, op=ALU.is_ge))
                        st(e.scalar_tensor_tensor(out=em[:], in0=ge1[:], scalar=-2e30, in1=e_[:],
                                                  op0=ALU.mult, op1=ALU.add))
                        st(e.reduce_max(out=m2[:], in_=em[:], axis=AX.X))
                        st(e.tensor_scalar_add(out=den[:], in0=m2[:], scalar1=1.0))
                        st(e.reciprocal(out=rden[:], in_=den[:]))
                        st(e.tensor_single_scalar(out=gem[:], in_=e_[:], scalar=m2[:, 0:1], op=ALU.is_ge))
                        st(e.tensor_tensor(out=cwf[:], in0=e_[:], in1=gem[:], op=ALU.mult))
                        st(e.tensor_scalar_mul(out=cwt[:], in0=cwf[:], scalar1=rden[:, 0:1]))
                        e.wait_ge(s_rt.h, rt0 + i + 1)
                        s_rc.inc(e.tensor_copy(out=cwT[:, 128 * i:128 * (i + 1)], in_=ps_t[0:8, 0:128]))
                        e.wait_ge(s_rc.h, s_rc.n)

                @blk.scalar
                def _(e):
                    for i in range(NTI):
                        e.wait_ge(s_rv.h, rv0 + VC * i + 2)
                        s_ra.inc(e.activation(out=e_[:], in_=lg[:], func=AF.Exp, bias=negm[:, 0:1]))

            with nc.Block() as blk:
                @blk.sync
                def _(e):
                    for ee in range(NE):
                        if ee > 0:
                            e.wait_ge(s_cbm.h, cbm0 + NTT * ee)
                        s_row.inc(e.dma_start(out=cwrow[:], in_=cwT[ee:ee + 1, :]), 16)

                @blk.gpsimd
                def _(e):
                    for ee, th in ETH:
                        ei = ee * NTH + th
                        for hp in range(2):
                            for g8 in range(NG // 2):
                                g = hp * (NG // 2) + g8
                                gi = ei * NG + g
                                if gi >= 2:
                                    e.wait_ge(s_u.h, u0 + (gi - 1) * NGF * TTH)
                                fsl = slice(NGF * 128 * g, NGF * 128 * (g + 1))
                                W = NGF * 128
                                s_w[gi % 2].inc(e.dma_start(
                                    out=wgu_b[gi % 2][:, 0:NCH * W].rearrange("p (c f) -> p c f", c=NCH),
                                    in_=wg_d[ee].rearrange("(c p) f -> p c f", p=128)[:, :, fsl]), 16)
                                s_w[gi % 2].inc(e.dma_start(
                                    out=wgu_b[gi % 2][:, NCH * W:].rearrange("p (c f) -> p c f", c=NCH),
                                    in_=wu_d[ee].rearrange("(c p) f -> p c f", p=128)[:, :, fsl]), 16)
                            for q in range(2):
                                qi = (ei * 2 + hp) * 2 + q
                                if qi >= 2:
                                    e.wait_ge(s_o.h, o0 + 16 * (qi - 1))
                                s_wd[qi % 2].inc(e.dma_start(
                                    out=wd_b[qi % 2][:].rearrange("p (fc o) -> p fc o", fc=8),
                                    in_=wd_d[ee].rearrange("(fc p) o -> p fc o", p=128)[:, 16 * hp + 8 * q:16 * hp + 8 * (q + 1), :]), 16)

                @blk.tensor
                def _(e):
                    gstage = 0
                    dstage = 0
                    for ee, th in ETH:
                        ei = ee * NTH + th
                        if th == 0:
                            e.wait_ge(s_oc.h, oc0 + 8 * dstage)
                            for t in range(NTT):
                                ci = NTT * ee + t
                                e.wait_ge(s_row.h, row0 + 16 * (ee + 1))
                                if ci >= 1:
                                    e.wait_ge(s_cbe.h, cbe0 + ci)
                                s_cbm.inc(e.matmul(ps_r[:, :], P.onesbr[:], cwrow[0:1, TT * t:TT * (t + 1)],
                                                   start=True, stop=True))
                        for hp in range(2):
                            if ei * 2 + hp >= 1:
                                e.wait_ge(s_oc.h, oc0 + 8 * dstage)
                            for g in range(NG // 2):
                                gi = ei * NG + hp * (NG // 2) + g
                                e.wait_ge(s_w[gi % 2].h, w0[gi % 2] + 32 * (gi // 2 + 1))
                                if hp == 0 and g == 0:
                                    e.wait_ge(s_cbe.h, cbe0 + NTT * (ee + 1))
                                for f2 in range(NGF):
                                    fc = (hp * (NG // 2) + g) * NGF + f2
                                    W = NGF * 128
                                    for tt in range(TTH):
                                        spar = gstage % 2
                                        if gstage >= 2:
                                            e.wait_ge(s_sl.h, sl0 + gstage - 1)
                                            e.wait_ge(s_h.h, h0 + 2 * (gstage - 1))
                                        col = TH * th + TT * tt
                                        for dk in range(NCH):
                                            ii = e.matmul(psG[spar][:, :],
                                                          wgu_b[gi % 2][:, W * dk + 128 * f2:W * dk + 128 * f2 + 128],
                                                          xln[dk][:, col:col + TT],
                                                          start=(dk == 0), stop=(dk == NCH - 1))
                                            if dk == NCH - 1:
                                                s_g.inc(ii)
                                        for dk in range(NCH):
                                            ii = e.matmul(psU[spar][:, :],
                                                          wgu_b[gi % 2][:, NCH * W + W * dk + 128 * f2:NCH * W + W * dk + 128 * f2 + 128],
                                                          xln[dk][:, col:col + TT],
                                                          start=(dk == 0), stop=(dk == NCH - 1))
                                            if dk == NCH - 1:
                                                s_u.inc(ii)
                                        gstage += 1
                            e.wait_ge(s_h.h, h0 + 2 * gstage)
                            e.wait_ge(s_sl.h, sl0 + gstage)
                            for q in range(2):
                                qi = (ei * 2 + hp) * 2 + q
                                e.wait_ge(s_wd[qi % 2].h, wdd0[qi % 2] + 16 * (qi // 2 + 1))
                                for tt in range(TTH):
                                    if dstage >= 1:
                                        e.wait_ge(s_oc.h, oc0 + 8 * dstage)
                                    col = TT * tt
                                    for fcl in range(8):
                                        fh = q * 8 + fcl
                                        for do in range(NCH):
                                            ii = e.matmul(PSO[do][:, :],
                                                          wd_b[qi % 2][:, 1024 * fcl + 128 * do:1024 * fcl + 128 * do + 128],
                                                          h_sb[:, TH * fh + col:TH * fh + col + TT],
                                                          start=(fcl == 0), stop=(fcl == 7))
                                            if fcl == 7:
                                                s_o.inc(ii)
                                    dstage += 1

                @blk.scalar
                def _(e):
                    gstage = 0
                    for ee, th in ETH:
                        for fc in range(NFC):
                            for tt in range(TTH):
                                e.wait_ge(s_g.h, g0 + gstage + 1)
                                spar = gstage % 2
                                if gstage >= 2:
                                    e.wait_ge(s_h.h, h0 + 2 * (gstage - 1))
                                s_sl.inc(e.activation(out=silu_b[:, TT * spar:TT * (spar + 1)],
                                                      in_=psG[spar][:, :], func=AF.Silu))
                                gstage += 1

                @blk.vector
                def _(e):
                    gstage = 0
                    dstage = 0
                    for ee, th in ETH:
                        if th == 0:
                            for t in range(NTT):
                                ci = NTT * ee + t
                                e.wait_ge(s_cbm.h, cbm0 + ci + 1)
                                s_cbe.inc(e.tensor_copy(out=cwb[:, TT * t:TT * (t + 1)], in_=ps_r[:, :]))
                                e.wait_ge(s_cbe.h, s_cbe.n)
                        for hp in range(2):
                            for fcpos in range(16):
                                fh = fcpos
                                for tt in range(TTH):
                                    spar = gstage % 2
                                    e.wait_ge(s_sl.h, sl0 + gstage + 1)
                                    e.wait_ge(s_u.h, u0 + gstage + 1)
                                    col = TH * th + TT * tt
                                    s_h.inc(e.tensor_tensor(out=tsl_b[:, TT * spar:TT * (spar + 1)],
                                                            in0=silu_b[:, TT * spar:TT * (spar + 1)],
                                                            in1=psU[spar][:, :], op=ALU.mult))
                                    e.wait_ge(s_h.h, s_h.n)
                                    s_h.inc(e.tensor_tensor(out=h_sb[:, TH * fh + TT * tt:TH * fh + TT * tt + TT],
                                                            in0=tsl_b[:, TT * spar:TT * (spar + 1)],
                                                            in1=cwb[:, col:col + TT], op=ALU.mult))
                                    e.wait_ge(s_h.h, s_h.n)
                                    gstage += 1
                            first = (ee == 0 and hp == 0)
                            for q in range(2):
                                for tt in range(TTH):
                                    col = TH * th + TT * tt
                                    for do in range(NCH):
                                        e.wait_ge(s_o.h, o0 + 8 * dstage + do + 1)
                                        if first and q == 0:
                                            s_oc.inc(e.tensor_copy(out=acc[do][:, col:col + TT], in_=PSO[do][:, :]))
                                        else:
                                            s_oc.inc(e.tensor_tensor(out=acc[do][:, col:col + TT],
                                                                     in0=acc[do][:, col:col + TT],
                                                                     in1=PSO[do][:, :], op=ALU.add))
                                        e.wait_ge(s_oc.h, s_oc.n)
                                    dstage += 1

            oc_total = P.sems["m2_oc"].n

        with contextlib.ExitStack() as sc_f:
            sbf = lambda n, s, d: sc_f.enter_context(nc.sbuf_tensor(f"{tag}_{n}", s, d))
            xf2 = [sbf(f"xf2{p}", [128, TPC], F32) for p in range(2)]
            yf = [sbf(f"yf{p}", [128, TPC], F32) for p in range(2)]
            s_fl = [P.sem("fin_lA"), P.sem("fin_lB")]
            s_fy = P.sem("fin_y")
            s_fs = [P.sem("fin_sA"), P.sem("fin_sB")]
            fl0 = [P.sems["fin_lA"].n, P.sems["fin_lB"].n]
            fs0 = [P.sems["fin_sA"].n, P.sems["fin_sB"].n]
            fy0 = P.sems["fin_y"].n
            s_oc_f = P.sem("m2_oc")

            with nc.Block() as blk:
                @blk.sync
                def _(e):
                    e.wait_ge(s_oc_f.h, oc_total)
                    for c in range(NCH):
                        if c >= 2:
                            e.wait_ge(s_fy.h, fy0 + c - 1)
                        s_fl[c % 2].inc(e.dma_start(out=xf2[c % 2][:], in_=x_dram[128 * c:128 * (c + 1), :]), 16)

                @blk.vector
                def _(e):
                    for c in range(NCH):
                        e.wait_ge(s_fl[c % 2].h, fl0[c % 2] + 16 * (c // 2 + 1))
                        if c >= 2:
                            e.wait_ge(s_fs[c % 2].h, fs0[c % 2] + 16 * (c // 2))
                        s_fy.inc(e.scalar_tensor_tensor(out=yf[c % 2][:], in0=acc[c][:], scalar=0.5,
                                                        in1=xf2[c % 2][:], op0=ALU.mult, op1=ALU.add))
                        e.wait_ge(s_fy.h, s_fy.n)

                @blk.gpsimd
                def _(e):
                    for c in range(NCH):
                        e.wait_ge(s_fy.h, fy0 + c + 1)
                        s_fs[c % 2].inc(e.dma_start(out=y_dram[128 * c:128 * (c + 1), :], in_=yf[c % 2][:]), 16)
                    e.wait_ge(s_fs[0].h, s_fs[0].n)
                    e.wait_ge(s_fs[1].h, s_fs[1].n)

    return P.sem("fin_sA")


_CACHE = {}


def build_moe2_prog():
    P = Prog()
    nc = P.nc
    x = nc.dram_tensor("x", [D, TPC], F32, kind="ExternalInput").ap()
    rwt = nc.dram_tensor("rwt", [D, NE], F32, kind="ExternalInput").ap()
    wg = nc.dram_tensor("wg", [NE, D, FF], F32, kind="ExternalInput").ap()
    wu = nc.dram_tensor("wu", [NE, D, FF], F32, kind="ExternalInput").ap()
    wd = nc.dram_tensor("wd", [NE, FF, D], F32, kind="ExternalInput").ap()
    y = nc.dram_tensor("y", [D, TPC], F32, kind="ExternalOutput").ap()
    emit_moe2(P, "m2", x, y, rwt, wg, wu, wd)
    return P


def run_moe2(x_fm_shards, router_w, w_gate, w_up, w_down):
    if "moe2" not in _CACHE:
        _CACHE["moe2"] = build_moe2_prog()
    P = _CACHE["moe2"]
    rwt = np.ascontiguousarray(router_w.T.astype(np.float32))
    in_maps = [{"x": np.ascontiguousarray(xs), "rwt": rwt,
                "wg": w_gate, "wu": w_up, "wd": w_down} for xs in x_fm_shards]
    res = run_bass_kernel_spmd(P.nc, in_maps, core_ids=list(range(NCORES)))
    return [r["y"] for r in res.results]




def emit_moe1(P, tag, x_dram, y_dram, gwt_d, gbr_d, w1_d, b1_d, w2_d, b2_d,
              premask_d, t16_h, gt_h):
    nc = P.nc
    with contextlib.ExitStack() as sc_main:
        sbm = lambda n, s, d: sc_main.enter_context(nc.sbuf_tensor(f"{tag}_{n}", s, d))
        xln = [sbm(f"xln{c}", [128, TPC], BF16) for c in range(NCH)]
        acc = [sbm(f"acc{c}", [128, TPC], F32) for c in range(NCH)]
        ps8 = sc_main.enter_context(nc.psum_tensor(f"{tag}_ps8", [128, TT], F32))

        with contextlib.ExitStack() as sc_ln:
            sbl = lambda n, s, d: sc_ln.enter_context(nc.sbuf_tensor(f"{tag}_{n}", s, d))
            xb = [sbl(f"xb{c}", [128, TPC], BF16) for c in range(NCH)]
            sq = sbl("sq", [128, 2 * TT], BF16)
            rows = [sbl(f"r{i}", [1, TPC], F32) for i in range(6)]
            tmpf = sbl("tmpf", [128, 2 * TT], F32)
            s_ln = emit_ln(P, tag + "_ln", x_dram, xb, xln, sq, rows, tmpf)
            ln_done = s_ln.n

        with contextlib.ExitStack() as sc:
            sb = lambda n, s, d: sc.enter_context(nc.sbuf_tensor(f"{tag}_{n}", s, d))
            rwt_sb = sb("rwt", [128, NCH * NE], BF16)
            gb_sb = sb("gb", [1, NE], BF16)
            gbbc = sb("gbbc", [128, NE], F32)
            premask = sb("premask", [128, 16], BF16)
            triu_sb = sb("triu", [128, 128], BF16)
            lg = sb("lg", [128, NE], F32)
            negm = sb("negm", [128, 1], F32)
            e_ = sb("e_", [128, NE], F32)
            esum = sb("esum", [128, 1], F32)
            tp_all = sb("tp_all", [128, 16], F32)
            oh_all = sb("oh_all", [128, 16 * NE], F32)
            ohb = sb("ohb", [128, NE], BF16)
            inc_all = sb("incall", [128, 16 * NE], F32)
            tot16 = sb("tot16", [16, NE], F32)
            gtb = sb("gtb", [128, NE], BF16)
            offs16 = sb("offs16", [16, NE], F32)
            offrow = sb("offrow", [1, NE], F32)
            offbc = sb("offbc", [128, NE], F32)
            ra_ = sb("ra_", [128, NE], F32)
            krank = sb("krank", [128, 1], F32)
            keep = sb("keep", [128, 1], F32)
            tpk = sb("tpk", [128, 1], F32)
            cwt = sb("cwt", [128, NE], F32)
            cwT = sb("cwT", [8, TPC], BF16)
            cwrow = sb("cwrow", [1, TPC], BF16)

            s_rld = P.sem("rt_ld")
            s_rmm = P.sem("rt_mm")
            s_rv = P.sem("rt_v")
            s_ra = P.sem("rt_a")
            s_rt = P.sem("rt_t")
            s_rc = P.sem("rt_c")
            s_row = P.sem("rt_row")
            s_cs = P.sem("r1_cs")
            s_ce = P.sem("r1_ce")
            s_tt = P.sem("r1_tt")
            s_ag = P.sem("r1_ag")
            s_of = P.sem("r1_of")
            s_obd = P.sem("r1_obd")
            s_obm = P.sem("r1_obm")
            rld0, rmm0, rv0, ra0, rt0, rc0, row0, cs0, ce0, tt0, ag0, of0, obd0, obm0 = P.sem0(
                "rt_ld", "rt_mm", "rt_v", "rt_a", "rt_t", "rt_c", "rt_row",
                "r1_cs", "r1_ce", "r1_tt", "r1_ag", "r1_of", "r1_obd", "r1_obm")

            NTI = TPC // 128
            ps_r = P.ps[6]
            psc = [P.ps[4], P.ps[5]]
            ps_t = ps8
            VC1 = 5

            with nc.Block() as blk:
                @blk.sync
                def _(e):
                    e.wait_ge(s_ln.h, ln_done)
                    s_rld.inc(e.dma_start(out=premask[:], in_=premask_d[:, :]), 16)
                    s_rld.inc(e.dma_start(
                        out=triu_sb[:],
                        in_=P.const("c_triu_b", np.triu(np.ones((128, 128), np.float32)).astype(BF))[:, :]), 16)
                    for i in range(NTI):
                        e.wait_ge(s_ce.h, ce0 + i + 1)
                        s_tt.inc(e.dma_start(out=tot16[i:i + 1, :], in_=inc_all[127:128, NE * i:NE * (i + 1)]), 16)

                @blk.gpsimd
                def _(e):
                    e.wait_ge(s_ln.h, ln_done)
                    s_rld.inc(e.dma_start(out=rwt_sb[:].rearrange("p (c ee) -> p c ee", c=NCH),
                                          in_=gwt_d.rearrange("(c p) ee -> p c ee", p=128)), 16)
                    s_rld.inc(e.dma_start(out=gb_sb[:], in_=gbr_d[:, :]), 16)
                    e.wait_ge(s_tt.h, tt0 + 16 * NTI)
                    s_ag.inc(e.dma_start(out=t16_h.ap()[:, :], in_=tot16[:]), 16)
                    e.wait_ge(s_ag.h, ag0 + 16)
                    s_ag.inc(e.collective_compute(
                        "AllGather", ALU.bypass, replica_groups=RG,
                        ins=[t16_h.ap().opt()], outs=[gt_h.ap().opt()]), 1)
                    e.wait_ge(s_ag.h, ag0 + 17)
                    s_ag.inc(e.dma_start(out=gtb[:], in_=gt_h.ap()[:, :]), 16)

                @blk.tensor
                def _(e):
                    e.wait_ge(s_rld.h, rld0 + 64)
                    e.wait_ge(P.s_const.h, 96)
                    s_rmm.inc(e.matmul(ps_r[:, 0:NE], P.onesbr[:], gb_sb[:], start=True, stop=True))

                    def logits(i):
                        if i >= 1:
                            e.wait_ge(s_rv.h, rv0 + 5 * i - 3)
                        else:
                            e.wait_ge(s_rv.h, rv0 + 1)
                        for c in range(NCH):
                            ii = e.matmul(ps_r[:, 0:NE], xln[c][:, 128 * i:128 * (i + 1)],
                                          rwt_sb[:, NE * c:NE * (c + 1)],
                                          start=(c == 0), stop=(c == NCH - 1))
                            if c == NCH - 1:
                                s_rmm.inc(ii)

                    def cumsum(i):
                        e.wait_ge(s_rv.h, rv0 + 5 * i + 5)
                        if i >= 2:
                            e.wait_ge(s_ce.h, ce0 + i - 1)
                        s_cs.inc(e.matmul(psc[i % 2][:, 0:NE], triu_sb[:], ohb[:],
                                          start=True, stop=True))

                    logits(0)
                    for i in range(1, NTI):
                        logits(i)
                        cumsum(i - 1)
                    cumsum(NTI - 1)

                @blk.vector
                def _(e):
                    st = chained(e, s_rv)
                    e.wait_ge(s_rmm.h, rmm0 + 1)
                    st(e.tensor_copy(out=gbbc[:], in_=ps_r[:, 0:NE]))
                    for i in range(NTI):
                        e.wait_ge(s_rmm.h, rmm0 + i + 2)
                        st(e.tensor_tensor(out=lg[:], in0=ps_r[:, 0:NE], in1=gbbc[:], op=ALU.add))
                        st(e.reduce_max(out=negm[:], in_=lg[:], axis=AX.X, negate=True))
                        e.wait_ge(s_ra.h, ra0 + i + 1)
                        st(e.tensor_single_scalar(out=oh_all[:, NE * i:NE * (i + 1)], in_=e_[:],
                                                  scalar=1.0, op=ALU.is_ge))
                        st(e.tensor_copy(out=ohb[:], in_=oh_all[:, NE * i:NE * (i + 1)]))
                        st(e.reciprocal(out=tp_all[:, i:i + 1], in_=esum[:]))
                        e.wait_ge(s_cs.h, cs0 + i + 1)
                        s_ce.inc(e.tensor_copy(out=inc_all[:, NE * i:NE * (i + 1)],
                                               in_=psc[i % 2][:, 0:NE]))
                        e.wait_ge(s_ce.h, s_ce.n)

                @blk.scalar
                def _(e):
                    for i in range(NTI):
                        e.wait_ge(s_rv.h, rv0 + 5 * i + 3)
                        s_ra.inc(e.activation(out=e_[:], in_=lg[:], func=AF.Exp, bias=negm[:, 0:1],
                                              accum_out=esum[:, 0:1]))

            rvB0 = P.sems["rt_v"].n
            VC2 = 6
            with nc.Block() as blk:
                @blk.sync
                def _(e):
                    for i in range(NTI):
                        e.wait_ge(s_of.h, of0 + 2)
                        if i >= 1:
                            e.wait_ge(s_obm.h, obm0 + i)
                        s_obd.inc(e.dma_start(out=offrow[:], in_=offs16[i:i + 1, :]), 16)

                @blk.tensor
                def _(e):
                    e.wait_ge(s_ag.h, ag0 + 33)
                    s_of.inc(e.matmul(ps_r[0:16, 0:NE], premask[:], gtb[:], start=True, stop=True))
                    for i in range(NTI):
                        e.wait_ge(s_obd.h, obd0 + 16 * (i + 1))
                        if i >= 2:
                            e.wait_ge(s_rv.h, rvB0 + 6 * i - 10)
                        s_obm.inc(e.matmul(psc[i % 2][:, 0:NE], P.onesf[:], offrow[:],
                                           start=True, stop=True))
                        e.wait_ge(s_rv.h, rvB0 + 1 + VC2 * i + 6)
                        if i >= 1:
                            e.wait_ge(s_rc.h, rc0 + i)
                        s_rt.inc(e.transpose(ps_t[0:8, 0:128], cwt[:], P.identf[:]))

                @blk.vector
                def _(e):
                    st = chained(e, s_rv)
                    e.wait_ge(s_of.h, of0 + 1)
                    st(e.tensor_copy(out=offs16[:], in_=ps_r[0:16, 0:NE]))
                    s_of.inc(e.engine_nop())
                    for i in range(NTI):
                        e.wait_ge(s_obm.h, obm0 + i + 1)
                        st(e.tensor_tensor(out=offbc[:], in0=inc_all[:, NE * i:NE * (i + 1)],
                                           in1=psc[i % 2][:, 0:NE], op=ALU.add))
                        st(e.tensor_tensor(out=ra_[:], in0=oh_all[:, NE * i:NE * (i + 1)],
                                           in1=offbc[:], op=ALU.mult))
                        st(e.reduce_sum(out=krank[:], in_=ra_[:], axis=AX.X))
                        st(e.tensor_single_scalar(out=keep[:], in_=krank[:], scalar=CAP + 0.5, op=ALU.is_lt))
                        st(e.tensor_tensor(out=tpk[:], in0=tp_all[:, i:i + 1], in1=keep[:], op=ALU.mult))
                        st(e.tensor_scalar_mul(out=cwt[:], in0=oh_all[:, NE * i:NE * (i + 1)],
                                               scalar1=tpk[:, 0:1]))
                        e.wait_ge(s_rt.h, rt0 + i + 1)
                        s_rc.inc(e.tensor_copy(out=cwT[:, 128 * i:128 * (i + 1)], in_=ps_t[0:8, 0:128]))
                        e.wait_ge(s_rc.h, s_rc.n)

            NGF1 = 2
            NG1 = NFC // NGF1
            w1_b = [sb(f"w1b{p}", [128, NCH * NGF1 * 128], BF16) for p in range(2)]
            wd_b = [sb(f"wdq{p}", [128, 8 * 1024], BF16) for p in range(2)]
            h_sb = sb("h", [128, 16 * TH], BF16)
            rl_b = sb("rl", [128, 2 * TT], BF16)
            cwb = sb("cwb", [128, TPC], BF16)
            b1_sb = [sb(f"b1{p}", [128, NFC], F32) for p in range(2)]
            b2_sb = [sb(f"b2{p}", [1, D], BF16) for p in range(2)]

            s_w = [P.sem("m2_wA"), P.sem("m2_wB")]
            s_wd = [P.sem("m2_wdA"), P.sem("m2_wdB")]
            s_b = [P.sem("t_wl"), P.sem("t_ms")]
            s_cbm = P.sem("m2_cbm")
            s_cbe = P.sem("m2_cbe")
            s_g = P.sem("m2_g")
            s_sl = P.sem("m2_sl")
            s_h = P.sem("m2_h")
            s_o = P.sem("m2_o")
            s_oc = P.sem("m2_oc")
            w0 = [P.sems["m2_wA"].n, P.sems["m2_wB"].n]
            wdd0 = [P.sems["m2_wdA"].n, P.sems["m2_wdB"].n]
            b0 = [P.sems["t_wl"].n, P.sems["t_ms"].n]
            cbm0, cbe0, g0, sl0, h0, o0, oc0 = P.sem0(
                "m2_cbm", "m2_cbe", "m2_g", "m2_sl", "m2_h", "m2_o", "m2_oc")

            psG = [P.ps[0], P.ps[1]]
            PSO = P.ps[:7] + [ps8]
            ETH1 = [(ee, th) for ee in range(NE) for th in range(NTH)]

            with nc.Block() as blk:
                @blk.sync
                def _(e):
                    for ee in range(NE):
                        if ee > 0:
                            e.wait_ge(s_cbm.h, cbm0 + NTT * ee)
                        s_row.inc(e.dma_start(out=cwrow[:], in_=cwT[ee:ee + 1, :]), 16)

                @blk.gpsimd
                def _(e):
                    for ee, th in ETH1:
                        ei = ee * NTH + th
                        if th == 0:
                            if ee >= 2:
                                e.wait_ge(s_sl.h, sl0 + 128 * (ee - 1))
                                e.wait_ge(s_o.h, o0 + 64 * (ee - 1))
                            s_b[ee % 2].inc(e.dma_start(
                                out=b1_sb[ee % 2][:], in_=b1_d[ee]), 16)
                            s_b[ee % 2].inc(e.dma_start(
                                out=b2_sb[ee % 2][0:1, :], in_=b2_d[ee:ee + 1, :]), 16)
                        for hp in range(2):
                            for g8 in range(NG1 // 2):
                                g = hp * (NG1 // 2) + g8
                                gi = ei * NG1 + g
                                if gi >= 2:
                                    e.wait_ge(s_g.h, g0 + (gi - 1) * NGF1 * TTH)
                                fsl = slice(NGF1 * 128 * g, NGF1 * 128 * (g + 1))
                                s_w[gi % 2].inc(e.dma_start(
                                    out=w1_b[gi % 2][:].rearrange("p (c f) -> p c f", c=NCH),
                                    in_=w1_d[ee].rearrange("(c p) f -> p c f", p=128)[:, :, fsl]), 16)
                            for q in range(2):
                                qi = (ei * 2 + hp) * 2 + q
                                if qi >= 2:
                                    e.wait_ge(s_o.h, o0 + 16 * (qi - 1))
                                s_wd[qi % 2].inc(e.dma_start(
                                    out=wd_b[qi % 2][:].rearrange("p (fc o) -> p fc o", fc=8),
                                    in_=w2_d[ee].rearrange("(fc p) o -> p fc o", p=128)[:, 16 * hp + 8 * q:16 * hp + 8 * (q + 1), :]), 16)

                @blk.tensor
                def _(e):
                    gstage = 0
                    dstage = 0
                    for ee, th in ETH1:
                        ei = ee * NTH + th
                        if th == 0:
                            e.wait_ge(s_oc.h, oc0 + 8 * dstage)
                            for t in range(NTT):
                                ci = NTT * ee + t
                                e.wait_ge(s_row.h, row0 + 16 * (ee + 1))
                                if ci >= 1:
                                    e.wait_ge(s_cbe.h, cbe0 + ci)
                                s_cbm.inc(e.matmul(ps_r[:, :], P.onesbr[:], cwrow[0:1, TT * t:TT * (t + 1)],
                                                   start=True, stop=True))
                        for hp in range(2):
                            if ei * 2 + hp >= 1:
                                e.wait_ge(s_oc.h, oc0 + 8 * dstage)
                            for g in range(NG1 // 2):
                                gi = ei * NG1 + hp * (NG1 // 2) + g
                                e.wait_ge(s_w[gi % 2].h, w0[gi % 2] + 16 * (gi // 2 + 1))
                                if hp == 0 and g == 0:
                                    e.wait_ge(s_cbe.h, cbe0 + NTT * (ee + 1))
                                    e.wait_ge(s_b[ee % 2].h, b0[ee % 2] + 32 * (ee // 2 + 1))
                                for f2 in range(NGF1):
                                    fc = (hp * (NG1 // 2) + g) * NGF1 + f2
                                    W1W = NGF1 * 128
                                    for tt in range(TTH):
                                        spar = gstage % 2
                                        if gstage >= 2:
                                            e.wait_ge(s_sl.h, sl0 + gstage - 1)
                                        col = TH * th + TT * tt
                                        for dk in range(NCH):
                                            ii = e.matmul(psG[spar][:, :],
                                                          w1_b[gi % 2][:, W1W * dk + 128 * f2:W1W * dk + 128 * f2 + 128],
                                                          xln[dk][:, col:col + TT],
                                                          start=(dk == 0), stop=(dk == NCH - 1))
                                            if dk == NCH - 1:
                                                s_g.inc(ii)
                                        gstage += 1
                            e.wait_ge(s_h.h, h0 + gstage)
                            e.wait_ge(s_sl.h, sl0 + gstage)
                            for q in range(2):
                                qi = (ei * 2 + hp) * 2 + q
                                e.wait_ge(s_wd[qi % 2].h, wdd0[qi % 2] + 16 * (qi // 2 + 1))
                                lastq = hp == 1 and q == 1
                                for tt in range(TTH):
                                    if dstage >= 1:
                                        e.wait_ge(s_oc.h, oc0 + 8 * dstage)
                                    col = TT * tt
                                    colg = TH * th + TT * tt
                                    for fcl in range(8):
                                        fh = q * 8 + fcl
                                        for do in range(NCH):
                                            ii = e.matmul(PSO[do][:, :],
                                                          wd_b[qi % 2][:, 1024 * fcl + 128 * do:1024 * fcl + 128 * do + 128],
                                                          h_sb[:, TH * fh + col:TH * fh + col + TT],
                                                          start=(fcl == 0), stop=(fcl == 7 and not lastq))
                                            if fcl == 7 and not lastq:
                                                s_o.inc(ii)
                                    if lastq:
                                        for do in range(NCH):
                                            s_o.inc(e.matmul(PSO[do][:, :],
                                                             b2_sb[ee % 2][0:1, 128 * do:128 * (do + 1)],
                                                             cwb[0:1, colg:colg + TT],
                                                             start=False, stop=True))
                                    dstage += 1

                @blk.scalar
                def _(e):
                    gstage = 0
                    for ee, th in ETH1:
                        for fc in range(NFC):
                            for tt in range(TTH):
                                e.wait_ge(s_g.h, g0 + gstage + 1)
                                spar = gstage % 2
                                if gstage >= 2:
                                    e.wait_ge(s_h.h, h0 + gstage - 1)
                                s_sl.inc(e.activation(out=rl_b[:, TT * spar:TT * (spar + 1)],
                                                      in_=psG[spar][:, :], func=AF.Relu,
                                                      bias=b1_sb[ee % 2][:, fc:fc + 1]))
                                gstage += 1

                @blk.vector
                def _(e):
                    gstage = 0
                    dstage = 0
                    for ee, th in ETH1:
                        if th == 0:
                            for t in range(NTT):
                                ci = NTT * ee + t
                                e.wait_ge(s_cbm.h, cbm0 + ci + 1)
                                s_cbe.inc(e.tensor_copy(out=cwb[:, TT * t:TT * (t + 1)], in_=ps_r[:, :]))
                                e.wait_ge(s_cbe.h, s_cbe.n)
                        for hp in range(2):
                            for fcpos in range(16):
                                for tt in range(TTH):
                                    spar = gstage % 2
                                    e.wait_ge(s_sl.h, sl0 + gstage + 1)
                                    col = TH * th + TT * tt
                                    s_h.inc(e.tensor_tensor(out=h_sb[:, TH * fcpos + TT * tt:TH * fcpos + TT * tt + TT],
                                                            in0=rl_b[:, TT * spar:TT * (spar + 1)],
                                                            in1=cwb[:, col:col + TT], op=ALU.mult))
                                    e.wait_ge(s_h.h, s_h.n)
                                    gstage += 1
                            first = (ee == 0 and hp == 0)
                            for q in range(2):
                                for tt in range(TTH):
                                    col = TH * th + TT * tt
                                    for do in range(NCH):
                                        e.wait_ge(s_o.h, o0 + 8 * dstage + do + 1)
                                        if first and q == 0:
                                            s_oc.inc(e.tensor_copy(out=acc[do][:, col:col + TT], in_=PSO[do][:, :]))
                                        else:
                                            s_oc.inc(e.tensor_tensor(out=acc[do][:, col:col + TT],
                                                                     in0=acc[do][:, col:col + TT],
                                                                     in1=PSO[do][:, :], op=ALU.add))
                                        e.wait_ge(s_oc.h, s_oc.n)
                                    dstage += 1

            oc_total = P.sems["m2_oc"].n

        with contextlib.ExitStack() as sc_f:
            sbf = lambda n, s, d: sc_f.enter_context(nc.sbuf_tensor(f"{tag}_{n}", s, d))
            xf2 = [sbf(f"xf2{p}", [128, TPC], F32) for p in range(2)]
            yf = [sbf(f"yf{p}", [128, TPC], F32) for p in range(2)]
            s_fl = [P.sem("fin_lA"), P.sem("fin_lB")]
            s_fy = P.sem("fin_y")
            s_fs = [P.sem("fin_sA"), P.sem("fin_sB")]
            fl0 = [P.sems["fin_lA"].n, P.sems["fin_lB"].n]
            fs0 = [P.sems["fin_sA"].n, P.sems["fin_sB"].n]
            fy0 = P.sems["fin_y"].n
            s_oc_f = P.sem("m2_oc")

            with nc.Block() as blk:
                @blk.sync
                def _(e):
                    e.wait_ge(s_oc_f.h, oc_total)
                    for c in range(NCH):
                        if c >= 2:
                            e.wait_ge(s_fy.h, fy0 + c - 1)
                        s_fl[c % 2].inc(e.dma_start(out=xf2[c % 2][:], in_=x_dram[128 * c:128 * (c + 1), :]), 16)

                @blk.vector
                def _(e):
                    for c in range(NCH):
                        e.wait_ge(s_fl[c % 2].h, fl0[c % 2] + 16 * (c // 2 + 1))
                        if c >= 2:
                            e.wait_ge(s_fs[c % 2].h, fs0[c % 2] + 16 * (c // 2))
                        s_fy.inc(e.scalar_tensor_tensor(out=yf[c % 2][:], in0=acc[c][:], scalar=0.5,
                                                        in1=xf2[c % 2][:], op0=ALU.mult, op1=ALU.add))
                        e.wait_ge(s_fy.h, s_fy.n)

                @blk.gpsimd
                def _(e):
                    for c in range(NCH):
                        e.wait_ge(s_fy.h, fy0 + c + 1)
                        s_fs[c % 2].inc(e.dma_start(out=y_dram[128 * c:128 * (c + 1), :], in_=yf[c % 2][:]), 16)
                    e.wait_ge(s_fs[0].h, s_fs[0].n)
                    e.wait_ge(s_fs[1].h, s_fs[1].n)

    return P.sem("fin_sA")


def build_moe1_prog(uid="a"):
    P = Prog()
    nc = P.nc
    x = nc.dram_tensor("x", [D, TPC], F32, kind="ExternalInput").ap()
    gwt = nc.dram_tensor("gwt", [D, NE], F32, kind="ExternalInput").ap()
    gbr = nc.dram_tensor("gbr", [1, NE], F32, kind="ExternalInput").ap()
    w1 = nc.dram_tensor("w1", [NE, D, FF], F32, kind="ExternalInput").ap()
    b1 = nc.dram_tensor("b1", [NE, 128, NFC], F32, kind="ExternalInput").ap()
    w2 = nc.dram_tensor("w2", [NE, FF, D], F32, kind="ExternalInput").ap()
    b2 = nc.dram_tensor("b2", [NE, D], F32, kind="ExternalInput").ap()
    pm = nc.dram_tensor("premask", [128, 16], BF16, kind="ExternalInput").ap()
    y = nc.dram_tensor("y", [D, TPC], F32, kind="ExternalOutput").ap()
    t16 = nc.dram_tensor(f"t16_{uid}", [16, NE], F32, kind="Internal")
    gt = nc.dram_tensor(f"gt_{uid}", [128, NE], F32, kind="Internal")
    emit_moe1(P, "m1", x, y, gwt, gbr, w1, b1, w2, b2, pm, t16, gt)
    return P


def premask_for_core(core):
    m = np.zeros((128, 16), np.float32)
    for j in range(16):
        m[:16 * core + j, j] = 1.0
    return m.astype(BF)


def run_moe1(x_fm_shards, gate_w, gate_b, w1, b1, w2, b2):
    if "moe1" not in _CACHE:
        _CACHE["moe1"] = build_moe1_prog()
    P = _CACHE["moe1"]
    gwt = np.ascontiguousarray(gate_w.T.astype(np.float32))
    gbr = np.ascontiguousarray(gate_b.astype(np.float32).reshape(1, NE))
    b1t = np.ascontiguousarray(b1.reshape(NE, NFC, 128).transpose(0, 2, 1))
    in_maps = [{"x": np.ascontiguousarray(xs), "gwt": gwt, "gbr": gbr,
                "w1": w1, "b1": b1t, "w2": w2, "b2": b2,
                "premask": premask_for_core(c)} for c, xs in enumerate(x_fm_shards)]
    res = run_bass_kernel_spmd(P.nc, in_maps, core_ids=list(range(NCORES)))
    return [r["y"] for r in res.results]




def emit_l0attn(P, tag, x_dram, y_dram, W):
    nc = P.nc
    NTI = TPC // 128
    KCH = 37

    with contextlib.ExitStack() as sc_main:
        sbm = lambda n, s, d: sc_main.enter_context(nc.sbuf_tensor(f"{tag}_{n}", s, d))
        xln = [sbm(f"xln{c}", [128, TPC], BF16) for c in range(NCH)]
        bot = sbm("bot", [128, NCH], F32)
        g1t = sbm("g1t", [128, NCH], F32)
        be1t = sbm("be1t", [128, NCH], F32)
        g2t = sbm("g2t", [128, NCH], F32)
        be2t = sbm("be2t", [128, NCH], F32)
        b1t = sbm("b1t", [128, NFC], F32)
        b2t = sbm("b2t", [128, NCH], F32)

        with contextlib.ExitStack() as sc_ln:
            sbl = lambda n, s, d: sc_ln.enter_context(nc.sbuf_tensor(f"{tag}_{n}", s, d))
            xb = [sbl(f"xb{c}", [128, TPC], BF16) for c in range(NCH)]
            sq = sbl("sq", [128, 2 * TT], BF16)
            rows = [sbl(f"r{i}", [1, TPC], F32) for i in range(6)]
            tmpf = sbl("tmpf", [128, 2 * TT], F32)
            s_ln = emit_ln(P, tag + "_ln", x_dram, xb, xln, sq, rows, tmpf)
            ln_done = s_ln.n

        s_awg = P.sem("a_wg")
        s_aws = P.sem("a_ws")
        s_pk = P.sem("a_pk")
        s_pe = P.sem("a_pe")
        s_av = P.sem("a_av")
        s_aa = P.sem("a_aa")
        s_qt = P.sem("a_qt")
        s_qte = P.sem("a_qte")
        s_bb = P.sem("a_bb")
        s_bbe = P.sem("a_bbe")
        s_kv = P.sem("a_kv")
        s_kve = P.sem("a_kve")
        s_ar = P.sem("a_ar")
        s_cx = P.sem("a_cx")
        s_cxe = P.sem("a_cxe")
        s_wo = P.sem("a_wo")
        s_woa = P.sem("a_woa")
        s_wod = P.sem("a_wod")
        s_wos = [P.sem("a_wosA"), P.sem("a_wosB")]
        (awg0, aws0, pk0, pe0, av0, aa0, qt0, qte0, bb0, bbe0, kv0, kve0,
         ar0, cx0, cxe0, wo0, woa0, wod0) = P.sem0(
            "a_wg", "a_ws", "a_pk", "a_pe", "a_av", "a_aa", "a_qt", "a_qte",
            "a_bb", "a_bbe", "a_kv", "a_kve", "a_ar", "a_cx", "a_cxe",
            "a_wo", "a_woa", "a_wod")
        wos0 = [P.sems["a_wosA"].n, P.sems["a_wosB"].n]

        with contextlib.ExitStack() as s1:
            sb1 = lambda n, s, d: s1.enter_context(nc.sbuf_tensor(f"{tag}_{n}", s, d))
            q_fm = sb1("qfm", [128, 16384], BF16)
            kvraw = sb1("kvraw", [64, 2048], F32)
            kvc = sb1("kvc", [64, 2048], F32)
            kvt = sb1("kvt", [64, 1024], F32)
            kvm = sb1("kvm", [128, 1024], BF16)
            bsel = sb1("bsel", [64, 2], F32)

            with contextlib.ExitStack() as s2a:
                sb2 = lambda n, s, d: s2a.enter_context(nc.sbuf_tensor(f"{tag}_{n}", s, d))
                wbig = sb2("wA", [128, 8192], BF16)
                wbig2 = sb2("wB", [128, 8192], BF16)
                bqr = sb2("bqr", [1, D], BF16)
                bkr = sb2("bkr", [1, D], BF16)
                bvr = sb2("bvr", [1, D], BF16)
                bbc = [sb2(f"bbc{j}", [128, D], BF16) for j in range(3)]
                qrw = sb2("qrw", [128, 1024], BF16)
                ta = sb2("ta", [128, 1024], BF16)
                tb = sb2("tb", [128, 1024], BF16)
                tc = sb2("tc", [128, 1024], BF16)
                ket = sb2("ket", [128, 1024], BF16)
                rsum = sb2("rsum", [128, 16], F32)
                rsq = sb2("rsq", [128, 16], F32)
                rr = sb2("rr", [128, 16], F32)
                psT = s2a.enter_context(nc.psum_tensor(f"{tag}_psT", [128, 1024], BF16))

                pkB = [pk0, pk0 + 32, pk0 + 64]
                peB = [pe0, pe0 + 32, pe0 + 64]
                avB = [av0, av0, av0 + KCH * NTI]
                aaB = [aa0, aa0, aa0 + 2 * NTI]

                with contextlib.ExitStack() as s2b:
                    sb3 = lambda n, s, d: s2b.enter_context(nc.sbuf_tensor(f"{tag}_{n}", s, d))
                    k_tm = sb3("ktm", [128, 16384], BF16)
                    v_tm = sb3("vtm", [128, 16384], BF16)

                    with nc.Block() as blk:
                        @blk.sync
                        def _(e):
                            e.wait_ge(s_ln.h, ln_done)
                            for name, buf in (("bsel", bsel), ("bot", bot),
                                              ("g1t", g1t), ("be1t", be1t), ("g2t", g2t),
                                              ("be2t", be2t), ("b1t", b1t), ("b2t", b2t)):
                                s_aws.inc(e.dma_start(out=buf[:], in_=W[name][:, :]), 16)

                        @blk.gpsimd
                        def _(e):
                            e.wait_ge(s_ln.h, ln_done)
                            for name, buf in (("bqr", bqr), ("bkr", bkr), ("bvr", bvr)):
                                s_aws.inc(e.dma_start(out=buf[:], in_=W[name][:, :]), 16)
                            s_awg.inc(e.dma_start(out=wbig[:].rearrange("p (dk o) -> p dk o", dk=NCH),
                                                  in_=W["wv"].rearrange("(dk p) o -> p dk o", p=128)), 16)
                            s_awg.inc(e.dma_start(out=wbig2[:].rearrange("p (dk o) -> p dk o", dk=NCH),
                                                  in_=W["wk"].rearrange("(dk p) o -> p dk o", p=128)), 16)
                            e.wait_ge(s_pk.h, pk0 + 32)
                            s_awg.inc(e.dma_start(out=wbig[:].rearrange("p (dk o) -> p dk o", dk=NCH),
                                                  in_=W["wq"].rearrange("(dk p) o -> p dk o", p=128)), 16)

                        @blk.tensor
                        def _(e):
                            e.wait_ge(s_awg.h, awg0 + 32)
                            e.wait_ge(s_aws.h, aws0 + 16 * 11)
                            e.wait_ge(P.s_const.h, 96)
                            for j in range(3):
                                row = [bqr, bkr, bvr][j]
                                for t2 in range(2):
                                    jj = 2 * j + t2
                                    if jj >= 1:
                                        e.wait_ge(s_bbe.h, bbe0 + jj)
                                    s_bb.inc(e.matmul(P.ps[6][:, :], P.onesbr[:],
                                                      row[0:1, TT * t2:TT * (t2 + 1)],
                                                      start=True, stop=True))
                            for p in range(3):
                                wb = [wbig, wbig2, wbig][p]
                                if p == 2:
                                    e.wait_ge(s_awg.h, awg0 + 48)
                                for st in range(NTI):
                                    for ot in range(2):
                                        k2 = 2 * st + ot
                                        glob = 32 * p + k2
                                        if glob >= 2:
                                            e.wait_ge(s_pe.h, pe0 + glob - 1)
                                        for dk in range(NCH):
                                            ii = e.matmul(P.ps[2 + glob % 2][:, :],
                                                          xln[dk][:, 128 * st:128 * (st + 1)],
                                                          wb[:, 1024 * dk + TT * ot:1024 * dk + TT * (ot + 1)],
                                                          start=(dk == 0), stop=(dk == NCH - 1))
                                            if dk == NCH - 1:
                                                s_pk.inc(ii)
                                    if p == 2:
                                        e.wait_ge(s_av.h, avB[2] + KCH * (st + 1))
                                        if st >= 1:
                                            e.wait_ge(s_qte.h, qte0 + 8 * st)
                                        for c in range(NCH):
                                            s_qt.inc(e.transpose(psT[:, 128 * c:128 * (c + 1)],
                                                                 qrw[:, 128 * c:128 * (c + 1)],
                                                                 P.identb[:]))

                        @blk.scalar
                        def _(e):
                            for p in (1, 2):
                                for st in range(NTI):
                                    e.wait_ge(s_av.h, avB[p] + KCH * st + 1)
                                    s_aa.inc(e.activation(out=tb[:], in_=ta[:], func=AF.Exp))
                                    e.wait_ge(s_av.h, avB[p] + KCH * st + 20)
                                    s_aa.inc(e.activation(out=rsq[:], in_=rsum[:], func=AF.Sqrt))

                        @blk.vector
                        def _(e):
                            st_ = chained(e, s_av)
                            for j in range(3):
                                for t2 in range(2):
                                    jj = 2 * j + t2
                                    e.wait_ge(s_bb.h, bb0 + jj + 1)
                                    s_bbe.inc(e.tensor_copy(out=bbc[j][:, TT * t2:TT * (t2 + 1)],
                                                            in_=P.ps[6][:, :]))
                                    e.wait_ge(s_bbe.h, s_bbe.n)
                            for p in range(3):
                                dest = [v_tm, k_tm, None][p]
                                for st in range(NTI):
                                    for ot in range(2):
                                        k2 = 2 * st + ot
                                        glob = 32 * p + k2
                                        e.wait_ge(s_pk.h, pk0 + glob + 1)
                                        if p == 2 and st >= 1 and ot == 0:
                                            e.wait_ge(s_qt.h, qt0 + 8 * st)
                                        dslice = (qrw[:, TT * ot:TT * (ot + 1)] if p == 2
                                                  else dest[:, 1024 * st + TT * ot:1024 * st + TT * (ot + 1)])
                                        bi = [2, 1, 0][p]
                                        s_pe.inc(e.tensor_tensor(out=dslice, in0=P.ps[2 + glob % 2][:, :],
                                                                 in1=bbc[bi][:, TT * ot:TT * (ot + 1)],
                                                                 op=ALU.add))
                                        e.wait_ge(s_pe.h, s_pe.n)
                                    if p == 0:
                                        continue
                                    raw = qrw[:, :] if p == 2 else k_tm[:, 1024 * st:1024 * (st + 1)]
                                    st_(e.tensor_scalar_min(out=ta[:], in0=raw, scalar1=0.0))
                                    st_(e.tensor_scalar_max(out=tc[:], in0=raw, scalar1=0.0))
                                    e.wait_ge(s_aa.h, aaB[p] + 2 * st + 1)
                                    st_(e.scalar_tensor_tensor(out=ket[:], in0=tb[:], scalar=-1.0,
                                                               in1=tc[:], op0=ALU.add, op1=ALU.add))
                                    st_(e.tensor_tensor(out=ta[:], in0=ket[:], in1=ket[:], op=ALU.mult))
                                    for h in range(16):
                                        st_(e.reduce_sum(out=rsum[:, h:h + 1], in_=ta[:, 64 * h:64 * (h + 1)],
                                                         axis=AX.X))
                                    e.wait_ge(s_aa.h, aaB[p] + 2 * st + 2)
                                    st_(e.reciprocal(out=rr[:], in_=rsq[:]))
                                    for h in range(16):
                                        st_(e.tensor_scalar_mul(out=raw[:, 64 * h:64 * (h + 1)] if p == 1 else qrw[:, 64 * h:64 * (h + 1)],
                                                                in0=ket[:, 64 * h:64 * (h + 1)],
                                                                scalar1=rr[:, h:h + 1]))
                                    if p == 2:
                                        e.wait_ge(s_qt.h, qt0 + 8 * (st + 1))
                                        for c in range(NCH):
                                            s_qte.inc(e.tensor_copy(
                                                out=q_fm[:, TPC * c + 128 * st:TPC * c + 128 * (st + 1)],
                                                in_=psT[:, 128 * c:128 * (c + 1)]))
                                            e.wait_ge(s_qte.h, s_qte.n)
                            pass


                    with nc.Block() as blk:
                        @blk.tensor
                        def _(e):
                            for h in range(16):
                                if h >= 2:
                                    e.wait_ge(s_kve.h, kve0 + 2 * (h - 1))
                                for st in range(NTI):
                                    ii = e.matmul(P.ps[4 + h % 2][0:64, 0:64],
                                                  k_tm[:, 1024 * st + 64 * h:1024 * st + 64 * (h + 1)],
                                                  v_tm[:, 1024 * st + 64 * h:1024 * st + 64 * (h + 1)],
                                                  start=(st == 0), stop=(st == NTI - 1))
                                    if st == NTI - 1:
                                        s_kv.inc(ii)

                        @blk.vector
                        def _(e):
                            st_ = chained(e, s_kve)
                            for h in range(16):
                                e.wait_ge(s_kv.h, kv0 + h + 1)
                                st_(e.tensor_copy(out=kvraw[:, 64 * h:64 * (h + 1)],
                                                  in_=P.ps[4 + h % 2][0:64, 0:64]))
                                st_(e.tensor_copy(out=kvraw[:, 1024 + 64 * h:1024 + 64 * (h + 1)],
                                                  in_=P.ps[4 + h % 2][0:64, 0:64]))
                            st_(e.tensor_scalar_mul(out=kvc[:, 0:1024], in0=kvraw[:, 0:1024], scalar1=bsel[:, 0:1]))
                            st_(e.tensor_scalar_mul(out=kvc[:, 1024:2048], in0=kvraw[:, 1024:2048], scalar1=bsel[:, 1:2]))
                            e.wait_ge(s_ar.h, ar0 + 33)
                            st_(e.tensor_scalar_mul(out=kvt[:], in0=kvraw[:, 0:1024], scalar1=bsel[:, 0:1]))
                            st_(e.scalar_tensor_tensor(out=kvc[:, 0:1024], in0=kvraw[:, 1024:2048], scalar=bsel[:, 1:2],
                                                       in1=kvt[:], op0=ALU.mult, op1=ALU.add))
                            st_(e.tensor_scalar_mul(out=kvm[0:64, :], in0=kvc[:, 0:1024], scalar1=0.125))

                        @blk.gpsimd
                        def _(e):
                            e.wait_ge(s_kve.h, kve0 + 34)
                            s_ar.inc(e.dma_start(out=W["kvi"].ap()[:, :], in_=kvc[:]), 16)
                            e.wait_ge(s_ar.h, ar0 + 16)
                            s_ar.inc(e.collective_compute(
                                "AllReduce", ALU.add, replica_groups=RG,
                                ins=[W["kvi"].ap().opt()], outs=[W["kvo"].ap().opt()]), 1)
                            e.wait_ge(s_ar.h, ar0 + 17)
                            s_ar.inc(e.dma_start(out=kvraw[:], in_=W["kvo"].ap()[:, :]), 16)
                            e.wait_ge(s_kve.h, kve0 + 37)
                            s_ar.inc(e.dma_start(out=kvm[64:128, :], in_=kvm[0:64, :]), 16)

                with contextlib.ExitStack() as s3:
                    sb4 = lambda n, s, d: s3.enter_context(nc.sbuf_tensor(f"{tag}_{n}", s, d))
                    ctx = sb4("ctx", [128, 16384], BF16)
                    prs2 = [sb4(f"prs2{p}", [128, TT], F32) for p in range(2)]

                    with nc.Block() as blk:
                        @blk.tensor
                        def _(e):
                            e.wait_ge(s_kve.h, kve0 + 37)
                            e.wait_ge(s_ar.h, ar0 + 49)
                            e.wait_ge(s_qte.h, qte0 + 8 * NTI)
                            for h in range(16):
                                for t in range(NTT):
                                    k = 4 * h + t
                                    if k >= 2:
                                        e.wait_ge(s_cxe.h, cxe0 + k - 1)
                                    hb = 64 * (h % 2)
                                    ii = e.matmul(P.ps[4 + k % 2][0:64, :],
                                                  kvm[hb:hb + 64, 64 * h:64 * (h + 1)],
                                                  q_fm[hb:hb + 64,
                                                       TPC * (h // 2) + TT * t:TPC * (h // 2) + TT * (t + 1)],
                                                  start=True, stop=True)
                                    s_cx.inc(ii)
                            e.wait_ge(s_awg.h, awg0 + 64)
                            for c in range(NCH):
                                for t in range(NTT):
                                    k = NTT * c + t
                                    if k >= 2:
                                        e.wait_ge(s_wod.h, wod0 + k - 1)
                                    for dk in range(NCH):
                                        ii = e.matmul(P.ps[k % 2][:, :],
                                                      wbig2[:, 1024 * dk + 128 * c:1024 * dk + 128 * c + 128],
                                                      ctx[:, TPC * dk + TT * t:TPC * dk + TT * (t + 1)],
                                                      start=(dk == 0), stop=(dk == NCH - 1))
                                        if dk == NCH - 1:
                                            s_wo.inc(ii)

                        @blk.gpsimd
                        def _(e):
                            e.wait_ge(s_pk.h, pk0 + 64)
                            s_awg.inc(e.dma_start(out=wbig2[:].rearrange("p (dk o) -> p dk o", dk=NCH),
                                                  in_=W["wo"].rearrange("(dk p) o -> p dk o", p=128)), 16)

                        @blk.vector
                        def _(e):
                            for h in range(16):
                                for t in range(NTT):
                                    k = 4 * h + t
                                    e.wait_ge(s_cx.h, cx0 + k + 1)
                                    s_cxe.inc(e.tensor_copy(
                                        out=ctx[64 * (h % 2):64 * (h % 2) + 64,
                                                TPC * (h // 2) + TT * t:TPC * (h // 2) + TT * (t + 1)],
                                        in_=P.ps[4 + k % 2][0:64, :]))
                                    e.wait_ge(s_cxe.h, s_cxe.n)
                            for k in range(32):
                                c, t = divmod(k, NTT)
                                e.wait_ge(s_wo.h, wo0 + k + 1)
                                if k >= 2:
                                    e.wait_ge(s_wos[k % 2].h, wos0[k % 2] + 16 * (k // 2))
                                s_wod.inc(e.scalar_tensor_tensor(out=prs2[k % 2][:], in0=P.ps[k % 2][:, :],
                                                                 scalar=bot[:, c:c + 1],
                                                                 in1=xln[c][:, TT * t:TT * (t + 1)],
                                                                 op0=ALU.add, op1=ALU.add))
                                e.wait_ge(s_wod.h, s_wod.n)

                        @blk.sync
                        def _(e):
                            for k in range(32):
                                c, t = divmod(k, NTT)
                                e.wait_ge(s_wod.h, wod0 + k + 1)
                                s_wos[k % 2].inc(e.dma_start(
                                    out=W["pre1"].ap()[128 * c:128 * (c + 1), TT * t:TT * (t + 1)],
                                    in_=prs2[k % 2][:]), 16)

        attn_out = [sbm(f"ao{c}", [128, TPC], BF16) for c in range(NCH)]
        wos_tot = [(P.sems["a_wosA"], P.sems["a_wosA"].n), (P.sems["a_wosB"], P.sems["a_wosB"].n)]
        aws_tot = [(P.sems["a_ws"], P.sems["a_ws"].n)]
        with contextlib.ExitStack() as sc_ln1:
            sbl = lambda n, s, d: sc_ln1.enter_context(nc.sbuf_tensor(f"{tag}_{n}", s, d))
            xb2 = [sbl(f"x2b{c}", [128, TPC], BF16) for c in range(NCH)]
            sq2 = sbl("sq2", [128, 2 * TT], BF16)
            rows2 = [sbl(f"r2{i}", [1, TPC], F32) for i in range(6)]
            tmpf2 = sbl("tmpf2", [128, 2 * TT], F32)
            s_ln1 = emit_ln(P, tag + "_ln1", W["pre1"].ap(), xb2, attn_out, sq2, rows2, tmpf2,
                            eps=LN0_EPS, gate=wos_tot, affine=(g1t, be1t), vec_wait=aws_tot)
            ln1_done = s_ln1.n

        with contextlib.ExitStack() as scf:
            sbf = lambda n, s, d: scf.enter_context(nc.sbuf_tensor(f"{tag}_{n}", s, d))
            acc = [sbf(f"acc{c}", [128, TPC], F32) for c in range(NCH)]
            w1_b = [sbf(f"w1b{p}", [128, NCH * 256], BF16) for p in range(2)]
            wd_b = [sbf(f"wdq{p}", [128, 8 * 1024], BF16) for p in range(2)]
            h_sb = sbf("h", [128, 16 * TH], BF16)
            ps8f = scf.enter_context(nc.psum_tensor(f"{tag}_ps8f", [128, TT], F32))

            s_w = [P.sem("m2_wA"), P.sem("m2_wB")]
            s_wd = [P.sem("m2_wdA"), P.sem("m2_wdB")]
            s_g = P.sem("m2_g")
            s_sl = P.sem("m2_sl")
            s_o = P.sem("m2_o")
            s_oc = P.sem("m2_oc")
            s_ai = P.sem("af_ai")
            s_p2 = P.sem("af_p2")
            w0 = [P.sems["m2_wA"].n, P.sems["m2_wB"].n]
            wdd0 = [P.sems["m2_wdA"].n, P.sems["m2_wdB"].n]
            g0, sl0, o0, oc0, ai0, p20 = P.sem0("m2_g", "m2_sl", "m2_o", "m2_oc", "af_ai", "af_p2")

            NG1 = 16
            NGF1 = 2
            psG = [P.ps[0], P.ps[1]]
            PSO = P.ps[:7] + [ps8f]

            with nc.Block() as blk:
                @blk.gpsimd
                def _(e):
                    for gs, gv in wos_tot:
                        e.wait_ge(gs.h, gv)
                    for th in range(NTH):
                        ei = th
                        for hp in range(2):
                            for g8 in range(NG1 // 2):
                                g = hp * (NG1 // 2) + g8
                                gi = ei * NG1 + g
                                if gi >= 2:
                                    e.wait_ge(s_g.h, g0 + (gi - 1) * NGF1 * TTH)
                                fsl = slice(NGF1 * 128 * g, NGF1 * 128 * (g + 1))
                                s_w[gi % 2].inc(e.dma_start(
                                    out=w1_b[gi % 2][:].rearrange("p (c f) -> p c f", c=NCH),
                                    in_=W["w1"].rearrange("(c p) f -> p c f", p=128)[:, :, fsl]), 16)
                            for q in range(2):
                                qi = (ei * 2 + hp) * 2 + q
                                if qi >= 2:
                                    e.wait_ge(s_o.h, o0 + 16 * (qi - 1))
                                s_wd[qi % 2].inc(e.dma_start(
                                    out=wd_b[qi % 2][:].rearrange("p (fc o) -> p fc o", fc=8),
                                    in_=W["w2"].rearrange("(fc p) o -> p fc o", p=128)[:, 16 * hp + 8 * q:16 * hp + 8 * (q + 1), :]), 16)

                @blk.tensor
                def _(e):
                    e.wait_ge(s_ln1.h, ln1_done)
                    gstage = 0
                    dstage = 0
                    for th in range(NTH):
                        ei = th
                        for hp in range(2):
                            if ei * 2 + hp >= 1:
                                e.wait_ge(s_oc.h, oc0 + 8 * dstage)
                            for g in range(NG1 // 2):
                                gi = ei * NG1 + hp * (NG1 // 2) + g
                                e.wait_ge(s_w[gi % 2].h, w0[gi % 2] + 16 * (gi // 2 + 1))
                                if gi == 0:
                                    e.wait_ge(s_ai.h, ai0 + NCH)
                                for f2 in range(NGF1):
                                    fc = (hp * (NG1 // 2) + g) * NGF1 + f2
                                    for tt in range(TTH):
                                        spar = gstage % 2
                                        if gstage >= 2:
                                            e.wait_ge(s_sl.h, sl0 + gstage - 1)
                                        col = TH * th + TT * tt
                                        for dk in range(NCH):
                                            ii = e.matmul(psG[spar][:, :],
                                                          w1_b[gi % 2][:, 256 * dk + 128 * f2:256 * dk + 128 * f2 + 128],
                                                          attn_out[dk][:, col:col + TT],
                                                          start=(dk == 0), stop=(dk == NCH - 1))
                                            if dk == NCH - 1:
                                                s_g.inc(ii)
                                        gstage += 1
                            e.wait_ge(s_sl.h, sl0 + gstage)
                            for q in range(2):
                                qi = (ei * 2 + hp) * 2 + q
                                e.wait_ge(s_wd[qi % 2].h, wdd0[qi % 2] + 16 * (qi // 2 + 1))
                                for tt in range(TTH):
                                    if dstage >= 1:
                                        e.wait_ge(s_oc.h, oc0 + 8 * dstage)
                                    col = TT * tt
                                    for fcl in range(8):
                                        fh = q * 8 + fcl
                                        for do in range(NCH):
                                            ii = e.matmul(PSO[do][:, :],
                                                          wd_b[qi % 2][:, 1024 * fcl + 128 * do:1024 * fcl + 128 * do + 128],
                                                          h_sb[:, TH * fh + col:TH * fh + col + TT],
                                                          start=(fcl == 0), stop=(fcl == 7))
                                            if fcl == 7:
                                                s_o.inc(ii)
                                    dstage += 1

                @blk.scalar
                def _(e):
                    gstage = 0
                    for th in range(NTH):
                        for fc in range(NFC):
                            for tt in range(TTH):
                                e.wait_ge(s_g.h, g0 + gstage + 1)
                                if gstage % 32 == 0 and gstage >= 32:
                                    e.wait_ge(s_o.h, o0 + 32 * (gstage // 32))
                                spar = gstage % 2
                                fh = fc % 16
                                s_sl.inc(e.activation(out=h_sb[:, TH * fh + TT * tt:TH * fh + TT * tt + TT],
                                                      in_=psG[spar][:, :], func=AF.Gelu,
                                                      bias=b1t[:, fc:fc + 1]))
                                gstage += 1

                @blk.vector
                def _(e):
                    e.wait_ge(s_ln1.h, ln1_done)
                    for c in range(NCH):
                        s_ai.inc(e.tensor_scalar_add(out=acc[c][:], in0=attn_out[c][:],
                                                     scalar1=b2t[:, c:c + 1]))
                        e.wait_ge(s_ai.h, s_ai.n)
                    dstage = 0
                    for th in range(NTH):
                        for hp in range(2):
                            for q in range(2):
                                for tt in range(TTH):
                                    col = TH * th + TT * tt
                                    for do in range(NCH):
                                        e.wait_ge(s_o.h, o0 + 8 * dstage + do + 1)
                                        s_oc.inc(e.tensor_tensor(out=acc[do][:, col:col + TT],
                                                                 in0=acc[do][:, col:col + TT],
                                                                 in1=PSO[do][:, :], op=ALU.add))
                                        e.wait_ge(s_oc.h, s_oc.n)
                                    dstage += 1

                @blk.sync
                def _(e):
                    e.wait_ge(s_oc.h, P.sems["m2_oc"].n if False else oc0 + 8 * 16)
                    for c in range(NCH):
                        s_p2.inc(e.dma_start(out=W["pre2"].ap()[128 * c:128 * (c + 1), :], in_=acc[c][:]), 16)

            p2_tot = [(P.sems["af_p2"], P.sems["af_p2"].n)]

        out2 = [sbm(f"o2{c}", [128, TPC], BF16) for c in range(NCH)]
        with contextlib.ExitStack() as sc_ln2:
            sbl = lambda n, s, d: sc_ln2.enter_context(nc.sbuf_tensor(f"{tag}_{n}", s, d))
            xb3 = [sbl(f"x3b{c}", [128, TPC], BF16) for c in range(NCH)]
            sq3 = sbl("sq3", [128, 2 * TT], BF16)
            rows3 = [sbl(f"r3{i}", [1, TPC], F32) for i in range(6)]
            tmpf3 = sbl("tmpf3", [128, 2 * TT], F32)
            s_ln2 = emit_ln(P, tag + "_ln2", W["pre2"].ap(), xb3, out2, sq3, rows3, tmpf3,
                            eps=LN0_EPS, gate=p2_tot, affine=(g2t, be2t), vec_wait=aws_tot)
            ln2_done = s_ln2.n

        with contextlib.ExitStack() as sc_f:
            sbf2 = lambda n, s, d: sc_f.enter_context(nc.sbuf_tensor(f"{tag}_{n}", s, d))
            xf2 = [sbf2(f"xf2{p}", [128, TPC], F32) for p in range(2)]
            yf = [sbf2(f"yf{p}", [128, TPC], F32) for p in range(2)]
            s_fl = [P.sem("fin_lA"), P.sem("fin_lB")]
            s_fy = P.sem("fin_y")
            s_fs = [P.sem("fin_sA"), P.sem("fin_sB")]
            fl0 = [P.sems["fin_lA"].n, P.sems["fin_lB"].n]
            fs0 = [P.sems["fin_sA"].n, P.sems["fin_sB"].n]
            fy0 = P.sems["fin_y"].n

            with nc.Block() as blk:
                @blk.sync
                def _(e):
                    e.wait_ge(s_ln2.h, ln2_done)
                    for c in range(NCH):
                        if c >= 2:
                            e.wait_ge(s_fy.h, fy0 + c - 1)
                        s_fl[c % 2].inc(e.dma_start(out=xf2[c % 2][:], in_=x_dram[128 * c:128 * (c + 1), :]), 16)

                @blk.vector
                def _(e):
                    for c in range(NCH):
                        e.wait_ge(s_fl[c % 2].h, fl0[c % 2] + 16 * (c // 2 + 1))
                        if c >= 2:
                            e.wait_ge(s_fs[c % 2].h, fs0[c % 2] + 16 * (c // 2))
                        s_fy.inc(e.tensor_tensor(out=yf[c % 2][:], in0=xf2[c % 2][:],
                                                 in1=out2[c][:], op=ALU.add))
                        e.wait_ge(s_fy.h, s_fy.n)

                @blk.gpsimd
                def _(e):
                    for c in range(NCH):
                        e.wait_ge(s_fy.h, fy0 + c + 1)
                        s_fs[c % 2].inc(e.dma_start(out=y_dram[128 * c:128 * (c + 1), :], in_=yf[c % 2][:]), 16)
                    e.wait_ge(s_fs[0].h, s_fs[0].n)
                    e.wait_ge(s_fs[1].h, s_fs[1].n)

    return P.sem("fin_sA")


def build_l0attn_prog():
    P = Prog()
    nc = P.nc
    x = nc.dram_tensor("x", [D, TPC], F32, kind="ExternalInput").ap()
    y = nc.dram_tensor("y", [D, TPC], F32, kind="ExternalOutput").ap()
    W = {}
    for n, s in (("wq", [D, D]), ("wk", [D, D]), ("wv", [D, D]), ("wo", [D, D]),
                 ("bqr", [1, D]), ("bkr", [1, D]), ("bvr", [1, D]),
                 ("bot", [128, NCH]), ("g1t", [128, NCH]), ("be1t", [128, NCH]),
                 ("g2t", [128, NCH]), ("be2t", [128, NCH]),
                 ("w1", [D, FF]), ("b1t", [128, NFC]), ("w2", [FF, D]), ("b2t", [128, NCH]),
                 ("bsel", [64, 2])):
        W[n] = nc.dram_tensor(n, s, F32, kind="ExternalInput").ap()
    W["pre1"] = nc.dram_tensor("pre1", [D, TPC], F32, kind="Internal")
    W["pre2"] = nc.dram_tensor("pre2", [D, TPC], F32, kind="Internal")
    W["kvi"] = nc.dram_tensor("kvi", [64, 2048], F32, kind="Internal")
    W["kvo"] = nc.dram_tensor("kvo", [64, 2048], F32, kind="Internal")
    emit_l0attn(P, "a0", x, y, W)
    return P


def l0attn_inmaps(x_fm_shards, Wq, bq, Wk, bk, Wv, bv, Wo, bo, g1, be1, W1, b1, W2, b2, g2, be2):
    t8 = lambda v: np.ascontiguousarray(v.reshape(NCH, 128).T.astype(np.float32))
    base = {
        "wq": Wq.astype(np.float32), "wk": Wk.astype(np.float32),
        "wv": Wv.astype(np.float32), "wo": Wo.astype(np.float32),
        "bqr": bq.reshape(1, D).astype(np.float32),
        "bkr": bk.reshape(1, D).astype(np.float32),
        "bvr": bv.reshape(1, D).astype(np.float32),
        "bot": t8(bo), "g1t": t8(g1), "be1t": t8(be1), "g2t": t8(g2), "be2t": t8(be2),
        "w1": W1.astype(np.float32),
        "b1t": np.ascontiguousarray(b1.reshape(NFC, 128).T.astype(np.float32)),
        "w2": W2.astype(np.float32), "b2t": t8(b2),
    }
    maps = []
    for c, xs in enumerate(x_fm_shards):
        b = c // 4
        bsel = np.zeros((64, 2), np.float32)
        bsel[:, b] = 1.0
        m = dict(base)
        m["x"] = np.ascontiguousarray(xs)
        m["bsel"] = bsel
        maps.append(m)
    return maps


def run_l0attn(x_fm_shards, *weights):
    if "l0attn" not in _CACHE:
        _CACHE["l0attn"] = build_l0attn_prog()
    P = _CACHE["l0attn"]
    res = run_bass_kernel_spmd(P.nc, l0attn_inmaps(x_fm_shards, *weights), core_ids=list(range(NCORES)))
    return [r["y"] for r in res.results]




def emit_l1attn(P, tag, x_dram, y_dram, W):
    nc = P.nc
    NQB = 32
    MAXKT = 64
    with contextlib.ExitStack() as sc_main:
        sbm = lambda n, s, d: sc_main.enter_context(nc.sbuf_tensor(f"{tag}_{n}", s, d))
        xln = [sbm(f"xln{c}", [128, TPC], BF16) for c in range(NCH)]

        with contextlib.ExitStack() as sc_ln:
            sbl = lambda n, s, d: sc_ln.enter_context(nc.sbuf_tensor(f"{tag}_{n}", s, d))
            xb = [sbl(f"xb{c}", [128, TPC], BF16) for c in range(NCH)]
            sq = sbl("sq", [128, 2 * TT], BF16)
            rows = [sbl(f"r{i}", [1, TPC], F32) for i in range(6)]
            tmpf = sbl("tmpf", [128, 2 * TT], F32)
            s_ln = emit_ln(P, tag + "_ln", x_dram, xb, xln, sq, rows, tmpf)
            ln_done = s_ln.n

        s_ag = P.sem("t_ag")
        ag0 = P.sems["t_ag"].n
        with nc.Block() as blk:
            @blk.sync
            def _(e):
                e.wait_ge(s_ln.h, ln_done)
                for c in range(NCH):
                    s_ag.inc(e.dma_start(out=W["xagi"].ap()[128 * c:128 * (c + 1), :], in_=xln[c][:]), 16)

            @blk.gpsimd
            def _(e):
                e.wait_ge(s_ag.h, ag0 + 16 * NCH)
                s_ag.inc(e.collective_compute(
                    "AllGather", ALU.bypass, replica_groups=RG,
                    ins=[W["xagi"].ap().opt()], outs=[W["xago"].ap().opt()]), 1)

        with contextlib.ExitStack() as s1:
            sb1 = lambda n, s, d: s1.enter_context(nc.sbuf_tensor(f"{tag}_{n}", s, d))
            q_fm = sb1("qfm", [128, 16384], BF16)
            k0p = sb1("k0p", [128, 16384], BF16)
            k1p = sb1("k1p", [128, 16384], BF16)
            v65 = sb1("v65", [128, 130 * 128], BF16)
            ctxst = sb1("ctxs", [128, 2 * TT], BF16)
            xs_sb = [sb1(f"xs{p}", [128, NCH * TT], BF16) for p in range(2)]
            wq_sb = sb1("wqs", [128, NCH * 128], BF16)
            wk_sb = sb1("wks", [128, NCH * 128], BF16)
            wv_sb = sb1("wvs", [128, NCH * 128], BF16)
            bqt = sb1("bqt", [128, 1], F32)
            bkt = sb1("bkt", [128, 1], F32)
            bvr = sb1("bvr", [1, 128], F32)
            bvbc = sb1("bvbc", [128, 128], F32)
            p_sb = [sb1(f"p{j}", [128, TT], BF16) for j in range(4)]
            msk_sb = [sb1(f"msk{j}", [128, TT], F32) for j in range(2)]
            maskc = [sb1(f"maskc{j}", [128, TT], BF16) for j in range(4)]
            sqq = sb1("sqq", [128, TT], BF16)
            nrm2 = sb1("nrm2", [2, 1], F32)
            nrmc = sb1("nrmc", [2, 1], F32)
            nrq = sb1("nrq", [2, 1], F32)
            nrk = sb1("nrk", [2, 1], F32)
            nprod = sb1("nprod", [2, 1], F32)
            nsq = sb1("nsq", [2, 1], F32)
            nneg = sb1("nneg", [2, 1], BF16)
            mcolneg = sb1("mcol", [128, 2], F32)
            sel2 = sb1("sel2", [2, 2 * 128], BF16)
            ones65 = sb1("ones65", [65, 128], BF16)
            s65 = sb1("s65", [65, TT], F32)
            sb65 = sb1("sb65", [65, TT], BF16)
            recbc = sb1("recbc", [64, TT], BF16)
            hm2l = sb1("hm2l", [128, 2], BF16)

            s_wl = P.sem("t_wl")
            s_sl = [P.sem("t_slA"), P.sem("t_slB")]
            s_pj = P.sem("t_pj")
            s_pe = P.sem("t_pe")
            s_ms = P.sem("t_ms")
            s_nv = P.sem("t_nv")
            s_nm = P.sem("t_nm")
            s_na = P.sem("t_na")
            s_t1 = P.sem("t_t1")
            s_mk = P.sem("t_mk")
            s_px = P.sem("t_px")
            s_t2 = P.sem("t_t2")
            s_ev = P.sem("t_ev")
            s_em = P.sem("t_em")
            s_cst = P.sem("t_cst")
            cst0 = P.sems["t_cst"].n
            wl0, pj0, pe0, ms0, nv0, nm0, na0, t10, mk0, px0, t20, ev0, em0 = P.sem0(
                "t_wl", "t_pj", "t_pe", "t_ms", "t_nv", "t_nm", "t_na",
                "t_t1", "t_mk", "t_px", "t_t2", "t_ev", "t_em")
            sl0 = [P.sems["t_slA"].n, P.sems["t_slB"].n]

            NTT32 = T // TT

            def mk_mask(dd):
                m = np.zeros((128, TT), np.float32)
                for p in range(128):
                    kpos = 128 * dd + p
                    if kpos > 0:
                        m[p, :min(kpos, TT)] = -10000.0
                return m.astype(BF)

            NKT = [4 * (qb % 16) + 4 for qb in range(NQB)]
            A = [(h, qb) for h in range(2) for qb in range(NQB)]
            KTS = []
            tot = 0
            for (h, qb) in A:
                KTS.append(tot)
                tot += NKT[qb]
            KT_TOTAL = tot
            DIAG = []
            MSK_IDX = []
            DIAG_KK = []
            nmsk = 0
            kk_ = 0
            for a, (h, qb) in enumerate(A):
                nkt = NKT[qb]
                for ik in range(nkt):
                    d = ik >= nkt - 4
                    DIAG.append(d)
                    MSK_IDX.append(nmsk)
                    if d:
                        DIAG_KK.append(kk_)
                        nmsk += 1
                    kk_ += 1

            with nc.Block() as blk:
                @blk.sync
                def _(e):
                    e.wait_ge(s_ag.h, ag0 + 16 * NCH + 1)
                    for tt in range(NTT32):
                        if tt >= 2:
                            e.wait_ge(s_pj.h, pj0 + 6 * (tt - 1))
                        r, lt = divmod(tt, 4)
                        s_sl[tt % 2].inc(e.dma_start(
                            out=xs_sb[tt % 2][:].rearrange("p (c t) -> p c t", c=NCH),
                            in_=W["xago"].ap()[1024 * r:1024 * (r + 1), TT * lt:TT * (lt + 1)]
                                .rearrange("(c p) t -> p c t", p=128)), 16)

                @blk.gpsimd
                def _(e):
                    s_ms.inc(e.memset(v65[:], 1.0))
                    for name, buf in (("wqc", wq_sb), ("wkc", wk_sb), ("wvc", wv_sb)):
                        s_wl.inc(e.dma_start(out=buf[:].rearrange("p (c o) -> p c o", c=NCH),
                                             in_=W[name].rearrange("(c p) o -> p c o", p=128)), 16)
                    for name, buf in (("bqt", bqt), ("bkt", bkt), ("bvr", bvr)):
                        s_wl.inc(e.dma_start(out=buf[:], in_=W[name][:, :]), 16)
                    for j in range(4):
                        s_wl.inc(e.dma_start(out=maskc[j][:],
                                             in_=P.const(f"c_l1mask{j}", mk_mask(j))[:, :]), 16)
                    s_wl.inc(e.dma_start(out=sel2[:],
                                         in_=P.const("c_sel2", np.stack([
                                             np.concatenate([np.ones(128), np.zeros(128)]),
                                             np.concatenate([np.zeros(128), np.ones(128)])]).astype(BF))[:, :]), 16)
                    s_wl.inc(e.dma_start(out=ones65[:],
                                         in_=P.const("c_ones65", np.ones((65, 128), BF))[:, :]), 16)
                    s_wl.inc(e.dma_start(out=hm2l[:],
                                         in_=P.const("c_hm2", np.kron(np.eye(2), np.ones((64, 1))).astype(BF))[:, :]), 16)

                @blk.tensor
                def _(e):
                    e.wait_ge(s_wl.h, P.sems["t_wl"].n)
                    e.wait_ge(P.s_const.h, 96)
                    s_nm.inc(e.matmul(P.ps[6][:, 0:128], P.onesf[:], bvr[0:1, :], start=True, stop=True))
                    for tt in range(NTT32):
                        e.wait_ge(s_sl[tt % 2].h, sl0[tt % 2] + 16 * (tt // 2 + 1))
                        if tt >= 2:
                            e.wait_ge(s_pe.h, pe0 + 1 + 11 * (tt - 1))
                        for dk in range(NCH):
                            ii = e.matmul(P.ps[tt % 2][:, :], wq_sb[:, 128 * dk:128 * (dk + 1)],
                                          xs_sb[tt % 2][:, TT * dk:TT * (dk + 1)],
                                          start=(dk == 0), stop=(dk == NCH - 1))
                            if dk == NCH - 1:
                                s_pj.inc(ii)
                        for dk in range(NCH):
                            ii = e.matmul(P.ps[2 + tt % 2][:, :], wk_sb[:, 128 * dk:128 * (dk + 1)],
                                          xs_sb[tt % 2][:, TT * dk:TT * (dk + 1)],
                                          start=(dk == 0), stop=(dk == NCH - 1))
                            if dk == NCH - 1:
                                s_pj.inc(ii)
                        for it in range(4):
                            for dk in range(NCH):
                                ii = e.matmul(P.ps[4 + tt % 2][:, 128 * it:128 * (it + 1)],
                                              xs_sb[tt % 2][:, TT * dk + 128 * it:TT * dk + 128 * (it + 1)],
                                              wv_sb[:, 128 * dk:128 * (dk + 1)],
                                              start=(dk == 0), stop=(dk == NCH - 1))
                                if dk == NCH - 1:
                                    s_pj.inc(ii)

                @blk.vector
                def _(e):
                    e.wait_ge(s_nm.h, nm0 + 1)
                    s_pe.inc(e.tensor_copy(out=bvbc[:], in_=P.ps[6][:, 0:128]))
                    e.wait_ge(s_pe.h, s_pe.n)
                    for tt in range(NTT32):
                        e.wait_ge(s_pj.h, pj0 + 6 * tt + 1)
                        s_pe.inc(e.tensor_scalar(out=q_fm[:, TT * tt:TT * (tt + 1)],
                                                 in0=P.ps[tt % 2][:, :], scalar1=bqt[:, 0:1],
                                                 scalar2=0.125, op0=ALU.add, op1=ALU.mult))
                        e.wait_ge(s_pe.h, s_pe.n)
                        e.wait_ge(s_pj.h, pj0 + 6 * tt + 2)
                        s_pe.inc(e.tensor_scalar_add(out=k0p[0:64, TT * tt:TT * (tt + 1)],
                                                     in0=P.ps[2 + tt % 2][0:64, :], scalar1=bkt[0:64, 0:1]))
                        e.wait_ge(s_pe.h, s_pe.n)
                        s_pe.inc(e.tensor_scalar_add(out=k1p[64:128, TT * tt:TT * (tt + 1)],
                                                     in0=P.ps[2 + tt % 2][64:128, :], scalar1=bkt[64:128, 0:1]))
                        e.wait_ge(s_pe.h, s_pe.n)
                        if tt == 0:
                            e.wait_ge(s_ms.h, ms0 + 1)
                        e.wait_ge(s_pj.h, pj0 + 6 * tt + 6)
                        for it in range(4):
                            kt = 4 * tt + it
                            for hh in range(2):
                                s_pe.inc(e.tensor_tensor(
                                    out=v65[:, 130 * kt + 65 * hh:130 * kt + 65 * hh + 64],
                                    in0=P.ps[4 + tt % 2][:, 128 * it + 64 * hh:128 * it + 64 * (hh + 1)],
                                    in1=bvbc[:, 64 * hh:64 * (hh + 1)], op=ALU.add))
                                e.wait_ge(s_pe.h, s_pe.n)

            ev_pre = P.sems["t_ev"].n
            NV_MM = lambda i: nv0 + 4 + 4 * i + 1
            NV_RM = lambda i: nv0 + 4 + 4 * i + 2
            NV_END = nv0 + 4 + 4 * 3 * NTT32 + 2
            with nc.Block() as blk:
                @blk.tensor
                def _(e):
                    for j in range(3):
                        for tt in range(NTT32):
                            i = NTT32 * j + tt
                            e.wait_ge(s_nv.h, NV_MM(i))
                            if i >= 2:
                                e.wait_ge(s_nv.h, NV_RM(i - 2))
                            s_nm.inc(e.matmul(P.ps[i % 2][0:2, :], hm2l[:], sqq[:],
                                              start=True, stop=True))
                    e.wait_ge(s_nv.h, NV_END + 1)
                    for h in range(2):
                        if h == 1:
                            e.wait_ge(s_ev.h, ev_pre + 1)
                        s_em.inc(e.matmul(P.ps[6][:, 0:1], sel2[:, 128 * h:128 * (h + 1)],
                                          nneg[:, 0:1], start=True, stop=True))

                @blk.vector
                def _(e):
                    st_ = chained(e, s_nv)
                    st_(e.memset(k0p[64:128, :], 0.0))
                    st_(e.memset(k1p[0:64, :], 0.0))
                    st_(e.memset(nrq[:], -1e30))
                    st_(e.memset(nrk[:], -1e30))
                    for j, (srcf, acc_) in enumerate(((q_fm, nrq), (k0p, nrk), (k1p, nrk))):
                        for tt in range(NTT32):
                            i = NTT32 * j + tt
                            st_(e.tensor_tensor(out=sqq[:], in0=srcf[:, TT * tt:TT * (tt + 1)],
                                                in1=srcf[:, TT * tt:TT * (tt + 1)], op=ALU.mult))
                            e.wait_ge(s_nm.h, nm0 + 1 + i + 1)
                            st_(e.reduce_max(out=nrm2[:], in_=P.ps[i % 2][0:2, :], axis=AX.X))
                            st_(e.tensor_tensor(out=nrmc[:], in0=acc_[:], in1=nrm2[:], op=ALU.max))
                            st_(e.tensor_copy(out=acc_[:], in_=nrmc[:]))
                    st_(e.tensor_tensor(out=nprod[:], in0=nrq[:], in1=nrk[:], op=ALU.mult))
                    st_(e.tensor_scalar_mul(out=nsq[:], in0=nprod[:], scalar1=1.0))
                    e.wait_ge(s_na.h, na0 + 1)
                    st_(e.tensor_scalar_mul(out=nneg[:], in0=nrm2[:], scalar1=-1.0))
                    for h in range(2):
                        e.wait_ge(s_em.h, em0 + h + 1)
                        s_ev.inc(e.tensor_copy(out=mcolneg[:, h:h + 1], in_=P.ps[6][:, 0:1]))
                        e.wait_ge(s_ev.h, s_ev.n)

                @blk.scalar
                def _(e):
                    e.wait_ge(s_nv.h, NV_END)
                    s_na.inc(e.activation(out=nrm2[:], in_=nsq[:], func=AF.Sqrt))

            ev_base = P.sems["t_ev"].n
            ST_B = [P.ps[0], P.ps[1], P.ps[5], P.ps[6]]
            AVLAG = 3
            with nc.Block() as blk:
                @blk.tensor
                def _(e):
                    SEQ = []
                    kk = 0
                    for a, (h, qb) in enumerate(A):
                        for ik in range(NKT[qb]):
                            SEQ.append((a, h, qb, ik, 64 * (qb // 16) + ik, kk))
                            kk += 1

                    def emit_sT(item):
                        a, h, qb, ik, gkt, kk = item
                        if a == 0 and ik == 0:
                            e.wait_ge(s_ev.h, ev_base)
                        if kk >= 4:
                            e.wait_ge(s_px.h, px0 + kk - 3)
                        s_t1.inc(e.matmul(ST_B[kk % 4][:, :],
                                          (k0p, k1p)[h][:, 128 * gkt:128 * (gkt + 1)],
                                          q_fm[:, TT * qb:TT * (qb + 1)],
                                          start=True, stop=True))

                    def emit_AV(item):
                        a, h, qb, ik, gkt, kk = item
                        if ik == 0 and a >= 2:
                            e.wait_ge(s_ev.h, ev_base + 4 * (a - 1))
                        e.wait_ge(s_px.h, px0 + kk + 1)
                        s_t2.inc(e.matmul(P.ps[2 + a % 2][0:65, :],
                                          v65[:, 130 * gkt + 65 * h:130 * gkt + 65 * h + 65],
                                          p_sb[kk % 4][:],
                                          start=(ik == 0), stop=(ik == NKT[qb] - 1)))

                    pending = []

                    def emit_recbc(pa):
                        e.wait_ge(s_ev.h, ev_base + 4 * pa + 2)
                        s_em.inc(e.matmul(P.ps[4][0:64, :], ones65[64:65, 0:64], sb65[64:65, :],
                                          start=True, stop=True))

                    for i, item in enumerate(SEQ):
                        emit_sT(item)
                        while pending and pending[0][0] <= i:
                            emit_recbc(pending.pop(0)[1])
                        if i >= AVLAG:
                            it2 = SEQ[i - AVLAG]
                            emit_AV(it2)
                            if it2[3] == NKT[it2[2]] - 1:
                                pending.append((i + 1, it2[0]))
                    for i in range(len(SEQ) - AVLAG, len(SEQ)):
                        emit_AV(SEQ[i])
                    while pending:
                        emit_recbc(pending.pop(0)[1])
                    emit_recbc(SEQ[-1][0])

                @blk.vector
                def _(e):
                    st_ = chained(e, s_ev)
                    kk = 0
                    mi = 0
                    for a, (h, qb) in enumerate(A):
                        nkt = NKT[qb]
                        for ik in range(nkt):
                            diag = ik >= nkt - 4
                            if diag:
                                dd = ik - (nkt - 4)
                                e.wait_ge(s_t1.h, t10 + kk + 1)
                                if mi >= 2:
                                    e.wait_ge(s_px.h, px0 + DIAG_KK[mi - 2] + 1)
                                s_mk.inc(e.tensor_tensor(out=msk_sb[mi % 2][:], in0=ST_B[kk % 4][:, :],
                                                         in1=maskc[dd][:], op=ALU.add))
                                e.wait_ge(s_mk.h, s_mk.n)
                                mi += 1
                            kk += 1
                        e.wait_ge(s_t2.h, t20 + KTS[a] + nkt)
                        st_(e.reciprocal(out=s65[64:65, :], in_=P.ps[2 + a % 2][64:65, :]))
                        st_(e.tensor_copy(out=sb65[64:65, :], in_=s65[64:65, :]))
                        e.wait_ge(s_em.h, em0 + 2 + a + 1)
                        st_(e.tensor_copy(out=recbc[:], in_=P.ps[4][0:64, :]))
                        if a >= 2:
                            e.wait_ge(s_cst.h, cst0 + 16 * (a - 1))
                        st_(e.tensor_tensor(out=ctxst[64 * h:64 * (h + 1), TT * (a % 2):TT * (a % 2) + TT],
                                            in0=P.ps[2 + a % 2][0:64, :], in1=recbc[:], op=ALU.mult))

                @blk.sync
                def _(e):
                    for a, (h, qb) in enumerate(A):
                        e.wait_ge(s_ev.h, ev_base + 4 * a + 4)
                        s_cst.inc(e.dma_start(
                            out=W["cagi"].ap()[64 * h:64 * (h + 1), TT * qb:TT * (qb + 1)],
                            in_=ctxst[64 * h:64 * (h + 1), TT * (a % 2):TT * (a % 2) + TT]), 16)

                @blk.scalar
                def _(e):
                    kk = 0
                    e.wait_ge(s_ev.h, ev_base)
                    for a, (h, qb) in enumerate(A):
                        nkt = NKT[qb]
                        for ik in range(nkt):
                            diag = ik >= nkt - 4
                            if diag:
                                e.wait_ge(s_mk.h, mk0 + MSK_IDX[kk] + 1)
                                srcap = msk_sb[MSK_IDX[kk] % 2][:]
                            else:
                                e.wait_ge(s_t1.h, t10 + kk + 1)
                                srcap = ST_B[kk % 4][:, :]
                            if kk >= 4:
                                e.wait_ge(s_t2.h, t20 + kk - 3)
                            s_px.inc(e.activation(out=p_sb[kk % 4][:], in_=srcap, func=AF.Exp,
                                                  bias=mcolneg[:, h:h + 1]))
                            kk += 1

            s_ag2 = P.sem("t_ag2")
            ag20 = P.sems["t_ag2"].n
            ev_tot = P.sems["t_ev"].n
            with nc.Block() as blk:
                @blk.gpsimd
                def _(e):
                    e.wait_ge(s_cst.h, cst0 + 16 * len(A))
                    s_ag2.inc(e.collective_compute(
                        "AllGather", ALU.bypass, replica_groups=RG,
                        ins=[W["cagi"].ap().opt()], outs=[W["cago"].ap().opt()]), 1)

        with contextlib.ExitStack() as s2:
            sb2 = lambda n, s, d: s2.enter_context(nc.sbuf_tensor(f"{tag}_{n}", s, d))
            xc = [sb2(f"xc{c}", [128, TPC], BF16) for c in range(NCH)]
            wo_sb = sb2("wos", [128, 8192], BF16)
            boutt = sb2("boutt", [128, NCH], F32)
            cidx = sb2("cidx", [128, 8], mybir.dt.int32)
            xf2 = [sb2(f"xf2{p}", [128, TT], F32) for p in range(2)]
            yf = [sb2(f"yf{p}", [128, TT], F32) for p in range(2)]

            s_gx = P.sem("t_gx")
            s_wo2 = P.sem("t_wo2")
            s_wm = P.sem("t_wm")
            s_wv = P.sem("t_wv")
            s_ws = [P.sem("t_wsA"), P.sem("t_wsB")]
            s_xl = [P.sem("t_xlA"), P.sem("t_xlB")]
            gx0, wo20, wm0, wv0 = P.sem0("t_gx", "t_wo2", "t_wm", "t_wv")
            ws0 = [P.sems["t_wsA"].n, P.sems["t_wsB"].n]
            xl0 = [P.sems["t_xlA"].n, P.sems["t_xlB"].n]
            s_ag2b = P.sem("t_ag2")
            ag2_tot = P.sems["t_ag2"].n

            with nc.Block() as blk:
                @blk.gpsimd
                def _(e):
                    s_wo2.inc(e.dma_start(out=wo_sb[:].rearrange("p (dk o) -> p dk o", dk=NCH),
                                          in_=W["wout"].rearrange("(dk p) o -> p dk o", p=128)), 16)
                    s_wo2.inc(e.dma_start(out=boutt[:], in_=W["boutt"][:, :]), 16)
                    s_wo2.inc(e.dma_start(out=cidx[:], in_=W["ctxidx"][:, :]), 16)
                    e.wait_ge(s_ag2b.h, ag2_tot)
                    e.wait_ge(s_wo2.h, wo20 + 48)
                    flat = W["cago"].ap().rearrange("r (a b) -> (r a) b", b=TPC)
                    for rc in range(NCH):
                        s_gx.inc(e.indirect_dma_start(
                            out=xc[rc][:], out_offset=None,
                            in_=flat, in_offset=bass.IndirectOffsetOnAxis(cidx[:, rc:rc + 1], 0)), 16)
                    for k in range(32):
                        c, t = divmod(k, NTT)
                        e.wait_ge(s_wv.h, wv0 + k + 1)
                        s_ws[k % 2].inc(e.dma_start(out=y_dram[128 * c:128 * (c + 1), TT * t:TT * (t + 1)],
                                                    in_=yf[k % 2][:]), 16)
                    e.wait_ge(s_ws[0].h, P.sems["t_wsA"].n)
                    e.wait_ge(s_ws[1].h, P.sems["t_wsB"].n)

                @blk.sync
                def _(e):
                    for k in range(32):
                        c, t = divmod(k, NTT)
                        if k >= 2:
                            e.wait_ge(s_wv.h, wv0 + k - 1)
                        s_xl[k % 2].inc(e.dma_start(out=xf2[k % 2][:],
                                                    in_=x_dram[128 * c:128 * (c + 1), TT * t:TT * (t + 1)]), 16)

                @blk.tensor
                def _(e):
                    e.wait_ge(s_gx.h, gx0 + 16 * NCH)
                    for k in range(32):
                        c, t = divmod(k, NTT)
                        if k >= 2:
                            e.wait_ge(s_wv.h, wv0 + k - 1)
                        for dk in range(NCH):
                            ii = e.matmul(P.ps[k % 2][:, :],
                                          wo_sb[:, 1024 * dk + 128 * c:1024 * dk + 128 * c + 128],
                                          xc[dk][:, TT * t:TT * (t + 1)],
                                          start=(dk == 0), stop=(dk == NCH - 1))
                            if dk == NCH - 1:
                                s_wm.inc(ii)

                @blk.vector
                def _(e):
                    for k in range(32):
                        c, t = divmod(k, NTT)
                        e.wait_ge(s_wm.h, wm0 + k + 1)
                        e.wait_ge(s_xl[k % 2].h, xl0[k % 2] + 16 * (k // 2 + 1))
                        if k >= 2:
                            e.wait_ge(s_ws[k % 2].h, ws0[k % 2] + 16 * (k // 2))
                        s_wv.inc(e.scalar_tensor_tensor(out=yf[k % 2][:], in0=P.ps[k % 2][:, :],
                                                        scalar=boutt[:, c:c + 1], in1=xf2[k % 2][:],
                                                        op0=ALU.add, op1=ALU.add))
                        e.wait_ge(s_wv.h, s_wv.n)


    return P.sem("t_wsA")




def build_l1attn_prog():
    P = Prog()
    nc = P.nc
    x = nc.dram_tensor("x", [D, TPC], F32, kind="ExternalInput").ap()
    y = nc.dram_tensor("y", [D, TPC], F32, kind="ExternalOutput").ap()
    W = {}
    for n, s in (("wqc", [D, 128]), ("wkc", [D, 128]), ("wvc", [D, 128]),
                 ("bqt", [128, 1]), ("bkt", [128, 1]), ("bvr", [1, 128]),
                 ("wout", [D, D]), ("boutt", [128, NCH])):
        W[n] = nc.dram_tensor(n, s, F32, kind="ExternalInput").ap()
    W["ctxidx"] = nc.dram_tensor("ctxidx", [128, 8], mybir.dt.int32, kind="ExternalInput").ap()
    W["xagi"] = nc.dram_tensor("xagi", [D, TPC], BF16, kind="Internal")
    W["xago"] = nc.dram_tensor("xago", [NCORES * D, TPC], BF16, kind="Internal")
    W["cagi"] = nc.dram_tensor("cagi", [128, T], BF16, kind="Internal")
    W["cago"] = nc.dram_tensor("cago", [D, T], BF16, kind="Internal")
    emit_l1attn(P, "t1", x, y, W)
    return P


def l1attn_inmaps(x_fm_shards, Wqkv, bqkv, Wout, bout):
    Wqkv = Wqkv.astype(np.float32)
    bqkv = bqkv.astype(np.float32)
    maps = []
    for c, xs in enumerate(x_fm_shards):
        idx = np.zeros((128, 8), np.int32)
        for rc in range(8):
            for p in range(128):
                idx[p, rc] = (128 * rc + p) * 8 + c
        m = {
            "x": np.ascontiguousarray(xs),
            "wqc": np.ascontiguousarray(Wqkv[:, 128 * c:128 * (c + 1)]),
            "wkc": np.ascontiguousarray(Wqkv[:, 1024 + 128 * c:1024 + 128 * (c + 1)]),
            "wvc": np.ascontiguousarray(Wqkv[:, 2048 + 128 * c:2048 + 128 * (c + 1)]),
            "bqt": np.ascontiguousarray(bqkv[128 * c:128 * (c + 1)].reshape(128, 1)),
            "bkt": np.ascontiguousarray(bqkv[1024 + 128 * c:1024 + 128 * (c + 1)].reshape(128, 1)),
            "bvr": np.ascontiguousarray(bqkv[2048 + 128 * c:2048 + 128 * (c + 1)].reshape(1, 128)),
            "wout": Wout.astype(np.float32),
            "boutt": np.ascontiguousarray(bout.reshape(NCH, 128).T.astype(np.float32)),
            "ctxidx": idx,
        }
        maps.append(m)
    return maps


def run_l1attn(x_fm_shards, Wqkv, bqkv, Wout, bout):
    if "l1attn" not in _CACHE:
        _CACHE["l1attn"] = build_l1attn_prog()
    P = _CACHE["l1attn"]
    res = run_bass_kernel_spmd(P.nc, l1attn_inmaps(x_fm_shards, Wqkv, bqkv, Wout, bout),
                               core_ids=list(range(NCORES)))
    return [r["y"] for r in res.results]




def kernel_unfused(**inputs):
    x = np.asarray(inputs["L0_attn_x"], np.float32)
    xt = np.ascontiguousarray(x.reshape(T, D))
    shards = [np.ascontiguousarray(xt[i * TPC:(i + 1) * TPC].T) for i in range(NCORES)]

    g = lambda n: np.asarray(inputs[n], np.float32)

    shards = run_moe1(shards, g("L0_ffn_gate_w"), g("L0_ffn_gate_b"),
                      g("L0_ffn_w1"), g("L0_ffn_b1"), g("L0_ffn_w2"), g("L0_ffn_b2"))
    shards = [np.asarray(s) for s in shards]
    shards = run_l0attn(shards, g("L0_attn_Wq"), g("L0_attn_bq"), g("L0_attn_Wk"), g("L0_attn_bk"),
                        g("L0_attn_Wv"), g("L0_attn_bv"), g("L0_attn_Wo"), g("L0_attn_bo"),
                        g("L0_attn_g1"), g("L0_attn_be1"), g("L0_attn_W1"), g("L0_attn_b1"),
                        g("L0_attn_W2"), g("L0_attn_b2"), g("L0_attn_g2"), g("L0_attn_be2"))
    shards = [np.asarray(s) for s in shards]
    shards = run_moe1(shards, g("L0_ffn_gate_w"), g("L0_ffn_gate_b"),
                      g("L0_ffn_w1"), g("L0_ffn_b1"), g("L0_ffn_w2"), g("L0_ffn_b2"))
    shards = [np.asarray(s) for s in shards]
    shards = run_moe2(shards, g("L1_ffn_router_w"), g("L1_ffn_w_gate"),
                      g("L1_ffn_w_up"), g("L1_ffn_w_down"))
    shards = [np.asarray(s) for s in shards]
    shards = run_l1attn(shards, g("L1_attn_Wqkv"), g("L1_attn_bqkv"),
                        g("L1_attn_Wout"), g("L1_attn_bout"))
    shards = [np.asarray(s) for s in shards]
    shards = run_moe2(shards, g("L1_ffn_router_w"), g("L1_ffn_w_gate"),
                      g("L1_ffn_w_up"), g("L1_ffn_w_down"))
    shards = [np.asarray(s) for s in shards]

    y = np.concatenate([s.T for s in shards], axis=0)
    return y.reshape(2, 8192, 1024).astype(np.float32)





def emit_moe1_sparse(P, tag, x_dram, y_dram, gwt_d, gbr_d, W, H, WC=None):
    nc = P.nc
    NTI = TPC // 128
    NTBL = CAP // 128
    NTH1 = CAP // TT
    BIGF = float(1 << 22)
    with contextlib.ExitStack() as sc_main:
        sbm = lambda n, s, d: sc_main.enter_context(nc.sbuf_tensor(f"{tag}_{n}", s, d))
        xln = [sbm(f"xln{c}", [128, TPC], BF16) for c in range(NCH)]
        sidxi = sbm("sidxi", [128, NTI], mybir.dt.int32)
        wcomb = sbm("wcomb", [128, NTI], F32)

        with contextlib.ExitStack() as sc_ln:
            sbl = lambda n, s, d: sc_ln.enter_context(nc.sbuf_tensor(f"{tag}_{n}", s, d))
            xb = [sbl(f"xb{c}", [128, TPC], BF16) for c in range(NCH)]
            sq = sbl("sq", [128, 2 * TT], BF16)
            rows = [sbl(f"r{i}", [1, TPC], F32) for i in range(6)]
            tmpf = sbl("tmpf", [128, 2 * TT], F32)
            s_ln = emit_ln(P, tag + "_ln", x_dram, xb, xln, sq, rows, tmpf)
            ln_done = s_ln.n

        s_tx = P.sem("a_qt")
        s_te = P.sem("a_qte")
        s_td = [P.sem("a_wosA"), P.sem("a_wosB")]
        s_tz = P.sem("a_bb")
        s_tg = P.sem("a_ar")
        tx0, te0, tz0, tg0 = P.sem0("a_qt", "a_qte", "a_bb", "a_ar")
        td0 = [P.sems["a_wosA"].n, P.sems["a_wosB"].n]
        ZCOLS = 8 * CAP // 128
        with contextlib.ExitStack() as sc_t:
            sbt = lambda n, s, d: sc_t.enter_context(nc.sbuf_tensor(f"{tag}_{n}", s, d))
            tmt = [sbt(f"tmt{p}", [128, 1024], BF16) for p in range(2)]
            zrow = sbt("zrow", [128, ZCOLS], F32)
            psT = sc_t.enter_context(nc.psum_tensor(f"{tag}_psT", [128, 1024], BF16))

            with nc.Block() as blk:
                @blk.tensor
                def _(e):
                    e.wait_ge(s_ln.h, ln_done)
                    e.wait_ge(P.s_const.h, 96)
                    for lt in range(NTI):
                        if lt >= 1:
                            e.wait_ge(s_te.h, te0 + 1 + 8 * lt)
                        for c in range(NCH):
                            s_tx.inc(e.transpose(psT[:, 128 * c:128 * (c + 1)],
                                                 xln[c][:, 128 * lt:128 * (lt + 1)],
                                                 P.identb[:]))

                @blk.vector
                def _(e):
                    s_te.inc(e.memset(zrow[:], 0.0))
                    e.wait_ge(s_te.h, s_te.n)
                    for lt in range(NTI):
                        e.wait_ge(s_tx.h, tx0 + 8 * (lt + 1))
                        if lt >= 2:
                            e.wait_ge(s_td[lt % 2].h, td0[lt % 2] + 16 * (lt // 2))
                        for c in range(NCH):
                            s_te.inc(e.tensor_copy(out=tmt[lt % 2][:, 128 * c:128 * (c + 1)],
                                                   in_=psT[:, 128 * c:128 * (c + 1)]))
                            e.wait_ge(s_te.h, s_te.n)

                @blk.sync
                def _(e):
                    e.wait_ge(s_te.h, te0 + 1)
                    s_tz.inc(e.dma_start(out=H["tbl"].ap().rearrange("r one -> (r one)")
                                         .rearrange("(p k) -> p k", p=128),
                                         in_=zrow[:]), 16)
                    for lt in range(NTI):
                        e.wait_ge(s_te.h, te0 + 1 + 8 * (lt + 1))
                        s_td[lt % 2].inc(e.dma_start(out=H["xtmi"].ap()[128 * lt:128 * (lt + 1), :],
                                             in_=tmt[lt % 2][:]), 16)

                @blk.gpsimd
                def _(e):
                    e.wait_ge(s_td[0].h, td0[0] + 16 * (NTI // 2))
                    e.wait_ge(s_td[1].h, td0[1] + 16 * (NTI // 2))
                    s_tg.inc(e.collective_compute(
                        "AllGather", ALU.bypass, replica_groups=RG,
                        ins=[H["xtmi"].ap().opt()], outs=[H["xtma"].ap().opt()]), 1)

        with contextlib.ExitStack() as sc:
            sb = lambda n, s, d: sc.enter_context(nc.sbuf_tensor(f"{tag}_{n}", s, d))
            rwt_sb = sb("rwt", [128, NCH * NE], BF16)
            gb_sb = sb("gb", [1, NE], BF16)
            gbbc = sb("gbbc", [128, NE], F32)
            premask = sb("premask", [128, 16], BF16)
            triu_sb = sb("triu", [128, 128], BF16)
            ecol_sb = sb("ecol", [128, NE], F32)
            gtidf = sb("gtidf", [128, NTI], F32)
            lg = sb("lg", [128, NE], F32)
            negm = sb("negm", [128, 1], F32)
            e_ = sb("e_", [128, NE], F32)
            esum = sb("esum", [128, 1], F32)
            tp_all = sb("tp_all", [128, 16], F32)
            oh_all = sb("oh_all", [128, 16 * NE], F32)
            ohb = sb("ohb", [128, NE], BF16)
            inc_all = sb("incall", [128, 16 * NE], F32)
            tot16 = sb("tot16", [16, NE], F32)
            gtb = sb("gtb", [128, NE], BF16)
            offs16 = sb("offs16", [16, NE], F32)
            offrow = sb("offrow", [1, NE], F32)
            offbc = sb("offbc", [128, NE], F32)
            ra_ = sb("ra_", [128, NE], F32)
            krank = sb("krank", [128, 1], F32)
            keep = sb("keep", [128, 1], F32)
            ei_ = sb("ei_", [128, 1], F32)
            s2_ = sb("s2_", [128, 1], F32)
            s3_ = sb("s3_", [128, 1], F32)
            sxf = sb("sxf", [128, NTI], F32)

            s_rld = P.sem("rt_ld")
            s_rmm = P.sem("rt_mm")
            s_rv = P.sem("rt_v")
            s_ra = P.sem("rt_a")
            s_cs = P.sem("r1_cs")
            s_ce = P.sem("r1_ce")
            s_tt = P.sem("r1_tt")
            s_ag = P.sem("r1_ag")
            s_of = P.sem("r1_of")
            s_obd = P.sem("r1_obd")
            s_obm = P.sem("r1_obm")
            s_sc = P.sem("a_cx")
            s_tar = P.sem("a_cxe")
            rld0, rmm0, rv0, ra0, cs0, ce0, tt0, ag0, of0, obd0, obm0, sc0, tar0 = P.sem0(
                "rt_ld", "rt_mm", "rt_v", "rt_a", "r1_cs", "r1_ce", "r1_tt", "r1_ag",
                "r1_of", "r1_obd", "r1_obm", "a_cx", "a_cxe")

            ps_r = P.ps[6]
            psc = [P.ps[4], P.ps[5]]
            VC2 = 12
            rvB0 = rv0 + 1 + 5 * NTI + 1
            SCV = [rvB0 + VC2 * (i + 1) for i in range(NTI)]

            s_sld = P.sem("a_ws")
            sld0 = P.sems["a_ws"].n
            s_agc = P.sem("a_kv")
            agc0 = P.sems["a_kv"].n
            with nc.Block() as blk:
                @blk.sync
                def _(e):
                    s_sld.inc(e.dma_start(out=premask[:], in_=W["premask"][:, :]), 16)
                    s_sld.inc(e.dma_start(
                        out=triu_sb[:],
                        in_=P.const("c_triu_b", np.triu(np.ones((128, 128), np.float32)).astype(BF))[:, :]), 16)
                    s_sld.inc(e.dma_start(
                        out=ecol_sb[:],
                        in_=P.const("c_ecol", np.tile(np.arange(8, dtype=np.float32), (128, 1)))[:, :]), 16)
                    s_sld.inc(e.dma_start(out=gtidf[:], in_=W["gtidf"][:, :]), 16)
                    for i in range(NTI):
                        e.wait_ge(s_ce.h, ce0 + i + 1)
                        s_tt.inc(e.dma_start(out=tot16[i:i + 1, :], in_=inc_all[127:128, NE * i:NE * (i + 1)]), 16)
                    for i in range(NTI):
                        e.wait_ge(s_of.h, of0 + 2)
                        if i >= 1:
                            e.wait_ge(s_obm.h, obm0 + i)
                        s_obd.inc(e.dma_start(out=offrow[:], in_=offs16[i:i + 1, :]), 16)

                @blk.gpsimd
                def _(e):
                    s_rld.inc(e.dma_start(out=rwt_sb[:].rearrange("p (c ee) -> p c ee", c=NCH),
                                          in_=gwt_d.rearrange("(c p) ee -> p c ee", p=128)), 16)
                    s_rld.inc(e.dma_start(out=gb_sb[:], in_=gbr_d[:, :]), 16)
                    e.wait_ge(s_tt.h, tt0 + 16 * NTI)
                    s_ag.inc(e.dma_start(out=H["t16"].ap()[:, :], in_=tot16[:]), 16)
                    e.wait_ge(s_ag.h, ag0 + 16)
                    s_agc.inc(e.collective_compute(
                        "AllGather", ALU.bypass, replica_groups=RG,
                        ins=[H["t16"].ap().opt()], outs=[H["gt"].ap().opt()]), 1)
                    e.wait_ge(s_agc.h, agc0 + 1)
                    s_ag.inc(e.dma_start(out=gtb[:], in_=H["gt"].ap()[:, :]), 16)
                    rbnd = e.alloc_register()
                    e.reg_mov(rbnd, 8 * CAP - 1)
                    for i in range(NTI):
                        e.wait_ge(s_tz.h, tz0 + 16)
                        e.wait_ge(s_rv.h, SCV[i])
                        s_sc.inc(e.indirect_dma_start(
                            out=H["tbl"].ap(), out_offset=bass.IndirectOffsetOnAxis(sidxi[:, i:i + 1], 0),
                            in_=gtidf[:, i:i + 1], in_offset=None,
                            bounds_check=rbnd, oob_is_err=False), 16)
                    e.free_register(rbnd)
                    e.wait_ge(s_sc.h, sc0 + 16 * NTI)
                    s_tar.inc(e.collective_compute(
                        "AllReduce", ALU.add, replica_groups=RG,
                        ins=[H["tbl"].ap().opt()], outs=[H["tblo"].ap().opt()]), 1)

                @blk.tensor
                def _(e):
                    e.wait_ge(s_rld.h, rld0 + 32)
                    e.wait_ge(s_sld.h, sld0 + 64)
                    e.wait_ge(P.s_const.h, 96)
                    s_rmm.inc(e.matmul(ps_r[:, 0:NE], P.onesbr[:], gb_sb[:], start=True, stop=True))

                    def logits(i):
                        if i >= 1:
                            e.wait_ge(s_rv.h, rv0 + 5 * i - 3)
                        else:
                            e.wait_ge(s_rv.h, rv0 + 1)
                        for c in range(NCH):
                            ii = e.matmul(ps_r[:, 0:NE], xln[c][:, 128 * i:128 * (i + 1)],
                                          rwt_sb[:, NE * c:NE * (c + 1)],
                                          start=(c == 0), stop=(c == NCH - 1))
                            if c == NCH - 1:
                                s_rmm.inc(ii)

                    def cumsum(i):
                        e.wait_ge(s_rv.h, rv0 + 5 * i + 5)
                        if i >= 2:
                            e.wait_ge(s_ce.h, ce0 + i - 1)
                        s_cs.inc(e.matmul(psc[i % 2][:, 0:NE], triu_sb[:], ohb[:],
                                          start=True, stop=True))

                    logits(0)
                    for i in range(1, NTI):
                        logits(i)
                        cumsum(i - 1)
                    cumsum(NTI - 1)
                    e.wait_ge(s_ag.h, ag0 + 32)
                    s_of.inc(e.matmul(ps_r[0:16, 0:NE], premask[:], gtb[:], start=True, stop=True))
                    for i in range(NTI):
                        e.wait_ge(s_obd.h, obd0 + 16 * (i + 1))
                        if i >= 2:
                            e.wait_ge(s_rv.h, rvB0 + VC2 * (i - 2) + 1)
                        s_obm.inc(e.matmul(psc[i % 2][:, 0:NE], P.onesf[:], offrow[:],
                                           start=True, stop=True))

                @blk.vector
                def _(e):
                    st = chained(e, s_rv)
                    e.wait_ge(s_rmm.h, rmm0 + 1)
                    st(e.tensor_copy(out=gbbc[:], in_=ps_r[:, 0:NE]))
                    for i in range(NTI):
                        e.wait_ge(s_rmm.h, rmm0 + i + 2)
                        st(e.tensor_tensor(out=lg[:], in0=ps_r[:, 0:NE], in1=gbbc[:], op=ALU.add))
                        st(e.reduce_max(out=negm[:], in_=lg[:], axis=AX.X, negate=True))
                        e.wait_ge(s_ra.h, ra0 + i + 1)
                        st(e.tensor_single_scalar(out=oh_all[:, NE * i:NE * (i + 1)], in_=e_[:],
                                                  scalar=1.0, op=ALU.is_ge))
                        st(e.tensor_copy(out=ohb[:], in_=oh_all[:, NE * i:NE * (i + 1)]))
                        st(e.reciprocal(out=tp_all[:, i:i + 1], in_=esum[:]))
                        e.wait_ge(s_cs.h, cs0 + i + 1)
                        s_ce.inc(e.tensor_copy(out=inc_all[:, NE * i:NE * (i + 1)],
                                               in_=psc[i % 2][:, 0:NE]))
                        e.wait_ge(s_ce.h, s_ce.n)
                    e.wait_ge(s_of.h, of0 + 1)
                    st(e.tensor_copy(out=offs16[:], in_=ps_r[0:16, 0:NE]))
                    s_of.inc(e.engine_nop())
                    assert P.sems["rt_v"].n == rvB0, (P.sems["rt_v"].n, rvB0)
                    for i in range(NTI):
                        e.wait_ge(s_obm.h, obm0 + i + 1)
                        st(e.tensor_tensor(out=offbc[:], in0=inc_all[:, NE * i:NE * (i + 1)],
                                           in1=psc[i % 2][:, 0:NE], op=ALU.add))
                        st(e.tensor_tensor(out=ra_[:], in0=oh_all[:, NE * i:NE * (i + 1)],
                                           in1=offbc[:], op=ALU.mult))
                        st(e.reduce_sum(out=krank[:], in_=ra_[:], axis=AX.X))
                        st(e.tensor_single_scalar(out=keep[:], in_=krank[:], scalar=CAP + 0.5, op=ALU.is_lt))
                        st(e.tensor_tensor(out=wcomb[:, i:i + 1], in0=tp_all[:, i:i + 1],
                                           in1=keep[:], op=ALU.mult))
                        st(e.tensor_tensor(out=ra_[:], in0=oh_all[:, NE * i:NE * (i + 1)],
                                           in1=ecol_sb[:], op=ALU.mult))
                        st(e.reduce_sum(out=ei_[:], in_=ra_[:], axis=AX.X))
                        st(e.tensor_scalar_mul(out=s2_[:], in0=ei_[:], scalar1=float(CAP)))
                        st(e.tensor_tensor(out=s3_[:], in0=s2_[:], in1=krank[:], op=ALU.add))
                        st(e.tensor_scalar_add(out=s2_[:], in0=s3_[:], scalar1=BIGF - 1.0))
                        st(e.scalar_tensor_tensor(out=sxf[:, i:i + 1], in0=keep[:], scalar=-BIGF,
                                                  in1=s2_[:], op0=ALU.mult, op1=ALU.add))
                        st(e.tensor_copy(out=sidxi[:, i:i + 1], in_=sxf[:, i:i + 1]))
                        assert P.sems["rt_v"].n == SCV[i]

                @blk.scalar
                def _(e):
                    for i in range(NTI):
                        e.wait_ge(s_rv.h, rv0 + 5 * i + 3)
                        s_ra.inc(e.activation(out=e_[:], in_=lg[:], func=AF.Exp, bias=negm[:, 0:1],
                                              accum_out=esum[:, 0:1]))

        s_gi = P.sem("a_wo")
        s_gd = P.sem("a_wod")
        s_gx = P.sem("a_pk")
        s_ge = P.sem("a_pe")
        s_gp = [P.sem("t_slA"), P.sem("t_slB")]
        gi0, gd0, gx0, ge0 = P.sem0("a_wo", "a_wod", "a_pk", "a_pe")
        gp0 = [P.sems["t_slA"].n, P.sems["t_slB"].n]
        with contextlib.ExitStack() as sc_mlp:
            sbg = lambda n, s, d: sc_mlp.enter_context(nc.sbuf_tensor(f"{tag}_{n}", s, d))
            xbuf = [sbg(f"xbuf{c}", [128, CAP], BF16) for c in range(NCH)]
            eout_fm = [sbg(f"eofm{c}", [128, CAP], BF16) for c in range(NCH)]
            with contextlib.ExitStack() as sc_g:
                sbg2 = lambda n, s, d: sc_g.enter_context(nc.sbuf_tensor(f"{tag}_{n}", s, d))
                tbli = sbg2("tbli", [128, NTBL], mybir.dt.int32)
                idtf = sbg2("idtf", [128, NTBL], F32)
                idti = sbg2("idti", [128, NTBL], mybir.dt.int32)
                gdt = [sbg2(f"gdt{p}", [128, 1024], BF16) for p in range(2)]
                psT2 = sc_g.enter_context(nc.psum_tensor(f"{tag}_psT2", [128, 1024], BF16))

                with nc.Block() as blk:
                    @blk.gpsimd
                    def _(e):
                        s_gi.inc(e.dma_start(out=tbli[:], in_=W["tbli"][:, :]), 16)
                        e.wait_ge(s_gi.h, gi0 + 16)
                        e.wait_ge(s_tar.h, tar0 + 1)
                        e.wait_ge(s_tg.h, tg0 + 1)
                        flat = H["tblo"].ap().rearrange("r one -> (r one)").rearrange("(a b) -> a b", b=1)
                        for j in range(NTBL):
                            s_gi.inc(e.indirect_dma_start(
                                out=idtf[:, j:j + 1], out_offset=None,
                                in_=flat, in_offset=bass.IndirectOffsetOnAxis(tbli[:, j:j + 1], 0)), 16)
                        for j in range(NTBL):
                            e.wait_ge(s_gd.h, gd0 + j + 1)
                            if j >= 2:
                                e.wait_ge(s_gx.h, gx0 + 8 * (j - 1))
                            s_gp[j % 2].inc(e.indirect_dma_start(
                                out=gdt[j % 2][:], out_offset=None,
                                in_=H["xtma"].ap(), in_offset=bass.IndirectOffsetOnAxis(idti[:, j:j + 1], 0)), 16)

                    @blk.vector
                    def _(e):
                        e.wait_ge(s_gi.h, gi0 + 16 * (NTBL + 1))
                        for j in range(NTBL):
                            s_gd.inc(e.tensor_copy(out=idti[:, j:j + 1], in_=idtf[:, j:j + 1]))
                            e.wait_ge(s_gd.h, s_gd.n)
                        for j in range(NTBL):
                            e.wait_ge(s_gx.h, gx0 + 8 * (j + 1))
                            for c in range(NCH):
                                s_ge.inc(e.tensor_copy(out=xbuf[c][:, 128 * j:128 * (j + 1)],
                                                       in_=psT2[:, 128 * c:128 * (c + 1)]))
                                e.wait_ge(s_ge.h, s_ge.n)

                    @blk.tensor
                    def _(e):
                        for j in range(NTBL):
                            e.wait_ge(s_gp[j % 2].h, gp0[j % 2] + 16 * (j // 2 + 1))
                            if j >= 1:
                                e.wait_ge(s_ge.h, ge0 + 8 * j)
                            for c in range(NCH):
                                s_gx.inc(e.transpose(psT2[:, 128 * c:128 * (c + 1)],
                                                     gdt[j % 2][:, 128 * c:128 * (c + 1)],
                                                     P.identb[:]))

            s_w = [P.sem("m2_wA"), P.sem("m2_wB")]
            s_wd = [P.sem("m2_wdA"), P.sem("m2_wdB")]
            s_b = [P.sem("t_wl"), P.sem("t_ms")]
            s_g = P.sem("m2_g")
            s_sl = P.sem("m2_sl")
            s_o = P.sem("m2_o")
            s_oc = P.sem("m2_oc")
            s_eb = P.sem("a_av")
            s_et = P.sem("a_aa")
            s_ee = P.sem("a_kve")
            s_ed = [P.sem("t_xlA"), P.sem("t_xlB")]
            s_eg = P.sem("t_ag2")
            w0 = [P.sems["m2_wA"].n, P.sems["m2_wB"].n]
            wdd0 = [P.sems["m2_wdA"].n, P.sems["m2_wdB"].n]
            b0 = [P.sems["t_wl"].n, P.sems["t_ms"].n]
            g0, sl0, o0, oc0, eb0, et0, ee0, eg0 = P.sem0(
                "m2_g", "m2_sl", "m2_o", "m2_oc", "a_av", "a_aa", "a_kve", "t_ag2")
            ed0 = [P.sems["t_xlA"].n, P.sems["t_xlB"].n]
            NG1 = 16
            NGF1 = 2
            with contextlib.ExitStack() as sc_m:
                sbq = lambda n, s, d: sc_m.enter_context(nc.sbuf_tensor(f"{tag}_{n}", s, d))
                w1_b = [sbq(f"w1b{p}", [128, NCH * 256], BF16) for p in range(2)]
                wd_b = [sbq(f"wdq{p}", [128, 8 * 1024], BF16) for p in range(2)]
                h_sb = sbq("h", [128, 16 * TT], BF16)
                acc = [sbq(f"acc{c}", [128, TT], F32) for c in range(NCH)]
                b1_sb = sbq("b1", [128, NFC], F32)
                b2_sb = sbq("b2", [128, NCH], F32)
                psG = [P.ps[0], P.ps[1]]
                ps8 = sc_m.enter_context(nc.psum_tensor(f"{tag}_ps8", [128, TT], F32))
                PSO = P.ps[:7] + [ps8]

                mode = WC["mode"] if WC is not None else None
                if WC is not None:
                    s_wb = P.sem("m1_wb")
                    s_wdb = P.sem("m1_wdb")
                    wb0 = P.sems["m1_wb"].n
                    wdb0 = P.sems["m1_wdb"].n

                with nc.Block() as blk:
                    if mode == "fill":
                        @blk.sync
                        def _(e):
                            for g in range(NG1):
                                e.wait_ge(s_w[g % 2].h, w0[g % 2] + 16 * (g // 2 + 1))
                                s_wb.inc(e.dma_start(out=WC["w1bf"].ap()[:, 2048 * g:2048 * (g + 1)],
                                                     in_=w1_b[g % 2][:]), 16)
                            for q4 in range(4):
                                e.wait_ge(s_wd[q4 % 2].h, wdd0[q4 % 2] + 16 * (q4 // 2 + 1))
                                s_wdb.inc(e.dma_start(out=WC["w2bf"].ap()[:, 8192 * q4:8192 * (q4 + 1)],
                                                      in_=wd_b[q4 % 2][:]), 16)

                    @blk.gpsimd
                    def _(e):
                        s_b[0].inc(e.dma_start(out=b1_sb[:], in_=W["b1tc"][:, :]), 16)
                        s_b[0].inc(e.dma_start(out=b2_sb[:], in_=W["b2tc"][:, :]), 16)
                        if mode == "use":
                            e.wait_ge(s_wb.h, WC["wb_w1"])
                            e.wait_ge(s_wdb.h, WC["wb_wd"])
                        for th in range(NTH1):
                            for hp in range(2):
                                for g8 in range(NG1 // 2):
                                    g = hp * (NG1 // 2) + g8
                                    gi = th * NG1 + g
                                    if gi >= 2:
                                        e.wait_ge(s_g.h, g0 + (gi - 1) * NGF1)
                                    if mode == "fill" and th == 0 and gi >= 2:
                                        e.wait_ge(s_wb.h, wb0 + 16 * (gi - 1))
                                    if mode == "fill" and th == 1 and g8 == 0 and hp == 0:
                                        e.wait_ge(s_wb.h, wb0 + 16 * NG1)
                                    fsl = slice(NGF1 * 128 * g, NGF1 * 128 * (g + 1))
                                    if mode == "use" or (mode == "fill" and th >= 1):
                                        s_w[gi % 2].inc(e.dma_start(
                                            out=w1_b[gi % 2][:],
                                            in_=WC["w1bf"].ap()[:, 2048 * g:2048 * (g + 1)]), 16)
                                    else:
                                        s_w[gi % 2].inc(e.dma_start(
                                            out=w1_b[gi % 2][:].rearrange("p (c f) -> p c f", c=NCH),
                                            in_=W["w1c"].rearrange("(c p) f -> p c f", p=128)[:, :, fsl]), 16)
                                for q in range(2):
                                    qi = (th * 2 + hp) * 2 + q
                                    if qi >= 2:
                                        e.wait_ge(s_o.h, o0 + 8 * (qi - 1))
                                    if mode == "fill" and th == 0 and qi >= 2:
                                        e.wait_ge(s_wdb.h, wdb0 + 16 * (qi - 1))
                                    if mode == "fill" and th == 1 and hp == 0 and q == 0:
                                        e.wait_ge(s_wdb.h, wdb0 + 64)
                                    if mode == "use" or (mode == "fill" and th >= 1):
                                        q4 = hp * 2 + q
                                        s_wd[qi % 2].inc(e.dma_start(
                                            out=wd_b[qi % 2][:],
                                            in_=WC["w2bf"].ap()[:, 8192 * q4:8192 * (q4 + 1)]), 16)
                                    else:
                                        s_wd[qi % 2].inc(e.dma_start(
                                            out=wd_b[qi % 2][:].rearrange("p (fc o) -> p fc o", fc=8),
                                            in_=W["w2c"].rearrange("(fc p) o -> p fc o", p=128)[:, 16 * hp + 8 * q:16 * hp + 8 * (q + 1), :]), 16)
                        if mode == "fill":
                            WC["wb_w1"] = P.sems["m1_wb"].n
                            WC["wb_wd"] = P.sems["m1_wdb"].n

                    @blk.tensor
                    def _(e):
                        e.wait_ge(s_b[0].h, b0[0] + 32)
                        gstage = 0
                        dstage = 0
                        for th in range(NTH1):
                            col = TT * th
                            for hp in range(2):
                                if th * 2 + hp >= 1:
                                    e.wait_ge(s_oc.h, oc0 + 8 * dstage)
                                for g in range(NG1 // 2):
                                    gi = th * NG1 + hp * (NG1 // 2) + g
                                    e.wait_ge(s_w[gi % 2].h, w0[gi % 2] + 16 * (gi // 2 + 1))
                                    if gi == 0:
                                        e.wait_ge(s_ge.h, ge0 + 8 * NTBL)
                                    for f2 in range(NGF1):
                                        fc = (hp * (NG1 // 2) + g) * NGF1 + f2
                                        spar = gstage % 2
                                        if gstage >= 2:
                                            e.wait_ge(s_sl.h, sl0 + gstage - 1)
                                        for dk in range(NCH):
                                            ii = e.matmul(psG[spar][:, :],
                                                          w1_b[gi % 2][:, 256 * dk + 128 * f2:256 * dk + 128 * f2 + 128],
                                                          xbuf[dk][:, col:col + TT],
                                                          start=(dk == 0), stop=(dk == NCH - 1))
                                            if dk == NCH - 1:
                                                s_g.inc(ii)
                                        gstage += 1
                                e.wait_ge(s_sl.h, sl0 + gstage)
                                for q in range(2):
                                    qi = (th * 2 + hp) * 2 + q
                                    e.wait_ge(s_wd[qi % 2].h, wdd0[qi % 2] + 16 * (qi // 2 + 1))
                                    if dstage >= 1:
                                        e.wait_ge(s_oc.h, oc0 + 8 * dstage)
                                    for fcl in range(8):
                                        fh = q * 8 + fcl
                                        for do in range(NCH):
                                            ii = e.matmul(PSO[do][:, :],
                                                          wd_b[qi % 2][:, 1024 * fcl + 128 * do:1024 * fcl + 128 * do + 128],
                                                          h_sb[:, TT * fh:TT * (fh + 1)],
                                                          start=(fcl == 0), stop=(fcl == 7))
                                            if fcl == 7:
                                                s_o.inc(ii)
                                    dstage += 1

                    @blk.scalar
                    def _(e):
                        gstage = 0
                        for th in range(NTH1):
                            for fc in range(NFC):
                                e.wait_ge(s_g.h, g0 + gstage + 1)
                                if gstage % 16 == 0 and gstage >= 16:
                                    e.wait_ge(s_o.h, o0 + 16 * (gstage // 16))
                                spar = gstage % 2
                                fh = fc % 16
                                s_sl.inc(e.activation(out=h_sb[:, TT * fh:TT * (fh + 1)],
                                                      in_=psG[spar][:, :], func=AF.Relu,
                                                      bias=b1_sb[:, fc:fc + 1]))
                                gstage += 1

                    @blk.vector
                    def _(e):
                        dstage = 0
                        for th in range(NTH1):
                            for hp in range(2):
                                for q in range(2):
                                    for do in range(NCH):
                                        e.wait_ge(s_o.h, o0 + 8 * dstage + do + 1)
                                        if hp == 0 and q == 0:
                                            s_oc.inc(e.tensor_copy(out=acc[do][:], in_=PSO[do][:, :]))
                                        else:
                                            s_oc.inc(e.tensor_tensor(out=acc[do][:], in0=acc[do][:],
                                                                     in1=PSO[do][:, :], op=ALU.add))
                                        e.wait_ge(s_oc.h, s_oc.n)
                                    dstage += 1
                            for c in range(NCH):
                                s_eb.inc(e.tensor_scalar_add(out=eout_fm[c][:, TT * th:TT * (th + 1)],
                                                             in0=acc[c][:],
                                                             scalar1=b2_sb[:, c:c + 1]))
                                e.wait_ge(s_eb.h, s_eb.n)

            with contextlib.ExitStack() as sc_e:
                emt = [sc_e.enter_context(nc.sbuf_tensor(f"{tag}_emt{p}", [128, 1024], BF16))
                       for p in range(2)]
                psT3 = sc_e.enter_context(nc.psum_tensor(f"{tag}_psT3", [128, 1024], BF16))
                with nc.Block() as blk:
                    @blk.tensor
                    def _(e):
                        for k in range(4 * NTH1):
                            th = k // 4
                            it = k % 4
                            e.wait_ge(s_eb.h, eb0 + 8 * (th + 1))
                            if k >= 1:
                                e.wait_ge(s_ee.h, ee0 + 8 * k)
                            for c in range(NCH):
                                s_et.inc(e.transpose(psT3[:, 128 * c:128 * (c + 1)],
                                                     eout_fm[c][:, 128 * k:128 * (k + 1)],
                                                     P.identb[:]))

                    @blk.vector
                    def _(e):
                        for k in range(4 * NTH1):
                            e.wait_ge(s_et.h, et0 + 8 * (k + 1))
                            if k >= 2:
                                e.wait_ge(s_ed[k % 2].h, ed0[k % 2] + 16 * (k // 2))
                            for c in range(NCH):
                                s_ee.inc(e.tensor_copy(out=emt[k % 2][:, 128 * c:128 * (c + 1)],
                                                       in_=psT3[:, 128 * c:128 * (c + 1)]))
                                e.wait_ge(s_ee.h, s_ee.n)

                    @blk.sync
                    def _(e):
                        for k in range(4 * NTH1):
                            e.wait_ge(s_ee.h, ee0 + 8 * (k + 1))
                            s_ed[k % 2].inc(e.dma_start(out=H["eoi"].ap()[128 * k:128 * (k + 1), :],
                                                 in_=emt[k % 2][:]), 16)

                    @blk.gpsimd
                    def _(e):
                        e.wait_ge(s_ed[0].h, ed0[0] + 16 * 2 * NTH1)
                        e.wait_ge(s_ed[1].h, ed0[1] + 16 * 2 * NTH1)
                        s_eg.inc(e.collective_compute(
                            "AllGather", ALU.bypass, replica_groups=RG,
                            ins=[H["eoi"].ap().opt()], outs=[H["eoa"].ap().opt()]), 1)

        s_bk = [P.sem("t_wsA"), P.sem("t_wsB")]
        s_bv = P.sem("t_wv")
        s_bt = P.sem("t_nm")
        s_by = P.sem("t_wm")
        s_xl = [P.sem("fin_lA"), P.sem("fin_lB")]
        s_ys = [P.sem("fin_sA"), P.sem("fin_sB")]
        bv0, bt0, by0 = P.sem0("t_wv", "t_nm", "t_wm")
        bk0 = [P.sems["t_wsA"].n, P.sems["t_wsB"].n]
        xl0 = [P.sems["fin_lA"].n, P.sems["fin_lB"].n]
        with contextlib.ExitStack() as sc_b:
            sbb = lambda n, s, d: sc_b.enter_context(nc.sbuf_tensor(f"{tag}_{n}", s, d))
            xf = [sbb(f"xf{c}", [128, TPC], F32) for c in range(NCH)]
            yrow = [sbb(f"yr{c}", [128, TPC], F32) for c in range(NCH)]
            g1 = [sbb(f"g1{p}", [128, 1024], BF16) for p in range(2)]
            gs = [sbb(f"gs{p}", [128, 1024], BF16) for p in range(2)]
            psT4 = sc_b.enter_context(nc.psum_tensor(f"{tag}_psT4", [128, 1024], BF16))

            with nc.Block() as blk:
                @blk.sync
                def _(e):
                    for c in range(NCH):
                        s_xl[c % 2].inc(e.dma_start(out=xf[c][:], in_=x_dram[128 * c:128 * (c + 1), :]), 16)

                @blk.gpsimd
                def _(e):
                    e.wait_ge(s_eg.h, eg0 + 1)
                    rbnd2 = e.alloc_register()
                    e.reg_mov(rbnd2, 8 * CAP - 1)
                    for i in range(NTI):
                        if i >= 2:
                            e.wait_ge(s_bv.h, bv0 + 2 * (i - 1))
                        s_bk[i % 2].inc(e.dma_start(out=g1[i % 2][:], in_=H["eoa"].ap()[0:128, :]), 16)
                        e.wait_ge(s_bk[i % 2].h, P.sems["t_wsA" if i % 2 == 0 else "t_wsB"].n)
                        s_bk[i % 2].inc(e.indirect_dma_start(
                            out=g1[i % 2][:], out_offset=None,
                            in_=H["eoa"].ap(), in_offset=bass.IndirectOffsetOnAxis(sidxi[:, i:i + 1], 0),
                            bounds_check=rbnd2, oob_is_err=False), 16)
                    e.free_register(rbnd2)
                    e.wait_ge(s_by.h, by0 + NCH * NTI)
                    for c in range(NCH):
                        s_ys[c % 2].inc(e.dma_start(out=y_dram[128 * c:128 * (c + 1), :], in_=yrow[c][:]), 16)
                    e.wait_ge(s_ys[0].h, P.sems["fin_sA"].n)
                    e.wait_ge(s_ys[1].h, P.sems["fin_sB"].n)

                @blk.vector
                def _(e):
                    e.wait_ge(s_xl[0].h, xl0[0] + 64)
                    e.wait_ge(s_xl[1].h, xl0[1] + 64)
                    for i in range(NTI):
                        e.wait_ge(s_bk[i % 2].h, bk0[i % 2] + 32 * (i // 2 + 1))
                        s_bv.inc(e.tensor_scalar_mul(out=gs[i % 2][:], in0=g1[i % 2][:],
                                                     scalar1=wcomb[:, i:i + 1]))
                        e.wait_ge(s_bv.h, s_bv.n)
                        s_bv.inc(e.engine_nop())
                        e.wait_ge(s_bt.h, bt0 + 8 * (i + 1))
                        for c in range(NCH):
                            s_by.inc(e.scalar_tensor_tensor(
                                out=yrow[c][:, 128 * i:128 * (i + 1)], in0=psT4[:, 128 * c:128 * (c + 1)],
                                scalar=0.5, in1=xf[c][:, 128 * i:128 * (i + 1)],
                                op0=ALU.mult, op1=ALU.add))
                            e.wait_ge(s_by.h, s_by.n)

                @blk.tensor
                def _(e):
                    for i in range(NTI):
                        e.wait_ge(s_bv.h, bv0 + 2 * i + 1)
                        if i >= 1:
                            e.wait_ge(s_by.h, by0 + NCH * i)
                        for c in range(NCH):
                            s_bt.inc(e.transpose(psT4[:, 128 * c:128 * (c + 1)],
                                                 gs[i % 2][:, 128 * c:128 * (c + 1)],
                                                 P.identb[:]))

    return P.sem("fin_sA")




def build_moe1s_prog():
    P = Prog()
    nc = P.nc
    x = nc.dram_tensor("x", [D, TPC], F32, kind="ExternalInput").ap()
    y = nc.dram_tensor("y", [D, TPC], F32, kind="ExternalOutput").ap()
    gwt = nc.dram_tensor("gwt", [D, NE], F32, kind="ExternalInput").ap()
    gbr = nc.dram_tensor("gbr", [1, NE], F32, kind="ExternalInput").ap()
    W = {
        "premask": nc.dram_tensor("premask", [128, 16], BF16, kind="ExternalInput").ap(),
        "gtidf": nc.dram_tensor("gtidf", [128, 16], F32, kind="ExternalInput").ap(),
        "tbli": nc.dram_tensor("tbli", [128, CAP // 128], mybir.dt.int32, kind="ExternalInput").ap(),
        "w1c": nc.dram_tensor("w1c", [D, FF], F32, kind="ExternalInput").ap(),
        "b1tc": nc.dram_tensor("b1tc", [128, NFC], F32, kind="ExternalInput").ap(),
        "w2c": nc.dram_tensor("w2c", [FF, D], F32, kind="ExternalInput").ap(),
        "b2tc": nc.dram_tensor("b2tc", [128, NCH], F32, kind="ExternalInput").ap(),
    }
    H = {
        "xtmi": nc.dram_tensor("xtmi", [TPC, D], BF16, kind="Internal"),
        "xtma": nc.dram_tensor("xtma", [T, D], BF16, kind="Internal"),
        "tbl": nc.dram_tensor("tbl", [8 * CAP, 1], F32, kind="Internal"),
        "tblo": nc.dram_tensor("tblo", [8 * CAP, 1], F32, kind="Internal"),
        "eoi": nc.dram_tensor("eoi", [CAP, D], BF16, kind="Internal"),
        "eoa": nc.dram_tensor("eoa", [8 * CAP, D], BF16, kind="Internal"),
        "t16": nc.dram_tensor("t16", [16, NE], F32, kind="Internal"),
        "gt": nc.dram_tensor("gt", [128, NE], F32, kind="Internal"),
    }
    emit_moe1_sparse(P, "s1", x, y, gwt, gbr, W, H)
    return P


def moe1s_core_inputs(c, gate_w, gate_b, w1, b1, w2, b2):
    gtidf = np.zeros((128, 16), np.float32)
    tbli = np.zeros((128, CAP // 128), np.int32)
    for i in range(16):
        for p in range(128):
            gtidf[p, i] = TPC * c + 128 * i + p
    for j in range(CAP // 128):
        for p in range(128):
            tbli[p, j] = CAP * c + 128 * j + p
    return {
        "gwt": np.ascontiguousarray(gate_w.T.astype(np.float32)),
        "gbr": np.ascontiguousarray(gate_b.astype(np.float32).reshape(1, NE)),
        "premask": premask_for_core(c),
        "gtidf": gtidf, "tbli": tbli,
        "w1c": np.ascontiguousarray(w1[c].astype(np.float32)),
        "b1tc": np.ascontiguousarray(b1[c].reshape(NFC, 128).T.astype(np.float32)),
        "w2c": np.ascontiguousarray(w2[c].astype(np.float32)),
        "b2tc": np.ascontiguousarray(b2[c].reshape(NCH, 128).T.astype(np.float32)),
    }


def run_moe1s(x_fm_shards, gate_w, gate_b, w1, b1, w2, b2):
    if "moe1s" not in _CACHE:
        _CACHE["moe1s"] = build_moe1s_prog()
    P = _CACHE["moe1s"]
    in_maps = []
    for c, xs in enumerate(x_fm_shards):
        m = moe1s_core_inputs(c, gate_w, gate_b, w1, b1, w2, b2)
        m["x"] = np.ascontiguousarray(xs)
        in_maps.append(m)
    res = run_bass_kernel_spmd(P.nc, in_maps, core_ids=list(range(NCORES)))
    return [r["y"] for r in res.results]

def build_full_prog():
    P = Prog()
    nc = P.nc
    DT = lambda n, s, d=F32, k="ExternalInput": nc.dram_tensor(n, s, d, kind=k)
    x0 = DT("x", [D, TPC]).ap()
    y = DT("y", [D, TPC], F32, "ExternalOutput").ap()
    mids = [DT(f"mid{i}", [D, TPC], F32, "Internal").ap() for i in range(5)]

    gwt = DT("gwt", [D, NE]).ap()
    gbr = DT("gbr", [1, NE]).ap()
    pm = DT("premask", [128, 16], BF16).ap()
    A0 = {}
    for n, s in (("wq", [D, D]), ("wk", [D, D]), ("wv", [D, D]), ("wo", [D, D]),
                 ("bqr", [1, D]), ("bkr", [1, D]), ("bvr", [1, D]),
                 ("bot", [128, NCH]), ("g1t", [128, NCH]), ("be1t", [128, NCH]),
                 ("g2t", [128, NCH]), ("be2t", [128, NCH]),
                 ("w1", [D, FF]), ("b1t", [128, NFC]), ("w2", [FF, D]), ("b2t", [128, NCH]),
                 ("bsel", [64, 2])):
        A0[n] = DT("i0_" + n, s).ap()
    for n, s in (("pre1", [D, TPC]), ("pre2", [D, TPC]), ("kvi", [64, 2048]), ("kvo", [64, 2048])):
        A0[n] = DT("i0_" + n, s, F32, "Internal")
    rwt = DT("rwt", [D, NE]).ap()
    wg = DT("wg", [NE, D, FF]).ap()
    wu = DT("wu", [NE, D, FF]).ap()
    wd = DT("wd", [NE, FF, D]).ap()
    T1 = {}
    for n, s in (("wqc", [D, 128]), ("wkc", [D, 128]), ("wvc", [D, 128]),
                 ("bqt", [128, 1]), ("bkt", [128, 1]), ("bvr", [1, 128]),
                 ("wout", [D, D]), ("boutt", [128, NCH])):
        T1[n] = DT("i1_" + n, s).ap()
    T1["ctxidx"] = DT("i1_ctxidx", [128, 8], mybir.dt.int32).ap()
    T1["xagi"] = DT("i1_xagi", [D, TPC], BF16, "Internal")
    T1["xago"] = DT("i1_xago", [NCORES * D, TPC], BF16, "Internal")
    T1["cagi"] = DT("i1_cagi", [128, T], BF16, "Internal")
    T1["cago"] = DT("i1_cago", [D, T], BF16, "Internal")
    WS = {
        "premask": pm,
        "gtidf": DT("gtidf", [128, 16]).ap(),
        "tbli": DT("tbli", [128, CAP // 128], mybir.dt.int32).ap(),
        "w1c": DT("w1c", [D, FF]).ap(),
        "b1tc": DT("b1tc", [128, NFC]).ap(),
        "w2c": DT("w2c", [FF, D]).ap(),
        "b2tc": DT("b2tc", [128, NCH]).ap(),
    }
    def mkH(u):
        return {
            "xtmi": DT(f"xtmi{u}", [TPC, D], BF16, "Internal"),
            "xtma": DT(f"xtma{u}", [T, D], BF16, "Internal"),
            "tbl": DT(f"tbl{u}", [8 * CAP, 1], F32, "Internal"),
            "tblo": DT(f"tblo{u}", [8 * CAP, 1], F32, "Internal"),
            "eoi": DT(f"eoi{u}", [CAP, D], BF16, "Internal"),
            "eoa": DT(f"eoa{u}", [8 * CAP, D], BF16, "Internal"),
            "t16": DT(f"t16{u}", [16, NE], F32, "Internal"),
            "gt": DT(f"gt{u}", [128, NE], F32, "Internal"),
        }

    WC = {"mode": "fill",
          "w1bf": DT("w1bfc", [128, 16 * 2048], BF16, "Internal"),
          "w2bf": DT("w2bfc", [128, 4 * 8192], BF16, "Internal")}
    emit_moe1_sparse(P, "m1a", x0, mids[0], gwt, gbr, WS, mkH("a"), WC=WC)
    WC["mode"] = "use"
    emit_l0attn(P, "a0", mids[0], mids[1], A0)
    emit_moe1_sparse(P, "m1b", mids[1], mids[2], gwt, gbr, WS, mkH("b"), WC=WC)
    emit_moe2(P, "m2a", mids[2], mids[3], rwt, wg, wu, wd)
    emit_l1attn(P, "t1", mids[3], mids[4], T1)
    emit_moe2(P, "m2b", mids[4], y, rwt, wg, wu, wd)
    return P


def full_inmaps(inputs, shards):
    g = lambda n: np.asarray(inputs[n], np.float32)
    t8 = lambda v: np.ascontiguousarray(v.reshape(NCH, 128).T.astype(np.float32))
    b1 = g("L0_ffn_b1")
    Wqkv = g("L1_attn_Wqkv")
    bqkv = g("L1_attn_bqkv")
    base = {
        "gwt": np.ascontiguousarray(g("L0_ffn_gate_w").T),
        "gbr": np.ascontiguousarray(g("L0_ffn_gate_b").reshape(1, NE)),
        "i0_wq": g("L0_attn_Wq"), "i0_wk": g("L0_attn_Wk"),
        "i0_wv": g("L0_attn_Wv"), "i0_wo": g("L0_attn_Wo"),
        "i0_bqr": g("L0_attn_bq").reshape(1, D),
        "i0_bkr": g("L0_attn_bk").reshape(1, D),
        "i0_bvr": g("L0_attn_bv").reshape(1, D),
        "i0_bot": t8(g("L0_attn_bo")),
        "i0_g1t": t8(g("L0_attn_g1")), "i0_be1t": t8(g("L0_attn_be1")),
        "i0_g2t": t8(g("L0_attn_g2")), "i0_be2t": t8(g("L0_attn_be2")),
        "i0_w1": g("L0_attn_W1"),
        "i0_b1t": np.ascontiguousarray(g("L0_attn_b1").reshape(NFC, 128).T),
        "i0_w2": g("L0_attn_W2"),
        "i0_b2t": t8(g("L0_attn_b2")),
        "rwt": np.ascontiguousarray(g("L1_ffn_router_w").T),
        "wg": g("L1_ffn_w_gate"), "wu": g("L1_ffn_w_up"), "wd": g("L1_ffn_w_down"),
        "i1_wout": g("L1_attn_Wout"),
        "i1_boutt": t8(g("L1_attn_bout")),
    }
    maps = []
    for c in range(NCORES):
        b = c // 4
        bsel = np.zeros((64, 2), np.float32)
        bsel[:, b] = 1.0
        idx = np.zeros((128, 8), np.int32)
        for rc in range(8):
            for p in range(128):
                idx[p, rc] = (128 * rc + p) * 8 + c
        m = dict(base)
        m["x"] = np.ascontiguousarray(shards[c])
        m["premask"] = premask_for_core(c)
        mc = moe1s_core_inputs(c, g("L0_ffn_gate_w"), g("L0_ffn_gate_b"),
                               g("L0_ffn_w1"), b1, g("L0_ffn_w2"), g("L0_ffn_b2"))
        for kk in ("gtidf", "tbli", "w1c", "b1tc", "w2c", "b2tc"):
            m[kk] = mc[kk]
        m["i0_bsel"] = bsel
        m["i1_wqc"] = np.ascontiguousarray(Wqkv[:, 128 * c:128 * (c + 1)])
        m["i1_wkc"] = np.ascontiguousarray(Wqkv[:, 1024 + 128 * c:1024 + 128 * (c + 1)])
        m["i1_wvc"] = np.ascontiguousarray(Wqkv[:, 2048 + 128 * c:2048 + 128 * (c + 1)])
        m["i1_bqt"] = np.ascontiguousarray(bqkv[128 * c:128 * (c + 1)].reshape(128, 1))
        m["i1_bkt"] = np.ascontiguousarray(bqkv[1024 + 128 * c:1024 + 128 * (c + 1)].reshape(128, 1))
        m["i1_bvr"] = np.ascontiguousarray(bqkv[2048 + 128 * c:2048 + 128 * (c + 1)].reshape(1, 128))
        m["i1_ctxidx"] = idx
        maps.append(m)
    return maps


def kernel(**inputs):
    x = np.asarray(inputs["L0_attn_x"], np.float32)
    xt = np.ascontiguousarray(x.reshape(T, D))
    shards = [np.ascontiguousarray(xt[i * TPC:(i + 1) * TPC].T) for i in range(NCORES)]
    if "full" not in _CACHE:
        _CACHE["full"] = build_full_prog()
    P = _CACHE["full"]
    res = run_bass_kernel_spmd(P.nc, full_inmaps(inputs, shards), core_ids=list(range(NCORES)))
    y = np.concatenate([np.asarray(r["y"]).T for r in res.results], axis=0)
    return y.reshape(2, 8192, 1024).astype(np.float32)


# version 08
# speedup vs baseline: 1.0206x; 1.0206x over previous
import contextlib
import numpy as np
import ml_dtypes
import concourse.bass as bass
import concourse.mybir as mybir
from concourse.bass_utils import run_bass_kernel_spmd

F32 = mybir.dt.float32
BF16 = mybir.dt.bfloat16
AF = mybir.ActivationFunctionType
ALU = mybir.AluOpType
AX = mybir.AxisListType

NCORES = 8
B, S, D = 2, 8192, 1024
T = B * S
TPC = T // NCORES
NCH = D // 128
FF = 4096
NFC = FF // 128
NH, HD = 16, 64
NE = 8
CAP = 2560
TT = 512
NTT = TPC // TT
TH = 1024
NTH = TPC // TH
TTH = TH // TT
LN_EPS = 1e-5
LN0_EPS = 1e-12
RG = [list(range(NCORES))]
BF = ml_dtypes.bfloat16


class Sem:
    def __init__(self, stack, nc, name):
        self.h = stack.enter_context(nc.semaphore(name))
        self.n = 0

    def inc(self, inst, k=1):
        inst.then_inc(self.h, k)
        self.n += k
        return self.n


class Prog:
    def __init__(self):
        self.nc = bass.Bass("TRN2", target_bir_lowering=False, debug=False)
        self.stack = contextlib.ExitStack()
        self.sems = {}
        self._consts = {}
        self.ps = [self.stack.enter_context(self.nc.psum_tensor(f"psf{i}", [128, TT], F32))
                   for i in range(7)]
        vals = [LN_EPS, LN0_EPS, -0.5, 0.5]
        arr = np.tile(np.asarray(vals, np.float32)[None, :], (128, 1))
        cst_d = self.nc.inline_tensor(np.ascontiguousarray(arr), "c_scalarpool").ap()
        self.cst = self.sbuf("cstpool", [128, len(vals)], F32)
        for i, v in enumerate(vals):
            self.nc.const_aps.register(F32, float(v), self.cst[:, i:i + 1])
        self.onesb = self.sbuf("c_onesb_sb", [128, 1], BF16)
        self.onesf = self.sbuf("c_onesf_sb", [1, 128], F32)
        self.onesbr = self.sbuf("c_onesbr_sb", [1, 128], BF16)
        self.identf = self.sbuf("c_identf_sb", [128, 128], F32)
        self.identb = self.sbuf("c_identb_sb", [128, 128], BF16)
        onesb_d = self.nc.inline_tensor(np.ones((128, 1), BF), "c_ones_128x1_b").ap()
        onesf_d = self.nc.inline_tensor(np.ones((1, 128), np.float32), "c_ones_1x128_f").ap()
        onesbr_d = self.nc.inline_tensor(np.ones((1, 128), BF), "c_ones_1x128_b").ap()
        identf_d = self.nc.inline_tensor(np.eye(128, dtype=np.float32), "c_ident_f").ap()
        identb_d = self.nc.inline_tensor(np.eye(128, dtype=BF), "c_ident_b").ap()
        self.s_const = self.sem("constld")
        with self.nc.Block() as blk:
            @blk.sync
            def _(e):
                self.s_const.inc(e.dma_start(out=self.cst[:], in_=cst_d[:, :]), 16)
                self.s_const.inc(e.dma_start(out=self.onesb[:], in_=onesb_d[:, :]), 16)
                self.s_const.inc(e.dma_start(out=self.onesf[:], in_=onesf_d[:, :]), 16)
                self.s_const.inc(e.dma_start(out=self.onesbr[:], in_=onesbr_d[:, :]), 16)
                self.s_const.inc(e.dma_start(out=self.identf[:], in_=identf_d[:, :]), 16)
                self.s_const.inc(e.dma_start(out=self.identb[:], in_=identb_d[:, :]), 16)

    def sem(self, name):
        if name not in self.sems:
            self.sems[name] = Sem(self.stack, self.nc, name)
        return self.sems[name]

    def sem0(self, *names):
        return tuple(self.sem(n).n for n in names)

    def sbuf(self, name, shape, dtype):
        return self.stack.enter_context(self.nc.sbuf_tensor(name, shape, dtype, side="right"))

    def const(self, name, arr):
        if name not in self._consts:
            self._consts[name] = self.nc.inline_tensor(np.ascontiguousarray(arr), name).ap()
        return self._consts[name]


def chained(eng, sem):
    def step(inst):
        sem.inc(inst)
        eng.wait_ge(sem.h, sem.n)
    return step


def emit_ln(P, tag, x_dram, xb, xln, sq, rows, tmp, eps=LN_EPS, gate=None, affine=None, vec_wait=None):
    nc = P.nc
    r_sx, r_sq, r_mean, r_var, r_a, r_b = rows
    c_onesb = P.onesb
    c_onesf = P.onesf

    s_ld = [P.sem(f"ln_ld{c}") for c in range(NCH)]
    s_sq = P.sem("ln_sq")
    s_mm = P.sem("ln_mm")
    s_ab = P.sem("ln_ab")
    s_act = P.sem("ln_act")
    s_bc = P.sem("ln_bc")
    s_out = P.sem("ln_out")
    ld0 = [P.sems[f"ln_ld{c}"].n for c in range(NCH)]
    sq0, mm0, ab0, act0, bc0, out0 = P.sem0(
        "ln_sq", "ln_mm", "ln_ab", "ln_act", "ln_bc", "ln_out")

    ps_sx = [P.ps[0][0:1, :], P.ps[2][0:1, :]]
    ps_sq = [P.ps[1][0:1, :], P.ps[3][0:1, :]]
    ps_a = P.ps[4]
    ps_b = P.ps[5]

    with nc.Block() as blk:
        @blk.sync
        def _(e):
            pass

        @blk.gpsimd
        def _(e):
            if gate is not None:
                for gs, gv in gate:
                    e.wait_ge(gs.h, gv)
            for c in range(NCH):
                s_ld[c].inc(e.dma_start(out=xb[c][:], in_=x_dram[128 * c:128 * (c + 1), :]), 16)

        @blk.vector
        def _(e):
            if vec_wait is not None:
                for gs, gv in vec_wait:
                    e.wait_ge(gs.h, gv)
            st = chained(e, s_ab)

            def evac(t):
                e.wait_ge(s_mm.h, mm0 + NCH * (t + 1))
                st(e.tensor_copy(out=r_sx[0:1, TT * t:TT * (t + 1)], in_=ps_sx[t % 2]))
                st(e.tensor_copy(out=r_sq[0:1, TT * t:TT * (t + 1)], in_=ps_sq[t % 2]))

            k = 0
            for t in range(NTT):
                for c in range(NCH):
                    e.wait_ge(s_ld[c].h, ld0[c] + 16)
                    if k >= 2:
                        e.wait_ge(s_mm.h, mm0 + k - 1)
                    s_sq.inc(e.tensor_tensor(out=sq[:, TT * (k % 2):TT * (k % 2 + 1)],
                                             in0=xb[c][:, TT * t:TT * (t + 1)],
                                             in1=xb[c][:, TT * t:TT * (t + 1)], op=ALU.mult))
                    e.wait_ge(s_sq.h, s_sq.n)
                    k += 1
                if t >= 1:
                    evac(t - 1)
            evac(NTT - 1)
            st(e.tensor_scalar_mul(out=r_mean[:], in0=r_sx[:], scalar1=1.0 / D))
            st(e.tensor_scalar_mul(out=r_var[:], in0=r_sq[:], scalar1=1.0 / D))
            st(e.tensor_tensor(out=r_sx[:], in0=r_mean[:], in1=r_mean[:], op=ALU.mult))
            st(e.tensor_tensor(out=r_var[:], in0=r_var[:], in1=r_sx[:], op=ALU.subtract))
            e.wait_ge(s_act.h, act0 + 2)
            st(e.scalar_tensor_tensor(out=r_b[:], in0=r_mean[:], scalar=-1.0, in1=r_a[:],
                                      op0=ALU.mult, op1=ALU.mult))
            for t in range(NTT):
                e.wait_ge(s_bc.h, bc0 + 2 * (t + 1))
                for c in range(NCH):
                    s_out.inc(e.tensor_tensor(out=tmp[:, 0:TT], in0=xb[c][:, TT * t:TT * (t + 1)],
                                              in1=ps_a[:, :], op=ALU.mult))
                    e.wait_ge(s_out.h, s_out.n)
                    if affine is None:
                        s_out.inc(e.tensor_tensor(out=xln[c][:, TT * t:TT * (t + 1)], in0=tmp[:, 0:TT],
                                                  in1=ps_b[:, :], op=ALU.add))
                        e.wait_ge(s_out.h, s_out.n)
                    else:
                        g_t, be_t = affine
                        s_out.inc(e.tensor_tensor(out=tmp[:, TT:2 * TT], in0=tmp[:, 0:TT],
                                                  in1=ps_b[:, :], op=ALU.add))
                        e.wait_ge(s_out.h, s_out.n)
                        s_out.inc(e.tensor_scalar(out=xln[c][:, TT * t:TT * (t + 1)], in0=tmp[:, TT:2 * TT],
                                                  scalar1=g_t[:, c:c + 1], scalar2=be_t[:, c:c + 1],
                                                  op0=ALU.mult, op1=ALU.add))
                        e.wait_ge(s_out.h, s_out.n)

        @blk.scalar
        def _(e):
            e.wait_ge(P.s_const.h, 96)
            e.wait_ge(s_ab.h, ab0 + 2 * NTT + 4)
            s_act.inc(e.activation(out=r_sq[:], in_=r_var[:], func=AF.Ln, bias=float(eps)))
            e.wait_ge(s_act.h, act0 + 1)
            s_act.inc(e.activation(out=r_a[:], in_=r_sq[:], func=AF.Exp, scale=-0.5))

        @blk.tensor
        def _(e):
            e.wait_ge(P.s_const.h, 96)
            k = 0
            for t in range(NTT):
                if t >= 2:
                    e.wait_ge(s_ab.h, ab0 + 2 * (t - 1))
                for c in range(NCH):
                    k += 1
                    e.wait_ge(s_sq.h, sq0 + k)
                    last = c == NCH - 1
                    e.matmul(ps_sx[t % 2], c_onesb[:], xb[c][:, TT * t:TT * (t + 1)],
                             start=(c == 0), stop=last)
                    s_mm.inc(e.matmul(ps_sq[t % 2], c_onesb[:],
                                      sq[:, TT * ((k - 1) % 2):TT * ((k - 1) % 2 + 1)],
                                      start=(c == 0), stop=last))
            for t in range(NTT):
                if t == 0:
                    e.wait_ge(s_ab.h, s_ab.n)
                else:
                    e.wait_ge(s_out.h, out0 + (3 if affine is not None else 2) * NCH * t)
                s_bc.inc(e.matmul(ps_a[:, :], c_onesf[:], r_a[0:1, TT * t:TT * (t + 1)],
                                  start=True, stop=True))
                s_bc.inc(e.matmul(ps_b[:, :], c_onesf[:], r_b[0:1, TT * t:TT * (t + 1)],
                                  start=True, stop=True))

    return s_out


def emit_moe2(P, tag, x_dram, y_dram, rwt_d, wg_d, wu_d, wd_d):
    nc = P.nc
    with contextlib.ExitStack() as sc_main:
        sbm = lambda n, s, d: sc_main.enter_context(nc.sbuf_tensor(f"{tag}_{n}", s, d))
        xln = [sbm(f"xln{c}", [128, TPC], BF16) for c in range(NCH)]
        acc = [sbm(f"acc{c}", [128, TPC], F32) for c in range(NCH)]
        ps8 = sc_main.enter_context(nc.psum_tensor(f"{tag}_ps8", [128, TT], F32))

        with contextlib.ExitStack() as sc_ln:
            sbl = lambda n, s, d: sc_ln.enter_context(nc.sbuf_tensor(f"{tag}_{n}", s, d))
            xb = [sbl(f"xb{c}", [128, TPC], BF16) for c in range(NCH)]
            sq = sbl("sq", [128, 2 * TT], BF16)
            rows = [sbl(f"r{i}", [1, TPC], F32) for i in range(6)]
            tmpf = sbl("tmpf", [128, 2 * TT], F32)
            s_ln = emit_ln(P, tag + "_ln", x_dram, xb, xln, sq, rows, tmpf)
            ln_done = s_ln.n

        with contextlib.ExitStack() as sc:
            sb = lambda n, s, d: sc.enter_context(nc.sbuf_tensor(f"{tag}_{n}", s, d))
            rwt_sb = sb("rwt", [128, NCH * NE], BF16)
            lg = sb("lg", [128, NE], F32)
            negm = sb("negm", [128, 1], F32)
            e_ = sb("e_", [128, NE], F32)
            ge1 = sb("ge1", [128, NE], F32)
            em = sb("em", [128, NE], F32)
            m2 = sb("m2", [128, 1], F32)
            den = sb("den", [128, 1], F32)
            rden = sb("rden", [128, 1], F32)
            gem = sb("gem", [128, NE], F32)
            cwf = sb("cwf", [128, NE], F32)
            cwt = sb("cwt", [128, NE], F32)
            cwT = sb("cwT", [8, TPC], BF16)
            cwrow = sb("cwrow", [1, TPC], BF16)

            s_rld = P.sem("rt_ld")
            s_rmm = P.sem("rt_mm")
            s_rv = P.sem("rt_v")
            s_ra = P.sem("rt_a")
            s_rt = P.sem("rt_t")
            s_rc = P.sem("rt_c")
            s_row = P.sem("rt_row")
            rld0, rmm0, rv0, ra0, rt0, rc0, row0 = P.sem0(
                "rt_ld", "rt_mm", "rt_v", "rt_a", "rt_t", "rt_c", "rt_row")

            NTI = TPC // 128
            ps_r = P.ps[6]
            ps_t = ps8
            VC = 10

            NGF = 2
            NG = NFC // NGF
            wgu_b = [sb(f"wgu{p}", [128, 2 * NCH * NGF * 128], BF16) for p in range(2)]
            wd_b = [sb(f"wdq{p}", [128, 8 * 1024], BF16) for p in range(2)]
            h_sb = sb("h", [128, 16 * TH], BF16)
            silu_b = sb("silu", [128, 2 * TT], BF16)
            tsl_b = sb("tsl", [128, 2 * TT], BF16)
            cwb = sb("cwb", [128, TPC], BF16)

            s_w = [P.sem("m2_wA"), P.sem("m2_wB")]
            s_wd = [P.sem("m2_wdA"), P.sem("m2_wdB")]
            s_cbm = P.sem("m2_cbm")
            s_cbe = P.sem("m2_cbe")
            s_g = P.sem("m2_g")
            s_u = P.sem("m2_u")
            s_sl = P.sem("m2_sl")
            s_h = P.sem("m2_h")
            s_o = P.sem("m2_o")
            s_oc = P.sem("m2_oc")
            w0 = [P.sems["m2_wA"].n, P.sems["m2_wB"].n]
            wdd0 = [P.sems["m2_wdA"].n, P.sems["m2_wdB"].n]
            cbm0, cbe0, g0, u0, sl0, h0, o0, oc0 = P.sem0(
                "m2_cbm", "m2_cbe", "m2_g", "m2_u", "m2_sl", "m2_h", "m2_o", "m2_oc")

            psG = [P.ps[0], P.ps[1]]
            psU = [P.ps[2], P.ps[3]]
            PSO = P.ps[:7] + [ps8]

            ETH = [(ee, th) for ee in range(NE) for th in range(NTH)]

            with nc.Block() as blk:
                @blk.gpsimd
                def _(e):
                    e.wait_ge(s_ln.h, ln_done)
                    s_rld.inc(e.dma_start(out=rwt_sb[:].rearrange("p (c ee) -> p c ee", c=NCH),
                                          in_=rwt_d.rearrange("(c p) ee -> p c ee", p=128)), 16)

                @blk.tensor
                def _(e):
                    e.wait_ge(s_rld.h, rld0 + 16)
                    e.wait_ge(P.s_const.h, 96)
                    def logits(i):
                        if i >= 1:
                            e.wait_ge(s_rv.h, rv0 + VC * (i - 1) + 1)
                        for c in range(NCH):
                            ii = e.matmul(ps_r[:, 0:NE], xln[c][:, 128 * i:128 * (i + 1)],
                                          rwt_sb[:, NE * c:NE * (c + 1)],
                                          start=(c == 0), stop=(c == NCH - 1))
                            if c == NCH - 1:
                                s_rmm.inc(ii)

                    def transp(i):
                        e.wait_ge(s_rv.h, rv0 + VC * (i + 1))
                        if i >= 1:
                            e.wait_ge(s_rc.h, rc0 + i)
                        s_rt.inc(e.transpose(ps_t[0:8, 0:128], cwt[:], P.identf[:]))

                    logits(0)
                    for i in range(1, NTI):
                        logits(i)
                        transp(i - 1)
                    transp(NTI - 1)

                @blk.vector
                def _(e):
                    st = chained(e, s_rv)
                    for i in range(NTI):
                        e.wait_ge(s_rmm.h, rmm0 + i + 1)
                        st(e.tensor_copy(out=lg[:], in_=ps_r[:, 0:NE]))
                        st(e.reduce_max(out=negm[:], in_=lg[:], axis=AX.X, negate=True))
                        e.wait_ge(s_ra.h, ra0 + i + 1)
                        st(e.tensor_single_scalar(out=ge1[:], in_=e_[:], scalar=1.0, op=ALU.is_ge))
                        st(e.scalar_tensor_tensor(out=em[:], in0=ge1[:], scalar=-2e30, in1=e_[:],
                                                  op0=ALU.mult, op1=ALU.add))
                        st(e.reduce_max(out=m2[:], in_=em[:], axis=AX.X))
                        st(e.tensor_scalar_add(out=den[:], in0=m2[:], scalar1=1.0))
                        st(e.reciprocal(out=rden[:], in_=den[:]))
                        st(e.tensor_single_scalar(out=gem[:], in_=e_[:], scalar=m2[:, 0:1], op=ALU.is_ge))
                        st(e.tensor_tensor(out=cwf[:], in0=e_[:], in1=gem[:], op=ALU.mult))
                        st(e.tensor_scalar_mul(out=cwt[:], in0=cwf[:], scalar1=rden[:, 0:1]))
                        e.wait_ge(s_rt.h, rt0 + i + 1)
                        s_rc.inc(e.tensor_copy(out=cwT[:, 128 * i:128 * (i + 1)], in_=ps_t[0:8, 0:128]))
                        e.wait_ge(s_rc.h, s_rc.n)

                @blk.scalar
                def _(e):
                    for i in range(NTI):
                        e.wait_ge(s_rv.h, rv0 + VC * i + 2)
                        s_ra.inc(e.activation(out=e_[:], in_=lg[:], func=AF.Exp, bias=negm[:, 0:1]))

            rc_tot = P.sems["rt_c"].n
            OVR = (0, 1, 8, 9)

            def occ_at(d):
                return sum(8 if dd in OVR else 16 for dd in range(d))

            with nc.Block() as blk:
                @blk.sync
                def _(e):
                    for ee in range(NE):
                        if ee == 0:
                            e.wait_ge(s_rc.h, rc_tot)
                        else:
                            e.wait_ge(s_cbm.h, cbm0 + NTT * ee)
                        s_row.inc(e.dma_start(out=cwrow[:], in_=cwT[ee:ee + 1, :]), 16)

                @blk.gpsimd
                def _(e):
                    for ee, th in ETH:
                        ei = ee * NTH + th
                        for hp in range(2):
                            for g8 in range(NG // 2):
                                g = hp * (NG // 2) + g8
                                gi = ei * NG + g
                                if gi >= 2:
                                    e.wait_ge(s_u.h, u0 + (gi - 1) * NGF * TTH)
                                fsl = slice(NGF * 128 * g, NGF * 128 * (g + 1))
                                W = NGF * 128
                                s_w[gi % 2].inc(e.dma_start(
                                    out=wgu_b[gi % 2][:, 0:NCH * W].rearrange("p (c f) -> p c f", c=NCH),
                                    in_=wg_d[ee].rearrange("(c p) f -> p c f", p=128)[:, :, fsl]), 16)
                                s_w[gi % 2].inc(e.dma_start(
                                    out=wgu_b[gi % 2][:, NCH * W:].rearrange("p (c f) -> p c f", c=NCH),
                                    in_=wu_d[ee].rearrange("(c p) f -> p c f", p=128)[:, :, fsl]), 16)
                            for q in range(2):
                                qi = (ei * 2 + hp) * 2 + q
                                if qi >= 2:
                                    e.wait_ge(s_o.h, o0 + 16 * (qi - 1))
                                s_wd[qi % 2].inc(e.dma_start(
                                    out=wd_b[qi % 2][:].rearrange("p (fc o) -> p fc o", fc=8),
                                    in_=wd_d[ee].rearrange("(fc p) o -> p fc o", p=128)[:, 16 * hp + 8 * q:16 * hp + 8 * (q + 1), :]), 16)

                @blk.tensor
                def _(e):
                    gstage = 0
                    dstage = 0
                    for ee, th in ETH:
                        ei = ee * NTH + th
                        for hp in range(2):
                            if ei * 2 + hp >= 1:
                                e.wait_ge(s_oc.h, oc0 + occ_at(dstage))
                            for g in range(NG // 2):
                                gi = ei * NG + hp * (NG // 2) + g
                                e.wait_ge(s_w[gi % 2].h, w0[gi % 2] + 32 * (gi // 2 + 1))
                                for f2 in range(NGF):
                                    fc = (hp * (NG // 2) + g) * NGF + f2
                                    W = NGF * 128
                                    for tt in range(TTH):
                                        spar = gstage % 2
                                        if gstage >= 2:
                                            e.wait_ge(s_sl.h, sl0 + gstage - 1)
                                            e.wait_ge(s_h.h, h0 + gstage - 1)
                                        col = TH * th + TT * tt
                                        for dk in range(NCH):
                                            ii = e.matmul(psG[spar][:, :],
                                                          wgu_b[gi % 2][:, W * dk + 128 * f2:W * dk + 128 * f2 + 128],
                                                          xln[dk][:, col:col + TT],
                                                          start=(dk == 0), stop=(dk == NCH - 1))
                                            if dk == NCH - 1:
                                                s_g.inc(ii)
                                        for dk in range(NCH):
                                            ii = e.matmul(psU[spar][:, :],
                                                          wgu_b[gi % 2][:, NCH * W + W * dk + 128 * f2:NCH * W + W * dk + 128 * f2 + 128],
                                                          xln[dk][:, col:col + TT],
                                                          start=(dk == 0), stop=(dk == NCH - 1))
                                            if dk == NCH - 1:
                                                s_u.inc(ii)
                                        gstage += 1
                            if th == 0 and hp == 0:
                                e.wait_ge(s_oc.h, oc0 + occ_at(dstage))
                                for t in range(NTT):
                                    ci = NTT * ee + t
                                    e.wait_ge(s_row.h, row0 + 16 * (ee + 1))
                                    if ci >= 1:
                                        e.wait_ge(s_cbe.h, cbe0 + ci)
                                    s_cbm.inc(e.matmul(ps_r[:, :], P.onesbr[:], cwrow[0:1, TT * t:TT * (t + 1)],
                                                       start=True, stop=True))
                            e.wait_ge(s_h.h, h0 + gstage)
                            e.wait_ge(s_sl.h, sl0 + gstage)
                            for q in range(2):
                                qi = (ei * 2 + hp) * 2 + q
                                e.wait_ge(s_wd[qi % 2].h, wdd0[qi % 2] + 16 * (qi // 2 + 1))
                                for tt in range(TTH):
                                    if dstage >= 1:
                                        e.wait_ge(s_oc.h, oc0 + occ_at(dstage))
                                    col = TT * tt
                                    for fcl in range(8):
                                        fh = q * 8 + fcl
                                        for do in range(NCH):
                                            ii = e.matmul(PSO[do][:, :],
                                                          wd_b[qi % 2][:, 1024 * fcl + 128 * do:1024 * fcl + 128 * do + 128],
                                                          h_sb[:, TH * fh + col:TH * fh + col + TT],
                                                          start=(fcl == 0), stop=(fcl == 7))
                                            if fcl == 7:
                                                s_o.inc(ii)
                                    dstage += 1

                @blk.scalar
                def _(e):
                    gstage = 0
                    for ee, th in ETH:
                        for fc in range(NFC):
                            for tt in range(TTH):
                                e.wait_ge(s_g.h, g0 + gstage + 1)
                                spar = gstage % 2
                                if gstage >= 2:
                                    e.wait_ge(s_h.h, h0 + gstage - 1)
                                s_sl.inc(e.activation(out=silu_b[:, TT * spar:TT * (spar + 1)],
                                                      in_=psG[spar][:, :], func=AF.Silu))
                                gstage += 1

                @blk.vector
                def _(e):
                    gstage = 0
                    dstage = 0
                    for ee, th in ETH:
                        for hp in range(2):
                            for fcpos in range(16):
                                fh = fcpos
                                for tt in range(TTH):
                                    spar = gstage % 2
                                    e.wait_ge(s_sl.h, sl0 + gstage + 1)
                                    e.wait_ge(s_u.h, u0 + gstage + 1)
                                    s_h.inc(e.tensor_tensor(out=h_sb[:, TH * fh + TT * tt:TH * fh + TT * tt + TT],
                                                            in0=silu_b[:, TT * spar:TT * (spar + 1)],
                                                            in1=psU[spar][:, :], op=ALU.mult))
                                    e.wait_ge(s_h.h, s_h.n)
                                    gstage += 1
                            if th == 0 and hp == 0:
                                for t in range(NTT):
                                    ci = NTT * ee + t
                                    e.wait_ge(s_cbm.h, cbm0 + ci + 1)
                                    s_cbe.inc(e.tensor_copy(out=cwb[:, TT * t:TT * (t + 1)], in_=ps_r[:, :]))
                                    e.wait_ge(s_cbe.h, s_cbe.n)
                            first = (ee == 0 and hp == 0)
                            for q in range(2):
                                for tt in range(TTH):
                                    col = TH * th + TT * tt
                                    for do in range(NCH):
                                        e.wait_ge(s_o.h, o0 + 8 * dstage + do + 1)
                                        if first and q == 0:
                                            s_oc.inc(e.tensor_tensor(out=acc[do][:, col:col + TT],
                                                                     in0=PSO[do][:, :],
                                                                     in1=cwb[:, col:col + TT], op=ALU.mult))
                                            e.wait_ge(s_oc.h, s_oc.n)
                                        else:
                                            s_oc.inc(e.tensor_tensor(out=tsl_b[:, 0:TT],
                                                                     in0=PSO[do][:, :],
                                                                     in1=cwb[:, col:col + TT], op=ALU.mult))
                                            e.wait_ge(s_oc.h, s_oc.n)
                                            s_oc.inc(e.tensor_tensor(out=acc[do][:, col:col + TT],
                                                                     in0=acc[do][:, col:col + TT],
                                                                     in1=tsl_b[:, 0:TT], op=ALU.add))
                                            e.wait_ge(s_oc.h, s_oc.n)
                                    dstage += 1

            oc_total = P.sems["m2_oc"].n

        with contextlib.ExitStack() as sc_f:
            sbf = lambda n, s, d: sc_f.enter_context(nc.sbuf_tensor(f"{tag}_{n}", s, d))
            xf2 = [sbf(f"xf2{p}", [128, TPC], F32) for p in range(2)]
            yf = [sbf(f"yf{p}", [128, TPC], F32) for p in range(2)]
            s_fl = [P.sem("fin_lA"), P.sem("fin_lB")]
            s_fy = P.sem("fin_y")
            s_fs = [P.sem("fin_sA"), P.sem("fin_sB")]
            fl0 = [P.sems["fin_lA"].n, P.sems["fin_lB"].n]
            fs0 = [P.sems["fin_sA"].n, P.sems["fin_sB"].n]
            fy0 = P.sems["fin_y"].n
            s_oc_f = P.sem("m2_oc")

            with nc.Block() as blk:
                @blk.sync
                def _(e):
                    e.wait_ge(s_oc_f.h, oc_total)
                    for c in range(NCH):
                        if c >= 2:
                            e.wait_ge(s_fy.h, fy0 + c - 1)
                        s_fl[c % 2].inc(e.dma_start(out=xf2[c % 2][:], in_=x_dram[128 * c:128 * (c + 1), :]), 16)

                @blk.vector
                def _(e):
                    for c in range(NCH):
                        e.wait_ge(s_fl[c % 2].h, fl0[c % 2] + 16 * (c // 2 + 1))
                        if c >= 2:
                            e.wait_ge(s_fs[c % 2].h, fs0[c % 2] + 16 * (c // 2))
                        s_fy.inc(e.scalar_tensor_tensor(out=yf[c % 2][:], in0=acc[c][:], scalar=0.5,
                                                        in1=xf2[c % 2][:], op0=ALU.mult, op1=ALU.add))
                        e.wait_ge(s_fy.h, s_fy.n)

                @blk.gpsimd
                def _(e):
                    for c in range(NCH):
                        e.wait_ge(s_fy.h, fy0 + c + 1)
                        s_fs[c % 2].inc(e.dma_start(out=y_dram[128 * c:128 * (c + 1), :], in_=yf[c % 2][:]), 16)
                    e.wait_ge(s_fs[0].h, s_fs[0].n)
                    e.wait_ge(s_fs[1].h, s_fs[1].n)

    return P.sem("fin_sA")


_CACHE = {}


def build_moe2_prog():
    P = Prog()
    nc = P.nc
    x = nc.dram_tensor("x", [D, TPC], F32, kind="ExternalInput").ap()
    rwt = nc.dram_tensor("rwt", [D, NE], F32, kind="ExternalInput").ap()
    wg = nc.dram_tensor("wg", [NE, D, FF], F32, kind="ExternalInput").ap()
    wu = nc.dram_tensor("wu", [NE, D, FF], F32, kind="ExternalInput").ap()
    wd = nc.dram_tensor("wd", [NE, FF, D], F32, kind="ExternalInput").ap()
    y = nc.dram_tensor("y", [D, TPC], F32, kind="ExternalOutput").ap()
    emit_moe2(P, "m2", x, y, rwt, wg, wu, wd)
    return P


def run_moe2(x_fm_shards, router_w, w_gate, w_up, w_down):
    if "moe2" not in _CACHE:
        _CACHE["moe2"] = build_moe2_prog()
    P = _CACHE["moe2"]
    rwt = np.ascontiguousarray(router_w.T.astype(np.float32))
    in_maps = [{"x": np.ascontiguousarray(xs), "rwt": rwt,
                "wg": w_gate, "wu": w_up, "wd": w_down} for xs in x_fm_shards]
    res = run_bass_kernel_spmd(P.nc, in_maps, core_ids=list(range(NCORES)))
    return [r["y"] for r in res.results]




def emit_moe1(P, tag, x_dram, y_dram, gwt_d, gbr_d, w1_d, b1_d, w2_d, b2_d,
              premask_d, t16_h, gt_h):
    nc = P.nc
    with contextlib.ExitStack() as sc_main:
        sbm = lambda n, s, d: sc_main.enter_context(nc.sbuf_tensor(f"{tag}_{n}", s, d))
        xln = [sbm(f"xln{c}", [128, TPC], BF16) for c in range(NCH)]
        acc = [sbm(f"acc{c}", [128, TPC], F32) for c in range(NCH)]
        ps8 = sc_main.enter_context(nc.psum_tensor(f"{tag}_ps8", [128, TT], F32))

        with contextlib.ExitStack() as sc_ln:
            sbl = lambda n, s, d: sc_ln.enter_context(nc.sbuf_tensor(f"{tag}_{n}", s, d))
            xb = [sbl(f"xb{c}", [128, TPC], BF16) for c in range(NCH)]
            sq = sbl("sq", [128, 2 * TT], BF16)
            rows = [sbl(f"r{i}", [1, TPC], F32) for i in range(6)]
            tmpf = sbl("tmpf", [128, 2 * TT], F32)
            s_ln = emit_ln(P, tag + "_ln", x_dram, xb, xln, sq, rows, tmpf)
            ln_done = s_ln.n

        with contextlib.ExitStack() as sc:
            sb = lambda n, s, d: sc.enter_context(nc.sbuf_tensor(f"{tag}_{n}", s, d))
            rwt_sb = sb("rwt", [128, NCH * NE], BF16)
            gb_sb = sb("gb", [1, NE], BF16)
            gbbc = sb("gbbc", [128, NE], F32)
            premask = sb("premask", [128, 16], BF16)
            triu_sb = sb("triu", [128, 128], BF16)
            lg = sb("lg", [128, NE], F32)
            negm = sb("negm", [128, 1], F32)
            e_ = sb("e_", [128, NE], F32)
            esum = sb("esum", [128, 1], F32)
            tp_all = sb("tp_all", [128, 16], F32)
            oh_all = sb("oh_all", [128, 16 * NE], F32)
            ohb = sb("ohb", [128, NE], BF16)
            inc_all = sb("incall", [128, 16 * NE], F32)
            tot16 = sb("tot16", [16, NE], F32)
            gtb = sb("gtb", [128, NE], BF16)
            offs16 = sb("offs16", [16, NE], F32)
            offrow = sb("offrow", [1, NE], F32)
            offbc = sb("offbc", [128, NE], F32)
            ra_ = sb("ra_", [128, NE], F32)
            krank = sb("krank", [128, 1], F32)
            keep = sb("keep", [128, 1], F32)
            tpk = sb("tpk", [128, 1], F32)
            cwt = sb("cwt", [128, NE], F32)
            cwT = sb("cwT", [8, TPC], BF16)
            cwrow = sb("cwrow", [1, TPC], BF16)

            s_rld = P.sem("rt_ld")
            s_rmm = P.sem("rt_mm")
            s_rv = P.sem("rt_v")
            s_ra = P.sem("rt_a")
            s_rt = P.sem("rt_t")
            s_rc = P.sem("rt_c")
            s_row = P.sem("rt_row")
            s_cs = P.sem("r1_cs")
            s_ce = P.sem("r1_ce")
            s_tt = P.sem("r1_tt")
            s_ag = P.sem("r1_ag")
            s_of = P.sem("r1_of")
            s_obd = P.sem("r1_obd")
            s_obm = P.sem("r1_obm")
            rld0, rmm0, rv0, ra0, rt0, rc0, row0, cs0, ce0, tt0, ag0, of0, obd0, obm0 = P.sem0(
                "rt_ld", "rt_mm", "rt_v", "rt_a", "rt_t", "rt_c", "rt_row",
                "r1_cs", "r1_ce", "r1_tt", "r1_ag", "r1_of", "r1_obd", "r1_obm")

            NTI = TPC // 128
            ps_r = P.ps[6]
            psc = [P.ps[4], P.ps[5]]
            ps_t = ps8
            VC1 = 5

            with nc.Block() as blk:
                @blk.sync
                def _(e):
                    e.wait_ge(s_ln.h, ln_done)
                    s_rld.inc(e.dma_start(out=premask[:], in_=premask_d[:, :]), 16)
                    s_rld.inc(e.dma_start(
                        out=triu_sb[:],
                        in_=P.const("c_triu_b", np.triu(np.ones((128, 128), np.float32)).astype(BF))[:, :]), 16)
                    for i in range(NTI):
                        e.wait_ge(s_ce.h, ce0 + i + 1)
                        s_tt.inc(e.dma_start(out=tot16[i:i + 1, :], in_=inc_all[127:128, NE * i:NE * (i + 1)]), 16)

                @blk.gpsimd
                def _(e):
                    e.wait_ge(s_ln.h, ln_done)
                    s_rld.inc(e.dma_start(out=rwt_sb[:].rearrange("p (c ee) -> p c ee", c=NCH),
                                          in_=gwt_d.rearrange("(c p) ee -> p c ee", p=128)), 16)
                    s_rld.inc(e.dma_start(out=gb_sb[:], in_=gbr_d[:, :]), 16)
                    e.wait_ge(s_tt.h, tt0 + 16 * NTI)
                    s_ag.inc(e.dma_start(out=t16_h.ap()[:, :], in_=tot16[:]), 16)
                    e.wait_ge(s_ag.h, ag0 + 16)
                    s_ag.inc(e.collective_compute(
                        "AllGather", ALU.bypass, replica_groups=RG,
                        ins=[t16_h.ap().opt()], outs=[gt_h.ap().opt()]), 1)
                    e.wait_ge(s_ag.h, ag0 + 17)
                    s_ag.inc(e.dma_start(out=gtb[:], in_=gt_h.ap()[:, :]), 16)

                @blk.tensor
                def _(e):
                    e.wait_ge(s_rld.h, rld0 + 64)
                    e.wait_ge(P.s_const.h, 96)
                    s_rmm.inc(e.matmul(ps_r[:, 0:NE], P.onesbr[:], gb_sb[:], start=True, stop=True))

                    def logits(i):
                        if i >= 1:
                            e.wait_ge(s_rv.h, rv0 + 5 * i - 3)
                        else:
                            e.wait_ge(s_rv.h, rv0 + 1)
                        for c in range(NCH):
                            ii = e.matmul(ps_r[:, 0:NE], xln[c][:, 128 * i:128 * (i + 1)],
                                          rwt_sb[:, NE * c:NE * (c + 1)],
                                          start=(c == 0), stop=(c == NCH - 1))
                            if c == NCH - 1:
                                s_rmm.inc(ii)

                    def cumsum(i):
                        e.wait_ge(s_rv.h, rv0 + 5 * i + 5)
                        if i >= 2:
                            e.wait_ge(s_ce.h, ce0 + i - 1)
                        s_cs.inc(e.matmul(psc[i % 2][:, 0:NE], triu_sb[:], ohb[:],
                                          start=True, stop=True))

                    logits(0)
                    for i in range(1, NTI):
                        logits(i)
                        cumsum(i - 1)
                    cumsum(NTI - 1)

                @blk.vector
                def _(e):
                    st = chained(e, s_rv)
                    e.wait_ge(s_rmm.h, rmm0 + 1)
                    st(e.tensor_copy(out=gbbc[:], in_=ps_r[:, 0:NE]))
                    for i in range(NTI):
                        e.wait_ge(s_rmm.h, rmm0 + i + 2)
                        st(e.tensor_tensor(out=lg[:], in0=ps_r[:, 0:NE], in1=gbbc[:], op=ALU.add))
                        st(e.reduce_max(out=negm[:], in_=lg[:], axis=AX.X, negate=True))
                        e.wait_ge(s_ra.h, ra0 + i + 1)
                        st(e.tensor_single_scalar(out=oh_all[:, NE * i:NE * (i + 1)], in_=e_[:],
                                                  scalar=1.0, op=ALU.is_ge))
                        st(e.tensor_copy(out=ohb[:], in_=oh_all[:, NE * i:NE * (i + 1)]))
                        st(e.reciprocal(out=tp_all[:, i:i + 1], in_=esum[:]))
                        e.wait_ge(s_cs.h, cs0 + i + 1)
                        s_ce.inc(e.tensor_copy(out=inc_all[:, NE * i:NE * (i + 1)],
                                               in_=psc[i % 2][:, 0:NE]))
                        e.wait_ge(s_ce.h, s_ce.n)

                @blk.scalar
                def _(e):
                    for i in range(NTI):
                        e.wait_ge(s_rv.h, rv0 + 5 * i + 3)
                        s_ra.inc(e.activation(out=e_[:], in_=lg[:], func=AF.Exp, bias=negm[:, 0:1],
                                              accum_out=esum[:, 0:1]))

            rvB0 = P.sems["rt_v"].n
            VC2 = 6
            with nc.Block() as blk:
                @blk.sync
                def _(e):
                    for i in range(NTI):
                        e.wait_ge(s_of.h, of0 + 2)
                        if i >= 1:
                            e.wait_ge(s_obm.h, obm0 + i)
                        s_obd.inc(e.dma_start(out=offrow[:], in_=offs16[i:i + 1, :]), 16)

                @blk.tensor
                def _(e):
                    e.wait_ge(s_ag.h, ag0 + 33)
                    s_of.inc(e.matmul(ps_r[0:16, 0:NE], premask[:], gtb[:], start=True, stop=True))
                    for i in range(NTI):
                        e.wait_ge(s_obd.h, obd0 + 16 * (i + 1))
                        if i >= 2:
                            e.wait_ge(s_rv.h, rvB0 + 6 * i - 10)
                        s_obm.inc(e.matmul(psc[i % 2][:, 0:NE], P.onesf[:], offrow[:],
                                           start=True, stop=True))
                        e.wait_ge(s_rv.h, rvB0 + 1 + VC2 * i + 6)
                        if i >= 1:
                            e.wait_ge(s_rc.h, rc0 + i)
                        s_rt.inc(e.transpose(ps_t[0:8, 0:128], cwt[:], P.identf[:]))

                @blk.vector
                def _(e):
                    st = chained(e, s_rv)
                    e.wait_ge(s_of.h, of0 + 1)
                    st(e.tensor_copy(out=offs16[:], in_=ps_r[0:16, 0:NE]))
                    s_of.inc(e.engine_nop())
                    for i in range(NTI):
                        e.wait_ge(s_obm.h, obm0 + i + 1)
                        st(e.tensor_tensor(out=offbc[:], in0=inc_all[:, NE * i:NE * (i + 1)],
                                           in1=psc[i % 2][:, 0:NE], op=ALU.add))
                        st(e.tensor_tensor(out=ra_[:], in0=oh_all[:, NE * i:NE * (i + 1)],
                                           in1=offbc[:], op=ALU.mult))
                        st(e.reduce_sum(out=krank[:], in_=ra_[:], axis=AX.X))
                        st(e.tensor_single_scalar(out=keep[:], in_=krank[:], scalar=CAP + 0.5, op=ALU.is_lt))
                        st(e.tensor_tensor(out=tpk[:], in0=tp_all[:, i:i + 1], in1=keep[:], op=ALU.mult))
                        st(e.tensor_scalar_mul(out=cwt[:], in0=oh_all[:, NE * i:NE * (i + 1)],
                                               scalar1=tpk[:, 0:1]))
                        e.wait_ge(s_rt.h, rt0 + i + 1)
                        s_rc.inc(e.tensor_copy(out=cwT[:, 128 * i:128 * (i + 1)], in_=ps_t[0:8, 0:128]))
                        e.wait_ge(s_rc.h, s_rc.n)

            NGF1 = 2
            NG1 = NFC // NGF1
            w1_b = [sb(f"w1b{p}", [128, NCH * NGF1 * 128], BF16) for p in range(2)]
            wd_b = [sb(f"wdq{p}", [128, 8 * 1024], BF16) for p in range(2)]
            h_sb = sb("h", [128, 16 * TH], BF16)
            rl_b = sb("rl", [128, 2 * TT], BF16)
            cwb = sb("cwb", [128, TPC], BF16)
            b1_sb = [sb(f"b1{p}", [128, NFC], F32) for p in range(2)]
            b2_sb = [sb(f"b2{p}", [1, D], BF16) for p in range(2)]

            s_w = [P.sem("m2_wA"), P.sem("m2_wB")]
            s_wd = [P.sem("m2_wdA"), P.sem("m2_wdB")]
            s_b = [P.sem("t_wl"), P.sem("t_ms")]
            s_cbm = P.sem("m2_cbm")
            s_cbe = P.sem("m2_cbe")
            s_g = P.sem("m2_g")
            s_sl = P.sem("m2_sl")
            s_h = P.sem("m2_h")
            s_o = P.sem("m2_o")
            s_oc = P.sem("m2_oc")
            w0 = [P.sems["m2_wA"].n, P.sems["m2_wB"].n]
            wdd0 = [P.sems["m2_wdA"].n, P.sems["m2_wdB"].n]
            b0 = [P.sems["t_wl"].n, P.sems["t_ms"].n]
            cbm0, cbe0, g0, sl0, h0, o0, oc0 = P.sem0(
                "m2_cbm", "m2_cbe", "m2_g", "m2_sl", "m2_h", "m2_o", "m2_oc")

            psG = [P.ps[0], P.ps[1]]
            PSO = P.ps[:7] + [ps8]
            ETH1 = [(ee, th) for ee in range(NE) for th in range(NTH)]

            with nc.Block() as blk:
                @blk.sync
                def _(e):
                    for ee in range(NE):
                        if ee > 0:
                            e.wait_ge(s_cbm.h, cbm0 + NTT * ee)
                        s_row.inc(e.dma_start(out=cwrow[:], in_=cwT[ee:ee + 1, :]), 16)

                @blk.gpsimd
                def _(e):
                    for ee, th in ETH1:
                        ei = ee * NTH + th
                        if th == 0:
                            if ee >= 2:
                                e.wait_ge(s_sl.h, sl0 + 128 * (ee - 1))
                                e.wait_ge(s_o.h, o0 + 64 * (ee - 1))
                            s_b[ee % 2].inc(e.dma_start(
                                out=b1_sb[ee % 2][:], in_=b1_d[ee]), 16)
                            s_b[ee % 2].inc(e.dma_start(
                                out=b2_sb[ee % 2][0:1, :], in_=b2_d[ee:ee + 1, :]), 16)
                        for hp in range(2):
                            for g8 in range(NG1 // 2):
                                g = hp * (NG1 // 2) + g8
                                gi = ei * NG1 + g
                                if gi >= 2:
                                    e.wait_ge(s_g.h, g0 + (gi - 1) * NGF1 * TTH)
                                fsl = slice(NGF1 * 128 * g, NGF1 * 128 * (g + 1))
                                s_w[gi % 2].inc(e.dma_start(
                                    out=w1_b[gi % 2][:].rearrange("p (c f) -> p c f", c=NCH),
                                    in_=w1_d[ee].rearrange("(c p) f -> p c f", p=128)[:, :, fsl]), 16)
                            for q in range(2):
                                qi = (ei * 2 + hp) * 2 + q
                                if qi >= 2:
                                    e.wait_ge(s_o.h, o0 + 16 * (qi - 1))
                                s_wd[qi % 2].inc(e.dma_start(
                                    out=wd_b[qi % 2][:].rearrange("p (fc o) -> p fc o", fc=8),
                                    in_=w2_d[ee].rearrange("(fc p) o -> p fc o", p=128)[:, 16 * hp + 8 * q:16 * hp + 8 * (q + 1), :]), 16)

                @blk.tensor
                def _(e):
                    gstage = 0
                    dstage = 0
                    for ee, th in ETH1:
                        ei = ee * NTH + th
                        if th == 0:
                            e.wait_ge(s_oc.h, oc0 + 8 * dstage)
                            for t in range(NTT):
                                ci = NTT * ee + t
                                e.wait_ge(s_row.h, row0 + 16 * (ee + 1))
                                if ci >= 1:
                                    e.wait_ge(s_cbe.h, cbe0 + ci)
                                s_cbm.inc(e.matmul(ps_r[:, :], P.onesbr[:], cwrow[0:1, TT * t:TT * (t + 1)],
                                                   start=True, stop=True))
                        for hp in range(2):
                            if ei * 2 + hp >= 1:
                                e.wait_ge(s_oc.h, oc0 + 8 * dstage)
                            for g in range(NG1 // 2):
                                gi = ei * NG1 + hp * (NG1 // 2) + g
                                e.wait_ge(s_w[gi % 2].h, w0[gi % 2] + 16 * (gi // 2 + 1))
                                if hp == 0 and g == 0:
                                    e.wait_ge(s_cbe.h, cbe0 + NTT * (ee + 1))
                                    e.wait_ge(s_b[ee % 2].h, b0[ee % 2] + 32 * (ee // 2 + 1))
                                for f2 in range(NGF1):
                                    fc = (hp * (NG1 // 2) + g) * NGF1 + f2
                                    W1W = NGF1 * 128
                                    for tt in range(TTH):
                                        spar = gstage % 2
                                        if gstage >= 2:
                                            e.wait_ge(s_sl.h, sl0 + gstage - 1)
                                        col = TH * th + TT * tt
                                        for dk in range(NCH):
                                            ii = e.matmul(psG[spar][:, :],
                                                          w1_b[gi % 2][:, W1W * dk + 128 * f2:W1W * dk + 128 * f2 + 128],
                                                          xln[dk][:, col:col + TT],
                                                          start=(dk == 0), stop=(dk == NCH - 1))
                                            if dk == NCH - 1:
                                                s_g.inc(ii)
                                        gstage += 1
                            e.wait_ge(s_h.h, h0 + gstage)
                            e.wait_ge(s_sl.h, sl0 + gstage)
                            for q in range(2):
                                qi = (ei * 2 + hp) * 2 + q
                                e.wait_ge(s_wd[qi % 2].h, wdd0[qi % 2] + 16 * (qi // 2 + 1))
                                lastq = hp == 1 and q == 1
                                for tt in range(TTH):
                                    if dstage >= 1:
                                        e.wait_ge(s_oc.h, oc0 + 8 * dstage)
                                    col = TT * tt
                                    colg = TH * th + TT * tt
                                    for fcl in range(8):
                                        fh = q * 8 + fcl
                                        for do in range(NCH):
                                            ii = e.matmul(PSO[do][:, :],
                                                          wd_b[qi % 2][:, 1024 * fcl + 128 * do:1024 * fcl + 128 * do + 128],
                                                          h_sb[:, TH * fh + col:TH * fh + col + TT],
                                                          start=(fcl == 0), stop=(fcl == 7 and not lastq))
                                            if fcl == 7 and not lastq:
                                                s_o.inc(ii)
                                    if lastq:
                                        for do in range(NCH):
                                            s_o.inc(e.matmul(PSO[do][:, :],
                                                             b2_sb[ee % 2][0:1, 128 * do:128 * (do + 1)],
                                                             cwb[0:1, colg:colg + TT],
                                                             start=False, stop=True))
                                    dstage += 1

                @blk.scalar
                def _(e):
                    gstage = 0
                    for ee, th in ETH1:
                        for fc in range(NFC):
                            for tt in range(TTH):
                                e.wait_ge(s_g.h, g0 + gstage + 1)
                                spar = gstage % 2
                                if gstage >= 2:
                                    e.wait_ge(s_h.h, h0 + gstage - 1)
                                s_sl.inc(e.activation(out=rl_b[:, TT * spar:TT * (spar + 1)],
                                                      in_=psG[spar][:, :], func=AF.Relu,
                                                      bias=b1_sb[ee % 2][:, fc:fc + 1]))
                                gstage += 1

                @blk.vector
                def _(e):
                    gstage = 0
                    dstage = 0
                    for ee, th in ETH1:
                        if th == 0:
                            for t in range(NTT):
                                ci = NTT * ee + t
                                e.wait_ge(s_cbm.h, cbm0 + ci + 1)
                                s_cbe.inc(e.tensor_copy(out=cwb[:, TT * t:TT * (t + 1)], in_=ps_r[:, :]))
                                e.wait_ge(s_cbe.h, s_cbe.n)
                        for hp in range(2):
                            for fcpos in range(16):
                                for tt in range(TTH):
                                    spar = gstage % 2
                                    e.wait_ge(s_sl.h, sl0 + gstage + 1)
                                    col = TH * th + TT * tt
                                    s_h.inc(e.tensor_tensor(out=h_sb[:, TH * fcpos + TT * tt:TH * fcpos + TT * tt + TT],
                                                            in0=rl_b[:, TT * spar:TT * (spar + 1)],
                                                            in1=cwb[:, col:col + TT], op=ALU.mult))
                                    e.wait_ge(s_h.h, s_h.n)
                                    gstage += 1
                            first = (ee == 0 and hp == 0)
                            for q in range(2):
                                for tt in range(TTH):
                                    col = TH * th + TT * tt
                                    for do in range(NCH):
                                        e.wait_ge(s_o.h, o0 + 8 * dstage + do + 1)
                                        if first and q == 0:
                                            s_oc.inc(e.tensor_copy(out=acc[do][:, col:col + TT], in_=PSO[do][:, :]))
                                        else:
                                            s_oc.inc(e.tensor_tensor(out=acc[do][:, col:col + TT],
                                                                     in0=acc[do][:, col:col + TT],
                                                                     in1=PSO[do][:, :], op=ALU.add))
                                        e.wait_ge(s_oc.h, s_oc.n)
                                    dstage += 1

            oc_total = P.sems["m2_oc"].n

        with contextlib.ExitStack() as sc_f:
            sbf = lambda n, s, d: sc_f.enter_context(nc.sbuf_tensor(f"{tag}_{n}", s, d))
            xf2 = [sbf(f"xf2{p}", [128, TPC], F32) for p in range(2)]
            yf = [sbf(f"yf{p}", [128, TPC], F32) for p in range(2)]
            s_fl = [P.sem("fin_lA"), P.sem("fin_lB")]
            s_fy = P.sem("fin_y")
            s_fs = [P.sem("fin_sA"), P.sem("fin_sB")]
            fl0 = [P.sems["fin_lA"].n, P.sems["fin_lB"].n]
            fs0 = [P.sems["fin_sA"].n, P.sems["fin_sB"].n]
            fy0 = P.sems["fin_y"].n
            s_oc_f = P.sem("m2_oc")

            with nc.Block() as blk:
                @blk.sync
                def _(e):
                    e.wait_ge(s_oc_f.h, oc_total)
                    for c in range(NCH):
                        if c >= 2:
                            e.wait_ge(s_fy.h, fy0 + c - 1)
                        s_fl[c % 2].inc(e.dma_start(out=xf2[c % 2][:], in_=x_dram[128 * c:128 * (c + 1), :]), 16)

                @blk.vector
                def _(e):
                    for c in range(NCH):
                        e.wait_ge(s_fl[c % 2].h, fl0[c % 2] + 16 * (c // 2 + 1))
                        if c >= 2:
                            e.wait_ge(s_fs[c % 2].h, fs0[c % 2] + 16 * (c // 2))
                        s_fy.inc(e.scalar_tensor_tensor(out=yf[c % 2][:], in0=acc[c][:], scalar=0.5,
                                                        in1=xf2[c % 2][:], op0=ALU.mult, op1=ALU.add))
                        e.wait_ge(s_fy.h, s_fy.n)

                @blk.gpsimd
                def _(e):
                    for c in range(NCH):
                        e.wait_ge(s_fy.h, fy0 + c + 1)
                        s_fs[c % 2].inc(e.dma_start(out=y_dram[128 * c:128 * (c + 1), :], in_=yf[c % 2][:]), 16)
                    e.wait_ge(s_fs[0].h, s_fs[0].n)
                    e.wait_ge(s_fs[1].h, s_fs[1].n)

    return P.sem("fin_sA")


def build_moe1_prog(uid="a"):
    P = Prog()
    nc = P.nc
    x = nc.dram_tensor("x", [D, TPC], F32, kind="ExternalInput").ap()
    gwt = nc.dram_tensor("gwt", [D, NE], F32, kind="ExternalInput").ap()
    gbr = nc.dram_tensor("gbr", [1, NE], F32, kind="ExternalInput").ap()
    w1 = nc.dram_tensor("w1", [NE, D, FF], F32, kind="ExternalInput").ap()
    b1 = nc.dram_tensor("b1", [NE, 128, NFC], F32, kind="ExternalInput").ap()
    w2 = nc.dram_tensor("w2", [NE, FF, D], F32, kind="ExternalInput").ap()
    b2 = nc.dram_tensor("b2", [NE, D], F32, kind="ExternalInput").ap()
    pm = nc.dram_tensor("premask", [128, 16], BF16, kind="ExternalInput").ap()
    y = nc.dram_tensor("y", [D, TPC], F32, kind="ExternalOutput").ap()
    t16 = nc.dram_tensor(f"t16_{uid}", [16, NE], F32, kind="Internal")
    gt = nc.dram_tensor(f"gt_{uid}", [128, NE], F32, kind="Internal")
    emit_moe1(P, "m1", x, y, gwt, gbr, w1, b1, w2, b2, pm, t16, gt)
    return P


def premask_for_core(core):
    m = np.zeros((128, 16), np.float32)
    for j in range(16):
        m[:16 * core + j, j] = 1.0
    return m.astype(BF)


def run_moe1(x_fm_shards, gate_w, gate_b, w1, b1, w2, b2):
    if "moe1" not in _CACHE:
        _CACHE["moe1"] = build_moe1_prog()
    P = _CACHE["moe1"]
    gwt = np.ascontiguousarray(gate_w.T.astype(np.float32))
    gbr = np.ascontiguousarray(gate_b.astype(np.float32).reshape(1, NE))
    b1t = np.ascontiguousarray(b1.reshape(NE, NFC, 128).transpose(0, 2, 1))
    in_maps = [{"x": np.ascontiguousarray(xs), "gwt": gwt, "gbr": gbr,
                "w1": w1, "b1": b1t, "w2": w2, "b2": b2,
                "premask": premask_for_core(c)} for c, xs in enumerate(x_fm_shards)]
    res = run_bass_kernel_spmd(P.nc, in_maps, core_ids=list(range(NCORES)))
    return [r["y"] for r in res.results]




def emit_l0attn(P, tag, x_dram, y_dram, W):
    nc = P.nc
    NTI = TPC // 128
    KCH = 37

    with contextlib.ExitStack() as sc_main:
        sbm = lambda n, s, d: sc_main.enter_context(nc.sbuf_tensor(f"{tag}_{n}", s, d))
        xln = [sbm(f"xln{c}", [128, TPC], BF16) for c in range(NCH)]
        bot = sbm("bot", [128, NCH], F32)
        g1t = sbm("g1t", [128, NCH], F32)
        be1t = sbm("be1t", [128, NCH], F32)
        g2t = sbm("g2t", [128, NCH], F32)
        be2t = sbm("be2t", [128, NCH], F32)
        b1t = sbm("b1t", [128, NFC], F32)
        b2t = sbm("b2t", [128, NCH], F32)

        with contextlib.ExitStack() as sc_ln:
            sbl = lambda n, s, d: sc_ln.enter_context(nc.sbuf_tensor(f"{tag}_{n}", s, d))
            xb = [sbl(f"xb{c}", [128, TPC], BF16) for c in range(NCH)]
            sq = sbl("sq", [128, 2 * TT], BF16)
            rows = [sbl(f"r{i}", [1, TPC], F32) for i in range(6)]
            tmpf = sbl("tmpf", [128, 2 * TT], F32)
            s_ln = emit_ln(P, tag + "_ln", x_dram, xb, xln, sq, rows, tmpf)
            ln_done = s_ln.n

        s_awg = P.sem("a_wg")
        s_aws = P.sem("a_ws")
        s_pk = P.sem("a_pk")
        s_pe = P.sem("a_pe")
        s_av = P.sem("a_av")
        s_aa = P.sem("a_aa")
        s_qt = P.sem("a_qt")
        s_qte = P.sem("a_qte")
        s_bb = P.sem("a_bb")
        s_bbe = P.sem("a_bbe")
        s_kv = P.sem("a_kv")
        s_kve = P.sem("a_kve")
        s_ar = P.sem("a_ar")
        s_cx = P.sem("a_cx")
        s_cxe = P.sem("a_cxe")
        s_wo = P.sem("a_wo")
        s_woa = P.sem("a_woa")
        s_wod = P.sem("a_wod")
        s_wos = [P.sem("a_wosA"), P.sem("a_wosB")]
        (awg0, aws0, pk0, pe0, av0, aa0, qt0, qte0, bb0, bbe0, kv0, kve0,
         ar0, cx0, cxe0, wo0, woa0, wod0) = P.sem0(
            "a_wg", "a_ws", "a_pk", "a_pe", "a_av", "a_aa", "a_qt", "a_qte",
            "a_bb", "a_bbe", "a_kv", "a_kve", "a_ar", "a_cx", "a_cxe",
            "a_wo", "a_woa", "a_wod")
        wos0 = [P.sems["a_wosA"].n, P.sems["a_wosB"].n]

        with contextlib.ExitStack() as s1:
            sb1 = lambda n, s, d: s1.enter_context(nc.sbuf_tensor(f"{tag}_{n}", s, d))
            q_fm = sb1("qfm", [128, 16384], BF16)
            kvraw = sb1("kvraw", [64, 2048], F32)
            kvc = sb1("kvc", [64, 2048], F32)
            kvt = sb1("kvt", [64, 1024], F32)
            kvm = sb1("kvm", [128, 1024], BF16)
            bsel = sb1("bsel", [64, 2], F32)

            with contextlib.ExitStack() as s2a:
                sb2 = lambda n, s, d: s2a.enter_context(nc.sbuf_tensor(f"{tag}_{n}", s, d))
                wbig = sb2("wA", [128, 8192], BF16)
                wbig2 = sb2("wB", [128, 8192], BF16)
                bqr = sb2("bqr", [1, D], BF16)
                bkr = sb2("bkr", [1, D], BF16)
                bvr = sb2("bvr", [1, D], BF16)
                bbc = [sb2(f"bbc{j}", [128, D], BF16) for j in range(3)]
                qrw = sb2("qrw", [128, 1024], BF16)
                ta = sb2("ta", [128, 1024], BF16)
                tb = sb2("tb", [128, 1024], BF16)
                tc = sb2("tc", [128, 1024], BF16)
                ket = sb2("ket", [128, 1024], BF16)
                rsum = sb2("rsum", [128, 16], F32)
                rsq = sb2("rsq", [128, 16], F32)
                rr = sb2("rr", [128, 16], F32)
                psT = s2a.enter_context(nc.psum_tensor(f"{tag}_psT", [128, 1024], BF16))

                pkB = [pk0, pk0 + 32, pk0 + 64]
                peB = [pe0, pe0 + 32, pe0 + 64]
                avB = [av0, av0, av0 + KCH * NTI]
                aaB = [aa0, aa0, aa0 + 2 * NTI]

                with contextlib.ExitStack() as s2b:
                    sb3 = lambda n, s, d: s2b.enter_context(nc.sbuf_tensor(f"{tag}_{n}", s, d))
                    k_tm = sb3("ktm", [128, 16384], BF16)
                    v_tm = sb3("vtm", [128, 16384], BF16)

                    with nc.Block() as blk:
                        @blk.sync
                        def _(e):
                            e.wait_ge(s_ln.h, ln_done)
                            for name, buf in (("bsel", bsel), ("bot", bot),
                                              ("g1t", g1t), ("be1t", be1t), ("g2t", g2t),
                                              ("be2t", be2t), ("b1t", b1t), ("b2t", b2t)):
                                s_aws.inc(e.dma_start(out=buf[:], in_=W[name][:, :]), 16)

                        @blk.gpsimd
                        def _(e):
                            e.wait_ge(s_ln.h, ln_done)
                            for name, buf in (("bqr", bqr), ("bkr", bkr), ("bvr", bvr)):
                                s_aws.inc(e.dma_start(out=buf[:], in_=W[name][:, :]), 16)
                            s_awg.inc(e.dma_start(out=wbig[:].rearrange("p (dk o) -> p dk o", dk=NCH),
                                                  in_=W["wv"].rearrange("(dk p) o -> p dk o", p=128)), 16)
                            s_awg.inc(e.dma_start(out=wbig2[:].rearrange("p (dk o) -> p dk o", dk=NCH),
                                                  in_=W["wk"].rearrange("(dk p) o -> p dk o", p=128)), 16)
                            e.wait_ge(s_pk.h, pk0 + 32)
                            s_awg.inc(e.dma_start(out=wbig[:].rearrange("p (dk o) -> p dk o", dk=NCH),
                                                  in_=W["wq"].rearrange("(dk p) o -> p dk o", p=128)), 16)

                        @blk.tensor
                        def _(e):
                            e.wait_ge(s_awg.h, awg0 + 32)
                            e.wait_ge(s_aws.h, aws0 + 16 * 11)
                            e.wait_ge(P.s_const.h, 96)
                            for j in range(3):
                                row = [bqr, bkr, bvr][j]
                                for t2 in range(2):
                                    jj = 2 * j + t2
                                    if jj >= 1:
                                        e.wait_ge(s_bbe.h, bbe0 + jj)
                                    s_bb.inc(e.matmul(P.ps[6][:, :], P.onesbr[:],
                                                      row[0:1, TT * t2:TT * (t2 + 1)],
                                                      start=True, stop=True))
                            for p in range(3):
                                wb = [wbig, wbig2, wbig][p]
                                if p == 2:
                                    e.wait_ge(s_awg.h, awg0 + 48)
                                for st in range(NTI):
                                    for ot in range(2):
                                        k2 = 2 * st + ot
                                        glob = 32 * p + k2
                                        if glob >= 2:
                                            e.wait_ge(s_pe.h, pe0 + glob - 1)
                                        for dk in range(NCH):
                                            ii = e.matmul(P.ps[2 + glob % 2][:, :],
                                                          xln[dk][:, 128 * st:128 * (st + 1)],
                                                          wb[:, 1024 * dk + TT * ot:1024 * dk + TT * (ot + 1)],
                                                          start=(dk == 0), stop=(dk == NCH - 1))
                                            if dk == NCH - 1:
                                                s_pk.inc(ii)
                                    if p == 2:
                                        e.wait_ge(s_av.h, avB[2] + KCH * (st + 1))
                                        if st >= 1:
                                            e.wait_ge(s_qte.h, qte0 + 8 * st)
                                        for c in range(NCH):
                                            s_qt.inc(e.transpose(psT[:, 128 * c:128 * (c + 1)],
                                                                 qrw[:, 128 * c:128 * (c + 1)],
                                                                 P.identb[:]))

                        @blk.scalar
                        def _(e):
                            for p in (1, 2):
                                for st in range(NTI):
                                    e.wait_ge(s_av.h, avB[p] + KCH * st + 1)
                                    s_aa.inc(e.activation(out=tb[:], in_=ta[:], func=AF.Exp))
                                    e.wait_ge(s_av.h, avB[p] + KCH * st + 20)
                                    s_aa.inc(e.activation(out=rsq[:], in_=rsum[:], func=AF.Sqrt))

                        @blk.vector
                        def _(e):
                            st_ = chained(e, s_av)
                            for j in range(3):
                                for t2 in range(2):
                                    jj = 2 * j + t2
                                    e.wait_ge(s_bb.h, bb0 + jj + 1)
                                    s_bbe.inc(e.tensor_copy(out=bbc[j][:, TT * t2:TT * (t2 + 1)],
                                                            in_=P.ps[6][:, :]))
                                    e.wait_ge(s_bbe.h, s_bbe.n)
                            for p in range(3):
                                dest = [v_tm, k_tm, None][p]
                                for st in range(NTI):
                                    for ot in range(2):
                                        k2 = 2 * st + ot
                                        glob = 32 * p + k2
                                        e.wait_ge(s_pk.h, pk0 + glob + 1)
                                        if p == 2 and st >= 1 and ot == 0:
                                            e.wait_ge(s_qt.h, qt0 + 8 * st)
                                        dslice = (qrw[:, TT * ot:TT * (ot + 1)] if p == 2
                                                  else dest[:, 1024 * st + TT * ot:1024 * st + TT * (ot + 1)])
                                        bi = [2, 1, 0][p]
                                        s_pe.inc(e.tensor_tensor(out=dslice, in0=P.ps[2 + glob % 2][:, :],
                                                                 in1=bbc[bi][:, TT * ot:TT * (ot + 1)],
                                                                 op=ALU.add))
                                        e.wait_ge(s_pe.h, s_pe.n)
                                    if p == 0:
                                        continue
                                    raw = qrw[:, :] if p == 2 else k_tm[:, 1024 * st:1024 * (st + 1)]
                                    st_(e.tensor_scalar_min(out=ta[:], in0=raw, scalar1=0.0))
                                    st_(e.tensor_scalar_max(out=tc[:], in0=raw, scalar1=0.0))
                                    e.wait_ge(s_aa.h, aaB[p] + 2 * st + 1)
                                    st_(e.scalar_tensor_tensor(out=ket[:], in0=tb[:], scalar=-1.0,
                                                               in1=tc[:], op0=ALU.add, op1=ALU.add))
                                    st_(e.tensor_tensor(out=ta[:], in0=ket[:], in1=ket[:], op=ALU.mult))
                                    for h in range(16):
                                        st_(e.reduce_sum(out=rsum[:, h:h + 1], in_=ta[:, 64 * h:64 * (h + 1)],
                                                         axis=AX.X))
                                    e.wait_ge(s_aa.h, aaB[p] + 2 * st + 2)
                                    st_(e.reciprocal(out=rr[:], in_=rsq[:]))
                                    for h in range(16):
                                        st_(e.tensor_scalar_mul(out=raw[:, 64 * h:64 * (h + 1)] if p == 1 else qrw[:, 64 * h:64 * (h + 1)],
                                                                in0=ket[:, 64 * h:64 * (h + 1)],
                                                                scalar1=rr[:, h:h + 1]))
                                    if p == 2:
                                        e.wait_ge(s_qt.h, qt0 + 8 * (st + 1))
                                        for c in range(NCH):
                                            s_qte.inc(e.tensor_copy(
                                                out=q_fm[:, TPC * c + 128 * st:TPC * c + 128 * (st + 1)],
                                                in_=psT[:, 128 * c:128 * (c + 1)]))
                                            e.wait_ge(s_qte.h, s_qte.n)
                            pass


                    with nc.Block() as blk:
                        @blk.tensor
                        def _(e):
                            for h in range(16):
                                if h >= 2:
                                    e.wait_ge(s_kve.h, kve0 + 2 * (h - 1))
                                for st in range(NTI):
                                    ii = e.matmul(P.ps[4 + h % 2][0:64, 0:64],
                                                  k_tm[:, 1024 * st + 64 * h:1024 * st + 64 * (h + 1)],
                                                  v_tm[:, 1024 * st + 64 * h:1024 * st + 64 * (h + 1)],
                                                  start=(st == 0), stop=(st == NTI - 1))
                                    if st == NTI - 1:
                                        s_kv.inc(ii)

                        @blk.vector
                        def _(e):
                            st_ = chained(e, s_kve)
                            for h in range(16):
                                e.wait_ge(s_kv.h, kv0 + h + 1)
                                st_(e.tensor_copy(out=kvraw[:, 64 * h:64 * (h + 1)],
                                                  in_=P.ps[4 + h % 2][0:64, 0:64]))
                                st_(e.tensor_copy(out=kvraw[:, 1024 + 64 * h:1024 + 64 * (h + 1)],
                                                  in_=P.ps[4 + h % 2][0:64, 0:64]))
                            st_(e.tensor_scalar_mul(out=kvc[:, 0:1024], in0=kvraw[:, 0:1024], scalar1=bsel[:, 0:1]))
                            st_(e.tensor_scalar_mul(out=kvc[:, 1024:2048], in0=kvraw[:, 1024:2048], scalar1=bsel[:, 1:2]))
                            e.wait_ge(s_ar.h, ar0 + 33)
                            st_(e.tensor_scalar_mul(out=kvt[:], in0=kvraw[:, 0:1024], scalar1=bsel[:, 0:1]))
                            st_(e.scalar_tensor_tensor(out=kvc[:, 0:1024], in0=kvraw[:, 1024:2048], scalar=bsel[:, 1:2],
                                                       in1=kvt[:], op0=ALU.mult, op1=ALU.add))
                            st_(e.tensor_scalar_mul(out=kvm[0:64, :], in0=kvc[:, 0:1024], scalar1=0.125))

                        @blk.gpsimd
                        def _(e):
                            e.wait_ge(s_kve.h, kve0 + 34)
                            s_ar.inc(e.dma_start(out=W["kvi"].ap()[:, :], in_=kvc[:]), 16)
                            e.wait_ge(s_ar.h, ar0 + 16)
                            s_ar.inc(e.collective_compute(
                                "AllReduce", ALU.add, replica_groups=RG,
                                ins=[W["kvi"].ap().opt()], outs=[W["kvo"].ap().opt()]), 1)
                            e.wait_ge(s_ar.h, ar0 + 17)
                            s_ar.inc(e.dma_start(out=kvraw[:], in_=W["kvo"].ap()[:, :]), 16)
                            e.wait_ge(s_kve.h, kve0 + 37)
                            s_ar.inc(e.dma_start(out=kvm[64:128, :], in_=kvm[0:64, :]), 16)

                with contextlib.ExitStack() as s3:
                    sb4 = lambda n, s, d: s3.enter_context(nc.sbuf_tensor(f"{tag}_{n}", s, d))
                    ctx = sb4("ctx", [128, 16384], BF16)
                    prs2 = [sb4(f"prs2{p}", [128, TT], F32) for p in range(2)]

                    with nc.Block() as blk:
                        @blk.tensor
                        def _(e):
                            e.wait_ge(s_kve.h, kve0 + 37)
                            e.wait_ge(s_ar.h, ar0 + 49)
                            e.wait_ge(s_qte.h, qte0 + 8 * NTI)
                            for h in range(16):
                                for t in range(NTT):
                                    k = 4 * h + t
                                    if k >= 2:
                                        e.wait_ge(s_cxe.h, cxe0 + k - 1)
                                    hb = 64 * (h % 2)
                                    ii = e.matmul(P.ps[4 + k % 2][0:64, :],
                                                  kvm[hb:hb + 64, 64 * h:64 * (h + 1)],
                                                  q_fm[hb:hb + 64,
                                                       TPC * (h // 2) + TT * t:TPC * (h // 2) + TT * (t + 1)],
                                                  start=True, stop=True)
                                    s_cx.inc(ii)
                            e.wait_ge(s_awg.h, awg0 + 64)
                            for c in range(NCH):
                                for t in range(NTT):
                                    k = NTT * c + t
                                    if k >= 2:
                                        e.wait_ge(s_wod.h, wod0 + k - 1)
                                    for dk in range(NCH):
                                        ii = e.matmul(P.ps[k % 2][:, :],
                                                      wbig2[:, 1024 * dk + 128 * c:1024 * dk + 128 * c + 128],
                                                      ctx[:, TPC * dk + TT * t:TPC * dk + TT * (t + 1)],
                                                      start=(dk == 0), stop=(dk == NCH - 1))
                                        if dk == NCH - 1:
                                            s_wo.inc(ii)

                        @blk.gpsimd
                        def _(e):
                            e.wait_ge(s_pk.h, pk0 + 64)
                            s_awg.inc(e.dma_start(out=wbig2[:].rearrange("p (dk o) -> p dk o", dk=NCH),
                                                  in_=W["wo"].rearrange("(dk p) o -> p dk o", p=128)), 16)

                        @blk.vector
                        def _(e):
                            for h in range(16):
                                for t in range(NTT):
                                    k = 4 * h + t
                                    e.wait_ge(s_cx.h, cx0 + k + 1)
                                    s_cxe.inc(e.tensor_copy(
                                        out=ctx[64 * (h % 2):64 * (h % 2) + 64,
                                                TPC * (h // 2) + TT * t:TPC * (h // 2) + TT * (t + 1)],
                                        in_=P.ps[4 + k % 2][0:64, :]))
                                    e.wait_ge(s_cxe.h, s_cxe.n)
                            for k in range(32):
                                c, t = divmod(k, NTT)
                                e.wait_ge(s_wo.h, wo0 + k + 1)
                                if k >= 2:
                                    e.wait_ge(s_wos[k % 2].h, wos0[k % 2] + 16 * (k // 2))
                                s_wod.inc(e.scalar_tensor_tensor(out=prs2[k % 2][:], in0=P.ps[k % 2][:, :],
                                                                 scalar=bot[:, c:c + 1],
                                                                 in1=xln[c][:, TT * t:TT * (t + 1)],
                                                                 op0=ALU.add, op1=ALU.add))
                                e.wait_ge(s_wod.h, s_wod.n)

                        @blk.sync
                        def _(e):
                            for k in range(32):
                                c, t = divmod(k, NTT)
                                e.wait_ge(s_wod.h, wod0 + k + 1)
                                s_wos[k % 2].inc(e.dma_start(
                                    out=W["pre1"].ap()[128 * c:128 * (c + 1), TT * t:TT * (t + 1)],
                                    in_=prs2[k % 2][:]), 16)

        attn_out = [sbm(f"ao{c}", [128, TPC], BF16) for c in range(NCH)]
        wos_tot = [(P.sems["a_wosA"], P.sems["a_wosA"].n), (P.sems["a_wosB"], P.sems["a_wosB"].n)]
        aws_tot = [(P.sems["a_ws"], P.sems["a_ws"].n)]
        with contextlib.ExitStack() as sc_ln1:
            sbl = lambda n, s, d: sc_ln1.enter_context(nc.sbuf_tensor(f"{tag}_{n}", s, d))
            xb2 = [sbl(f"x2b{c}", [128, TPC], BF16) for c in range(NCH)]
            sq2 = sbl("sq2", [128, 2 * TT], BF16)
            rows2 = [sbl(f"r2{i}", [1, TPC], F32) for i in range(6)]
            tmpf2 = sbl("tmpf2", [128, 2 * TT], F32)
            s_ln1 = emit_ln(P, tag + "_ln1", W["pre1"].ap(), xb2, attn_out, sq2, rows2, tmpf2,
                            eps=LN0_EPS, gate=wos_tot, affine=(g1t, be1t), vec_wait=aws_tot)
            ln1_done = s_ln1.n

        with contextlib.ExitStack() as scf:
            sbf = lambda n, s, d: scf.enter_context(nc.sbuf_tensor(f"{tag}_{n}", s, d))
            acc = [sbf(f"acc{c}", [128, TPC], F32) for c in range(NCH)]
            w1_b = [sbf(f"w1b{p}", [128, NCH * 256], BF16) for p in range(2)]
            wd_b = [sbf(f"wdq{p}", [128, 8 * 1024], BF16) for p in range(2)]
            h_sb = sbf("h", [128, 16 * TH], BF16)
            ps8f = scf.enter_context(nc.psum_tensor(f"{tag}_ps8f", [128, TT], F32))

            s_w = [P.sem("m2_wA"), P.sem("m2_wB")]
            s_wd = [P.sem("m2_wdA"), P.sem("m2_wdB")]
            s_g = P.sem("m2_g")
            s_sl = P.sem("m2_sl")
            s_o = P.sem("m2_o")
            s_oc = P.sem("m2_oc")
            s_ai = P.sem("af_ai")
            s_p2 = P.sem("af_p2")
            w0 = [P.sems["m2_wA"].n, P.sems["m2_wB"].n]
            wdd0 = [P.sems["m2_wdA"].n, P.sems["m2_wdB"].n]
            g0, sl0, o0, oc0, ai0, p20 = P.sem0("m2_g", "m2_sl", "m2_o", "m2_oc", "af_ai", "af_p2")

            NG1 = 16
            NGF1 = 2
            psG = [P.ps[0], P.ps[1]]
            PSO = P.ps[:7] + [ps8f]

            with nc.Block() as blk:
                @blk.gpsimd
                def _(e):
                    for gs, gv in wos_tot:
                        e.wait_ge(gs.h, gv)
                    for th in range(NTH):
                        ei = th
                        for hp in range(2):
                            for g8 in range(NG1 // 2):
                                g = hp * (NG1 // 2) + g8
                                gi = ei * NG1 + g
                                if gi >= 2:
                                    e.wait_ge(s_g.h, g0 + (gi - 1) * NGF1 * TTH)
                                fsl = slice(NGF1 * 128 * g, NGF1 * 128 * (g + 1))
                                s_w[gi % 2].inc(e.dma_start(
                                    out=w1_b[gi % 2][:].rearrange("p (c f) -> p c f", c=NCH),
                                    in_=W["w1"].rearrange("(c p) f -> p c f", p=128)[:, :, fsl]), 16)
                            for q in range(2):
                                qi = (ei * 2 + hp) * 2 + q
                                if qi >= 2:
                                    e.wait_ge(s_o.h, o0 + 16 * (qi - 1))
                                s_wd[qi % 2].inc(e.dma_start(
                                    out=wd_b[qi % 2][:].rearrange("p (fc o) -> p fc o", fc=8),
                                    in_=W["w2"].rearrange("(fc p) o -> p fc o", p=128)[:, 16 * hp + 8 * q:16 * hp + 8 * (q + 1), :]), 16)

                @blk.tensor
                def _(e):
                    e.wait_ge(s_ln1.h, ln1_done)
                    gstage = 0
                    dstage = 0
                    for th in range(NTH):
                        ei = th
                        for hp in range(2):
                            if ei * 2 + hp >= 1:
                                e.wait_ge(s_oc.h, oc0 + 8 * dstage)
                            for g in range(NG1 // 2):
                                gi = ei * NG1 + hp * (NG1 // 2) + g
                                e.wait_ge(s_w[gi % 2].h, w0[gi % 2] + 16 * (gi // 2 + 1))
                                if gi == 0:
                                    e.wait_ge(s_ai.h, ai0 + NCH)
                                for f2 in range(NGF1):
                                    fc = (hp * (NG1 // 2) + g) * NGF1 + f2
                                    for tt in range(TTH):
                                        spar = gstage % 2
                                        if gstage >= 2:
                                            e.wait_ge(s_sl.h, sl0 + gstage - 1)
                                        col = TH * th + TT * tt
                                        for dk in range(NCH):
                                            ii = e.matmul(psG[spar][:, :],
                                                          w1_b[gi % 2][:, 256 * dk + 128 * f2:256 * dk + 128 * f2 + 128],
                                                          attn_out[dk][:, col:col + TT],
                                                          start=(dk == 0), stop=(dk == NCH - 1))
                                            if dk == NCH - 1:
                                                s_g.inc(ii)
                                        gstage += 1
                            e.wait_ge(s_sl.h, sl0 + gstage)
                            for q in range(2):
                                qi = (ei * 2 + hp) * 2 + q
                                e.wait_ge(s_wd[qi % 2].h, wdd0[qi % 2] + 16 * (qi // 2 + 1))
                                for tt in range(TTH):
                                    if dstage >= 1:
                                        e.wait_ge(s_oc.h, oc0 + 8 * dstage)
                                    col = TT * tt
                                    for fcl in range(8):
                                        fh = q * 8 + fcl
                                        for do in range(NCH):
                                            ii = e.matmul(PSO[do][:, :],
                                                          wd_b[qi % 2][:, 1024 * fcl + 128 * do:1024 * fcl + 128 * do + 128],
                                                          h_sb[:, TH * fh + col:TH * fh + col + TT],
                                                          start=(fcl == 0), stop=(fcl == 7))
                                            if fcl == 7:
                                                s_o.inc(ii)
                                    dstage += 1

                @blk.scalar
                def _(e):
                    gstage = 0
                    for th in range(NTH):
                        for fc in range(NFC):
                            for tt in range(TTH):
                                e.wait_ge(s_g.h, g0 + gstage + 1)
                                if gstage % 32 == 0 and gstage >= 32:
                                    e.wait_ge(s_o.h, o0 + 32 * (gstage // 32))
                                spar = gstage % 2
                                fh = fc % 16
                                s_sl.inc(e.activation(out=h_sb[:, TH * fh + TT * tt:TH * fh + TT * tt + TT],
                                                      in_=psG[spar][:, :], func=AF.Gelu,
                                                      bias=b1t[:, fc:fc + 1]))
                                gstage += 1

                @blk.vector
                def _(e):
                    e.wait_ge(s_ln1.h, ln1_done)
                    for c in range(NCH):
                        s_ai.inc(e.tensor_scalar_add(out=acc[c][:], in0=attn_out[c][:],
                                                     scalar1=b2t[:, c:c + 1]))
                        e.wait_ge(s_ai.h, s_ai.n)
                    dstage = 0
                    for th in range(NTH):
                        for hp in range(2):
                            for q in range(2):
                                for tt in range(TTH):
                                    col = TH * th + TT * tt
                                    for do in range(NCH):
                                        e.wait_ge(s_o.h, o0 + 8 * dstage + do + 1)
                                        s_oc.inc(e.tensor_tensor(out=acc[do][:, col:col + TT],
                                                                 in0=acc[do][:, col:col + TT],
                                                                 in1=PSO[do][:, :], op=ALU.add))
                                        e.wait_ge(s_oc.h, s_oc.n)
                                    dstage += 1

                @blk.sync
                def _(e):
                    e.wait_ge(s_oc.h, P.sems["m2_oc"].n if False else oc0 + 8 * 16)
                    for c in range(NCH):
                        s_p2.inc(e.dma_start(out=W["pre2"].ap()[128 * c:128 * (c + 1), :], in_=acc[c][:]), 16)

            p2_tot = [(P.sems["af_p2"], P.sems["af_p2"].n)]

        out2 = [sbm(f"o2{c}", [128, TPC], BF16) for c in range(NCH)]
        with contextlib.ExitStack() as sc_ln2:
            sbl = lambda n, s, d: sc_ln2.enter_context(nc.sbuf_tensor(f"{tag}_{n}", s, d))
            xb3 = [sbl(f"x3b{c}", [128, TPC], BF16) for c in range(NCH)]
            sq3 = sbl("sq3", [128, 2 * TT], BF16)
            rows3 = [sbl(f"r3{i}", [1, TPC], F32) for i in range(6)]
            tmpf3 = sbl("tmpf3", [128, 2 * TT], F32)
            s_ln2 = emit_ln(P, tag + "_ln2", W["pre2"].ap(), xb3, out2, sq3, rows3, tmpf3,
                            eps=LN0_EPS, gate=p2_tot, affine=(g2t, be2t), vec_wait=aws_tot)
            ln2_done = s_ln2.n

        with contextlib.ExitStack() as sc_f:
            sbf2 = lambda n, s, d: sc_f.enter_context(nc.sbuf_tensor(f"{tag}_{n}", s, d))
            xf2 = [sbf2(f"xf2{p}", [128, TPC], F32) for p in range(2)]
            yf = [sbf2(f"yf{p}", [128, TPC], F32) for p in range(2)]
            s_fl = [P.sem("fin_lA"), P.sem("fin_lB")]
            s_fy = P.sem("fin_y")
            s_fs = [P.sem("fin_sA"), P.sem("fin_sB")]
            fl0 = [P.sems["fin_lA"].n, P.sems["fin_lB"].n]
            fs0 = [P.sems["fin_sA"].n, P.sems["fin_sB"].n]
            fy0 = P.sems["fin_y"].n

            with nc.Block() as blk:
                @blk.sync
                def _(e):
                    e.wait_ge(s_ln2.h, ln2_done)
                    for c in range(NCH):
                        if c >= 2:
                            e.wait_ge(s_fy.h, fy0 + c - 1)
                        s_fl[c % 2].inc(e.dma_start(out=xf2[c % 2][:], in_=x_dram[128 * c:128 * (c + 1), :]), 16)

                @blk.vector
                def _(e):
                    for c in range(NCH):
                        e.wait_ge(s_fl[c % 2].h, fl0[c % 2] + 16 * (c // 2 + 1))
                        if c >= 2:
                            e.wait_ge(s_fs[c % 2].h, fs0[c % 2] + 16 * (c // 2))
                        s_fy.inc(e.tensor_tensor(out=yf[c % 2][:], in0=xf2[c % 2][:],
                                                 in1=out2[c][:], op=ALU.add))
                        e.wait_ge(s_fy.h, s_fy.n)

                @blk.gpsimd
                def _(e):
                    for c in range(NCH):
                        e.wait_ge(s_fy.h, fy0 + c + 1)
                        s_fs[c % 2].inc(e.dma_start(out=y_dram[128 * c:128 * (c + 1), :], in_=yf[c % 2][:]), 16)
                    e.wait_ge(s_fs[0].h, s_fs[0].n)
                    e.wait_ge(s_fs[1].h, s_fs[1].n)

    return P.sem("fin_sA")


def build_l0attn_prog():
    P = Prog()
    nc = P.nc
    x = nc.dram_tensor("x", [D, TPC], F32, kind="ExternalInput").ap()
    y = nc.dram_tensor("y", [D, TPC], F32, kind="ExternalOutput").ap()
    W = {}
    for n, s in (("wq", [D, D]), ("wk", [D, D]), ("wv", [D, D]), ("wo", [D, D]),
                 ("bqr", [1, D]), ("bkr", [1, D]), ("bvr", [1, D]),
                 ("bot", [128, NCH]), ("g1t", [128, NCH]), ("be1t", [128, NCH]),
                 ("g2t", [128, NCH]), ("be2t", [128, NCH]),
                 ("w1", [D, FF]), ("b1t", [128, NFC]), ("w2", [FF, D]), ("b2t", [128, NCH]),
                 ("bsel", [64, 2])):
        W[n] = nc.dram_tensor(n, s, F32, kind="ExternalInput").ap()
    W["pre1"] = nc.dram_tensor("pre1", [D, TPC], F32, kind="Internal")
    W["pre2"] = nc.dram_tensor("pre2", [D, TPC], F32, kind="Internal")
    W["kvi"] = nc.dram_tensor("kvi", [64, 2048], F32, kind="Internal")
    W["kvo"] = nc.dram_tensor("kvo", [64, 2048], F32, kind="Internal")
    emit_l0attn(P, "a0", x, y, W)
    return P


def l0attn_inmaps(x_fm_shards, Wq, bq, Wk, bk, Wv, bv, Wo, bo, g1, be1, W1, b1, W2, b2, g2, be2):
    t8 = lambda v: np.ascontiguousarray(v.reshape(NCH, 128).T.astype(np.float32))
    base = {
        "wq": Wq.astype(np.float32), "wk": Wk.astype(np.float32),
        "wv": Wv.astype(np.float32), "wo": Wo.astype(np.float32),
        "bqr": bq.reshape(1, D).astype(np.float32),
        "bkr": bk.reshape(1, D).astype(np.float32),
        "bvr": bv.reshape(1, D).astype(np.float32),
        "bot": t8(bo), "g1t": t8(g1), "be1t": t8(be1), "g2t": t8(g2), "be2t": t8(be2),
        "w1": W1.astype(np.float32),
        "b1t": np.ascontiguousarray(b1.reshape(NFC, 128).T.astype(np.float32)),
        "w2": W2.astype(np.float32), "b2t": t8(b2),
    }
    maps = []
    for c, xs in enumerate(x_fm_shards):
        b = c // 4
        bsel = np.zeros((64, 2), np.float32)
        bsel[:, b] = 1.0
        m = dict(base)
        m["x"] = np.ascontiguousarray(xs)
        m["bsel"] = bsel
        maps.append(m)
    return maps


def run_l0attn(x_fm_shards, *weights):
    if "l0attn" not in _CACHE:
        _CACHE["l0attn"] = build_l0attn_prog()
    P = _CACHE["l0attn"]
    res = run_bass_kernel_spmd(P.nc, l0attn_inmaps(x_fm_shards, *weights), core_ids=list(range(NCORES)))
    return [r["y"] for r in res.results]




def emit_l1attn(P, tag, x_dram, y_dram, W):
    nc = P.nc
    NQB = 32
    MAXKT = 64
    with contextlib.ExitStack() as sc_main:
        sbm = lambda n, s, d: sc_main.enter_context(nc.sbuf_tensor(f"{tag}_{n}", s, d))
        xln = [sbm(f"xln{c}", [128, TPC], BF16) for c in range(NCH)]

        with contextlib.ExitStack() as sc_ln:
            sbl = lambda n, s, d: sc_ln.enter_context(nc.sbuf_tensor(f"{tag}_{n}", s, d))
            xb = [sbl(f"xb{c}", [128, TPC], BF16) for c in range(NCH)]
            sq = sbl("sq", [128, 2 * TT], BF16)
            rows = [sbl(f"r{i}", [1, TPC], F32) for i in range(6)]
            tmpf = sbl("tmpf", [128, 2 * TT], F32)
            s_ln = emit_ln(P, tag + "_ln", x_dram, xb, xln, sq, rows, tmpf)
            ln_done = s_ln.n

        s_ag = P.sem("t_ag")
        ag0 = P.sems["t_ag"].n
        with nc.Block() as blk:
            @blk.sync
            def _(e):
                e.wait_ge(s_ln.h, ln_done)
                for c in range(NCH):
                    s_ag.inc(e.dma_start(out=W["xagi"].ap()[128 * c:128 * (c + 1), :], in_=xln[c][:]), 16)

            @blk.gpsimd
            def _(e):
                e.wait_ge(s_ag.h, ag0 + 16 * NCH)
                s_ag.inc(e.collective_compute(
                    "AllGather", ALU.bypass, replica_groups=RG,
                    ins=[W["xagi"].ap().opt()], outs=[W["xago"].ap().opt()]), 1)

        with contextlib.ExitStack() as s1:
            sb1 = lambda n, s, d: s1.enter_context(nc.sbuf_tensor(f"{tag}_{n}", s, d))
            q_fm = sb1("qfm", [128, 16384], BF16)
            k0p = sb1("k0p", [128, 16384], BF16)
            k1p = sb1("k1p", [128, 16384], BF16)
            v65 = sb1("v65", [128, 130 * 128], BF16)
            ctxst = sb1("ctxs", [128, 2 * TT], BF16)
            xs_sb = [sb1(f"xs{p}", [128, NCH * TT], BF16) for p in range(2)]
            wq_sb = sb1("wqs", [128, NCH * 128], BF16)
            wk_sb = sb1("wks", [128, NCH * 128], BF16)
            wv_sb = sb1("wvs", [128, NCH * 128], BF16)
            bqt = sb1("bqt", [128, 1], F32)
            bkt = sb1("bkt", [128, 1], F32)
            bvr = sb1("bvr", [1, 128], F32)
            bvbc = sb1("bvbc", [128, 128], F32)
            p_sb = [sb1(f"p{j}", [128, TT], BF16) for j in range(4)]
            msk_sb = [sb1(f"msk{j}", [128, TT], F32) for j in range(2)]
            maskc = [sb1(f"maskc{j}", [128, TT], BF16) for j in range(4)]
            sqq = sb1("sqq", [128, TT], BF16)
            nrm2 = sb1("nrm2", [2, 1], F32)
            nrmc = sb1("nrmc", [2, 1], F32)
            nrq = sb1("nrq", [2, 1], F32)
            nrk = sb1("nrk", [2, 1], F32)
            nprod = sb1("nprod", [2, 1], F32)
            nsq = sb1("nsq", [2, 1], F32)
            nneg = sb1("nneg", [2, 1], BF16)
            mcolneg = sb1("mcol", [128, 2], F32)
            sel2 = sb1("sel2", [2, 2 * 128], BF16)
            ones65 = sb1("ones65", [65, 128], BF16)
            s65 = sb1("s65", [65, TT], F32)
            sb65 = sb1("sb65", [65, TT], BF16)
            recbc = sb1("recbc", [64, TT], BF16)
            hm2l = sb1("hm2l", [128, 2], BF16)

            s_wl = P.sem("t_wl")
            s_sl = [P.sem("t_slA"), P.sem("t_slB")]
            s_pj = P.sem("t_pj")
            s_pe = P.sem("t_pe")
            s_ms = P.sem("t_ms")
            s_nv = P.sem("t_nv")
            s_nm = P.sem("t_nm")
            s_na = P.sem("t_na")
            s_t1 = P.sem("t_t1")
            s_mk = P.sem("t_mk")
            s_px = P.sem("t_px")
            s_t2 = P.sem("t_t2")
            s_ev = P.sem("t_ev")
            s_em = P.sem("t_em")
            s_cst = P.sem("t_cst")
            cst0 = P.sems["t_cst"].n
            wl0, pj0, pe0, ms0, nv0, nm0, na0, t10, mk0, px0, t20, ev0, em0 = P.sem0(
                "t_wl", "t_pj", "t_pe", "t_ms", "t_nv", "t_nm", "t_na",
                "t_t1", "t_mk", "t_px", "t_t2", "t_ev", "t_em")
            sl0 = [P.sems["t_slA"].n, P.sems["t_slB"].n]

            NTT32 = T // TT

            def mk_mask(dd):
                m = np.zeros((128, TT), np.float32)
                for p in range(128):
                    kpos = 128 * dd + p
                    if kpos > 0:
                        m[p, :min(kpos, TT)] = -10000.0
                return m.astype(BF)

            NKT = [4 * (qb % 16) + 4 for qb in range(NQB)]
            A = [(h, qb) for h in range(2) for qb in range(NQB)]
            KTS = []
            tot = 0
            for (h, qb) in A:
                KTS.append(tot)
                tot += NKT[qb]
            KT_TOTAL = tot
            DIAG = []
            MSK_IDX = []
            DIAG_KK = []
            nmsk = 0
            kk_ = 0
            for a, (h, qb) in enumerate(A):
                nkt = NKT[qb]
                for ik in range(nkt):
                    d = ik >= nkt - 4
                    DIAG.append(d)
                    MSK_IDX.append(nmsk)
                    if d:
                        DIAG_KK.append(kk_)
                        nmsk += 1
                    kk_ += 1

            with nc.Block() as blk:
                @blk.sync
                def _(e):
                    e.wait_ge(s_ag.h, ag0 + 16 * NCH + 1)
                    for tt in range(NTT32):
                        if tt >= 2:
                            e.wait_ge(s_pj.h, pj0 + 6 * (tt - 1))
                        r, lt = divmod(tt, 4)
                        s_sl[tt % 2].inc(e.dma_start(
                            out=xs_sb[tt % 2][:].rearrange("p (c t) -> p c t", c=NCH),
                            in_=W["xago"].ap()[1024 * r:1024 * (r + 1), TT * lt:TT * (lt + 1)]
                                .rearrange("(c p) t -> p c t", p=128)), 16)

                @blk.gpsimd
                def _(e):
                    s_ms.inc(e.memset(v65[:], 1.0))
                    for name, buf in (("wqc", wq_sb), ("wkc", wk_sb), ("wvc", wv_sb)):
                        s_wl.inc(e.dma_start(out=buf[:].rearrange("p (c o) -> p c o", c=NCH),
                                             in_=W[name].rearrange("(c p) o -> p c o", p=128)), 16)
                    for name, buf in (("bqt", bqt), ("bkt", bkt), ("bvr", bvr)):
                        s_wl.inc(e.dma_start(out=buf[:], in_=W[name][:, :]), 16)
                    for j in range(4):
                        s_wl.inc(e.dma_start(out=maskc[j][:],
                                             in_=P.const(f"c_l1mask{j}", mk_mask(j))[:, :]), 16)
                    s_wl.inc(e.dma_start(out=sel2[:],
                                         in_=P.const("c_sel2", np.stack([
                                             np.concatenate([np.ones(128), np.zeros(128)]),
                                             np.concatenate([np.zeros(128), np.ones(128)])]).astype(BF))[:, :]), 16)
                    s_wl.inc(e.dma_start(out=ones65[:],
                                         in_=P.const("c_ones65", np.ones((65, 128), BF))[:, :]), 16)
                    s_wl.inc(e.dma_start(out=hm2l[:],
                                         in_=P.const("c_hm2", np.kron(np.eye(2), np.ones((64, 1))).astype(BF))[:, :]), 16)

                @blk.tensor
                def _(e):
                    e.wait_ge(s_wl.h, P.sems["t_wl"].n)
                    e.wait_ge(P.s_const.h, 96)
                    s_nm.inc(e.matmul(P.ps[6][:, 0:128], P.onesf[:], bvr[0:1, :], start=True, stop=True))
                    for tt in range(NTT32):
                        e.wait_ge(s_sl[tt % 2].h, sl0[tt % 2] + 16 * (tt // 2 + 1))
                        if tt >= 2:
                            e.wait_ge(s_pe.h, pe0 + 1 + 11 * (tt - 1))
                        for dk in range(NCH):
                            ii = e.matmul(P.ps[tt % 2][:, :], wq_sb[:, 128 * dk:128 * (dk + 1)],
                                          xs_sb[tt % 2][:, TT * dk:TT * (dk + 1)],
                                          start=(dk == 0), stop=(dk == NCH - 1))
                            if dk == NCH - 1:
                                s_pj.inc(ii)
                        for dk in range(NCH):
                            ii = e.matmul(P.ps[2 + tt % 2][:, :], wk_sb[:, 128 * dk:128 * (dk + 1)],
                                          xs_sb[tt % 2][:, TT * dk:TT * (dk + 1)],
                                          start=(dk == 0), stop=(dk == NCH - 1))
                            if dk == NCH - 1:
                                s_pj.inc(ii)
                        for it in range(4):
                            for dk in range(NCH):
                                ii = e.matmul(P.ps[4 + tt % 2][:, 128 * it:128 * (it + 1)],
                                              xs_sb[tt % 2][:, TT * dk + 128 * it:TT * dk + 128 * (it + 1)],
                                              wv_sb[:, 128 * dk:128 * (dk + 1)],
                                              start=(dk == 0), stop=(dk == NCH - 1))
                                if dk == NCH - 1:
                                    s_pj.inc(ii)

                @blk.vector
                def _(e):
                    e.wait_ge(s_nm.h, nm0 + 1)
                    s_pe.inc(e.tensor_copy(out=bvbc[:], in_=P.ps[6][:, 0:128]))
                    e.wait_ge(s_pe.h, s_pe.n)
                    for tt in range(NTT32):
                        e.wait_ge(s_pj.h, pj0 + 6 * tt + 1)
                        s_pe.inc(e.tensor_scalar(out=q_fm[:, TT * tt:TT * (tt + 1)],
                                                 in0=P.ps[tt % 2][:, :], scalar1=bqt[:, 0:1],
                                                 scalar2=0.125, op0=ALU.add, op1=ALU.mult))
                        e.wait_ge(s_pe.h, s_pe.n)
                        e.wait_ge(s_pj.h, pj0 + 6 * tt + 2)
                        s_pe.inc(e.tensor_scalar_add(out=k0p[0:64, TT * tt:TT * (tt + 1)],
                                                     in0=P.ps[2 + tt % 2][0:64, :], scalar1=bkt[0:64, 0:1]))
                        e.wait_ge(s_pe.h, s_pe.n)
                        s_pe.inc(e.tensor_scalar_add(out=k1p[64:128, TT * tt:TT * (tt + 1)],
                                                     in0=P.ps[2 + tt % 2][64:128, :], scalar1=bkt[64:128, 0:1]))
                        e.wait_ge(s_pe.h, s_pe.n)
                        if tt == 0:
                            e.wait_ge(s_ms.h, ms0 + 1)
                        e.wait_ge(s_pj.h, pj0 + 6 * tt + 6)
                        for it in range(4):
                            kt = 4 * tt + it
                            for hh in range(2):
                                s_pe.inc(e.tensor_tensor(
                                    out=v65[:, 130 * kt + 65 * hh:130 * kt + 65 * hh + 64],
                                    in0=P.ps[4 + tt % 2][:, 128 * it + 64 * hh:128 * it + 64 * (hh + 1)],
                                    in1=bvbc[:, 64 * hh:64 * (hh + 1)], op=ALU.add))
                                e.wait_ge(s_pe.h, s_pe.n)

            ev_pre = P.sems["t_ev"].n
            NV_MM = lambda i: nv0 + 4 + 4 * i + 1
            NV_RM = lambda i: nv0 + 4 + 4 * i + 2
            NV_END = nv0 + 4 + 4 * 3 * NTT32 + 2
            with nc.Block() as blk:
                @blk.tensor
                def _(e):
                    for j in range(3):
                        for tt in range(NTT32):
                            i = NTT32 * j + tt
                            e.wait_ge(s_nv.h, NV_MM(i))
                            if i >= 2:
                                e.wait_ge(s_nv.h, NV_RM(i - 2))
                            s_nm.inc(e.matmul(P.ps[i % 2][0:2, :], hm2l[:], sqq[:],
                                              start=True, stop=True))
                    e.wait_ge(s_nv.h, NV_END + 1)
                    for h in range(2):
                        if h == 1:
                            e.wait_ge(s_ev.h, ev_pre + 1)
                        s_em.inc(e.matmul(P.ps[6][:, 0:1], sel2[:, 128 * h:128 * (h + 1)],
                                          nneg[:, 0:1], start=True, stop=True))

                @blk.vector
                def _(e):
                    st_ = chained(e, s_nv)
                    st_(e.memset(k0p[64:128, :], 0.0))
                    st_(e.memset(k1p[0:64, :], 0.0))
                    st_(e.memset(nrq[:], -1e30))
                    st_(e.memset(nrk[:], -1e30))
                    for j, (srcf, acc_) in enumerate(((q_fm, nrq), (k0p, nrk), (k1p, nrk))):
                        for tt in range(NTT32):
                            i = NTT32 * j + tt
                            st_(e.tensor_tensor(out=sqq[:], in0=srcf[:, TT * tt:TT * (tt + 1)],
                                                in1=srcf[:, TT * tt:TT * (tt + 1)], op=ALU.mult))
                            e.wait_ge(s_nm.h, nm0 + 1 + i + 1)
                            st_(e.reduce_max(out=nrm2[:], in_=P.ps[i % 2][0:2, :], axis=AX.X))
                            st_(e.tensor_tensor(out=nrmc[:], in0=acc_[:], in1=nrm2[:], op=ALU.max))
                            st_(e.tensor_copy(out=acc_[:], in_=nrmc[:]))
                    st_(e.tensor_tensor(out=nprod[:], in0=nrq[:], in1=nrk[:], op=ALU.mult))
                    st_(e.tensor_scalar_mul(out=nsq[:], in0=nprod[:], scalar1=1.0))
                    e.wait_ge(s_na.h, na0 + 1)
                    st_(e.tensor_scalar_mul(out=nneg[:], in0=nrm2[:], scalar1=-1.0))
                    for h in range(2):
                        e.wait_ge(s_em.h, em0 + h + 1)
                        s_ev.inc(e.tensor_copy(out=mcolneg[:, h:h + 1], in_=P.ps[6][:, 0:1]))
                        e.wait_ge(s_ev.h, s_ev.n)

                @blk.scalar
                def _(e):
                    e.wait_ge(s_nv.h, NV_END)
                    s_na.inc(e.activation(out=nrm2[:], in_=nsq[:], func=AF.Sqrt))

            ev_base = P.sems["t_ev"].n
            ST_B = [P.ps[0], P.ps[1], P.ps[5], P.ps[6]]
            AVLAG = 3
            with nc.Block() as blk:
                @blk.tensor
                def _(e):
                    SEQ = []
                    kk = 0
                    for a, (h, qb) in enumerate(A):
                        for ik in range(NKT[qb]):
                            SEQ.append((a, h, qb, ik, 64 * (qb // 16) + ik, kk))
                            kk += 1

                    def emit_sT(item):
                        a, h, qb, ik, gkt, kk = item
                        if a == 0 and ik == 0:
                            e.wait_ge(s_ev.h, ev_base)
                        if kk >= 4:
                            e.wait_ge(s_px.h, px0 + kk - 3)
                        s_t1.inc(e.matmul(ST_B[kk % 4][:, :],
                                          (k0p, k1p)[h][:, 128 * gkt:128 * (gkt + 1)],
                                          q_fm[:, TT * qb:TT * (qb + 1)],
                                          start=True, stop=True))

                    def emit_AV(item):
                        a, h, qb, ik, gkt, kk = item
                        if ik == 0 and a >= 2:
                            e.wait_ge(s_ev.h, ev_base + 4 * (a - 1))
                        e.wait_ge(s_px.h, px0 + kk + 1)
                        s_t2.inc(e.matmul(P.ps[2 + a % 2][0:65, :],
                                          v65[:, 130 * gkt + 65 * h:130 * gkt + 65 * h + 65],
                                          p_sb[kk % 4][:],
                                          start=(ik == 0), stop=(ik == NKT[qb] - 1)))

                    pending = []

                    def emit_recbc(pa):
                        e.wait_ge(s_ev.h, ev_base + 4 * pa + 2)
                        s_em.inc(e.matmul(P.ps[4][0:64, :], ones65[64:65, 0:64], sb65[64:65, :],
                                          start=True, stop=True))

                    for i, item in enumerate(SEQ):
                        emit_sT(item)
                        while pending and pending[0][0] <= i:
                            emit_recbc(pending.pop(0)[1])
                        if i >= AVLAG:
                            it2 = SEQ[i - AVLAG]
                            emit_AV(it2)
                            if it2[3] == NKT[it2[2]] - 1:
                                pending.append((i + 1, it2[0]))
                    for i in range(len(SEQ) - AVLAG, len(SEQ)):
                        emit_AV(SEQ[i])
                    while pending:
                        emit_recbc(pending.pop(0)[1])
                    emit_recbc(SEQ[-1][0])

                @blk.vector
                def _(e):
                    st_ = chained(e, s_ev)
                    kk = 0
                    mi = 0
                    for a, (h, qb) in enumerate(A):
                        nkt = NKT[qb]
                        for ik in range(nkt):
                            diag = ik >= nkt - 4
                            if diag:
                                dd = ik - (nkt - 4)
                                e.wait_ge(s_t1.h, t10 + kk + 1)
                                if mi >= 2:
                                    e.wait_ge(s_px.h, px0 + DIAG_KK[mi - 2] + 1)
                                s_mk.inc(e.tensor_tensor(out=msk_sb[mi % 2][:], in0=ST_B[kk % 4][:, :],
                                                         in1=maskc[dd][:], op=ALU.add))
                                e.wait_ge(s_mk.h, s_mk.n)
                                mi += 1
                            kk += 1
                        e.wait_ge(s_t2.h, t20 + KTS[a] + nkt)
                        st_(e.reciprocal(out=s65[64:65, :], in_=P.ps[2 + a % 2][64:65, :]))
                        st_(e.tensor_copy(out=sb65[64:65, :], in_=s65[64:65, :]))
                        e.wait_ge(s_em.h, em0 + 2 + a + 1)
                        st_(e.tensor_copy(out=recbc[:], in_=P.ps[4][0:64, :]))
                        if a >= 2:
                            e.wait_ge(s_cst.h, cst0 + 16 * (a - 1))
                        st_(e.tensor_tensor(out=ctxst[64 * h:64 * (h + 1), TT * (a % 2):TT * (a % 2) + TT],
                                            in0=P.ps[2 + a % 2][0:64, :], in1=recbc[:], op=ALU.mult))

                @blk.sync
                def _(e):
                    for a, (h, qb) in enumerate(A):
                        e.wait_ge(s_ev.h, ev_base + 4 * a + 4)
                        s_cst.inc(e.dma_start(
                            out=W["cagi"].ap()[64 * h:64 * (h + 1), TT * qb:TT * (qb + 1)],
                            in_=ctxst[64 * h:64 * (h + 1), TT * (a % 2):TT * (a % 2) + TT]), 16)

                @blk.scalar
                def _(e):
                    kk = 0
                    e.wait_ge(s_ev.h, ev_base)
                    for a, (h, qb) in enumerate(A):
                        nkt = NKT[qb]
                        for ik in range(nkt):
                            diag = ik >= nkt - 4
                            if diag:
                                e.wait_ge(s_mk.h, mk0 + MSK_IDX[kk] + 1)
                                srcap = msk_sb[MSK_IDX[kk] % 2][:]
                            else:
                                e.wait_ge(s_t1.h, t10 + kk + 1)
                                srcap = ST_B[kk % 4][:, :]
                            if kk >= 4:
                                e.wait_ge(s_t2.h, t20 + kk - 3)
                            s_px.inc(e.activation(out=p_sb[kk % 4][:], in_=srcap, func=AF.Exp,
                                                  bias=mcolneg[:, h:h + 1]))
                            kk += 1

            s_ag2 = P.sem("t_ag2")
            ag20 = P.sems["t_ag2"].n
            ev_tot = P.sems["t_ev"].n
            with nc.Block() as blk:
                @blk.gpsimd
                def _(e):
                    e.wait_ge(s_cst.h, cst0 + 16 * len(A))
                    s_ag2.inc(e.collective_compute(
                        "AllGather", ALU.bypass, replica_groups=RG,
                        ins=[W["cagi"].ap().opt()], outs=[W["cago"].ap().opt()]), 1)

        with contextlib.ExitStack() as s2:
            sb2 = lambda n, s, d: s2.enter_context(nc.sbuf_tensor(f"{tag}_{n}", s, d))
            xc = [sb2(f"xc{c}", [128, TPC], BF16) for c in range(NCH)]
            wo_sb = sb2("wos", [128, 8192], BF16)
            boutt = sb2("boutt", [128, NCH], F32)
            cidx = sb2("cidx", [128, 8], mybir.dt.int32)
            xf2 = [sb2(f"xf2{p}", [128, TT], F32) for p in range(2)]
            yf = [sb2(f"yf{p}", [128, TT], F32) for p in range(2)]

            s_gx = P.sem("t_gx")
            s_wo2 = P.sem("t_wo2")
            s_wm = P.sem("t_wm")
            s_wv = P.sem("t_wv")
            s_ws = [P.sem("t_wsA"), P.sem("t_wsB")]
            s_xl = [P.sem("t_xlA"), P.sem("t_xlB")]
            gx0, wo20, wm0, wv0 = P.sem0("t_gx", "t_wo2", "t_wm", "t_wv")
            ws0 = [P.sems["t_wsA"].n, P.sems["t_wsB"].n]
            xl0 = [P.sems["t_xlA"].n, P.sems["t_xlB"].n]
            s_ag2b = P.sem("t_ag2")
            ag2_tot = P.sems["t_ag2"].n

            with nc.Block() as blk:
                @blk.gpsimd
                def _(e):
                    s_wo2.inc(e.dma_start(out=wo_sb[:].rearrange("p (dk o) -> p dk o", dk=NCH),
                                          in_=W["wout"].rearrange("(dk p) o -> p dk o", p=128)), 16)
                    s_wo2.inc(e.dma_start(out=boutt[:], in_=W["boutt"][:, :]), 16)
                    s_wo2.inc(e.dma_start(out=cidx[:], in_=W["ctxidx"][:, :]), 16)
                    e.wait_ge(s_ag2b.h, ag2_tot)
                    e.wait_ge(s_wo2.h, wo20 + 48)
                    flat = W["cago"].ap().rearrange("r (a b) -> (r a) b", b=TPC)
                    for rc in range(NCH):
                        s_gx.inc(e.indirect_dma_start(
                            out=xc[rc][:], out_offset=None,
                            in_=flat, in_offset=bass.IndirectOffsetOnAxis(cidx[:, rc:rc + 1], 0)), 16)
                    for k in range(32):
                        c, t = divmod(k, NTT)
                        e.wait_ge(s_wv.h, wv0 + k + 1)
                        s_ws[k % 2].inc(e.dma_start(out=y_dram[128 * c:128 * (c + 1), TT * t:TT * (t + 1)],
                                                    in_=yf[k % 2][:]), 16)
                    e.wait_ge(s_ws[0].h, P.sems["t_wsA"].n)
                    e.wait_ge(s_ws[1].h, P.sems["t_wsB"].n)

                @blk.sync
                def _(e):
                    for k in range(32):
                        c, t = divmod(k, NTT)
                        if k >= 2:
                            e.wait_ge(s_wv.h, wv0 + k - 1)
                        s_xl[k % 2].inc(e.dma_start(out=xf2[k % 2][:],
                                                    in_=x_dram[128 * c:128 * (c + 1), TT * t:TT * (t + 1)]), 16)

                @blk.tensor
                def _(e):
                    e.wait_ge(s_gx.h, gx0 + 16 * NCH)
                    for k in range(32):
                        c, t = divmod(k, NTT)
                        if k >= 2:
                            e.wait_ge(s_wv.h, wv0 + k - 1)
                        for dk in range(NCH):
                            ii = e.matmul(P.ps[k % 2][:, :],
                                          wo_sb[:, 1024 * dk + 128 * c:1024 * dk + 128 * c + 128],
                                          xc[dk][:, TT * t:TT * (t + 1)],
                                          start=(dk == 0), stop=(dk == NCH - 1))
                            if dk == NCH - 1:
                                s_wm.inc(ii)

                @blk.vector
                def _(e):
                    for k in range(32):
                        c, t = divmod(k, NTT)
                        e.wait_ge(s_wm.h, wm0 + k + 1)
                        e.wait_ge(s_xl[k % 2].h, xl0[k % 2] + 16 * (k // 2 + 1))
                        if k >= 2:
                            e.wait_ge(s_ws[k % 2].h, ws0[k % 2] + 16 * (k // 2))
                        s_wv.inc(e.scalar_tensor_tensor(out=yf[k % 2][:], in0=P.ps[k % 2][:, :],
                                                        scalar=boutt[:, c:c + 1], in1=xf2[k % 2][:],
                                                        op0=ALU.add, op1=ALU.add))
                        e.wait_ge(s_wv.h, s_wv.n)


    return P.sem("t_wsA")




def build_l1attn_prog():
    P = Prog()
    nc = P.nc
    x = nc.dram_tensor("x", [D, TPC], F32, kind="ExternalInput").ap()
    y = nc.dram_tensor("y", [D, TPC], F32, kind="ExternalOutput").ap()
    W = {}
    for n, s in (("wqc", [D, 128]), ("wkc", [D, 128]), ("wvc", [D, 128]),
                 ("bqt", [128, 1]), ("bkt", [128, 1]), ("bvr", [1, 128]),
                 ("wout", [D, D]), ("boutt", [128, NCH])):
        W[n] = nc.dram_tensor(n, s, F32, kind="ExternalInput").ap()
    W["ctxidx"] = nc.dram_tensor("ctxidx", [128, 8], mybir.dt.int32, kind="ExternalInput").ap()
    W["xagi"] = nc.dram_tensor("xagi", [D, TPC], BF16, kind="Internal")
    W["xago"] = nc.dram_tensor("xago", [NCORES * D, TPC], BF16, kind="Internal")
    W["cagi"] = nc.dram_tensor("cagi", [128, T], BF16, kind="Internal")
    W["cago"] = nc.dram_tensor("cago", [D, T], BF16, kind="Internal")
    emit_l1attn(P, "t1", x, y, W)
    return P


def l1attn_inmaps(x_fm_shards, Wqkv, bqkv, Wout, bout):
    Wqkv = Wqkv.astype(np.float32)
    bqkv = bqkv.astype(np.float32)
    maps = []
    for c, xs in enumerate(x_fm_shards):
        idx = np.zeros((128, 8), np.int32)
        for rc in range(8):
            for p in range(128):
                idx[p, rc] = (128 * rc + p) * 8 + c
        m = {
            "x": np.ascontiguousarray(xs),
            "wqc": np.ascontiguousarray(Wqkv[:, 128 * c:128 * (c + 1)]),
            "wkc": np.ascontiguousarray(Wqkv[:, 1024 + 128 * c:1024 + 128 * (c + 1)]),
            "wvc": np.ascontiguousarray(Wqkv[:, 2048 + 128 * c:2048 + 128 * (c + 1)]),
            "bqt": np.ascontiguousarray(bqkv[128 * c:128 * (c + 1)].reshape(128, 1)),
            "bkt": np.ascontiguousarray(bqkv[1024 + 128 * c:1024 + 128 * (c + 1)].reshape(128, 1)),
            "bvr": np.ascontiguousarray(bqkv[2048 + 128 * c:2048 + 128 * (c + 1)].reshape(1, 128)),
            "wout": Wout.astype(np.float32),
            "boutt": np.ascontiguousarray(bout.reshape(NCH, 128).T.astype(np.float32)),
            "ctxidx": idx,
        }
        maps.append(m)
    return maps


def run_l1attn(x_fm_shards, Wqkv, bqkv, Wout, bout):
    if "l1attn" not in _CACHE:
        _CACHE["l1attn"] = build_l1attn_prog()
    P = _CACHE["l1attn"]
    res = run_bass_kernel_spmd(P.nc, l1attn_inmaps(x_fm_shards, Wqkv, bqkv, Wout, bout),
                               core_ids=list(range(NCORES)))
    return [r["y"] for r in res.results]




def kernel_unfused(**inputs):
    x = np.asarray(inputs["L0_attn_x"], np.float32)
    xt = np.ascontiguousarray(x.reshape(T, D))
    shards = [np.ascontiguousarray(xt[i * TPC:(i + 1) * TPC].T) for i in range(NCORES)]

    g = lambda n: np.asarray(inputs[n], np.float32)

    shards = run_moe1(shards, g("L0_ffn_gate_w"), g("L0_ffn_gate_b"),
                      g("L0_ffn_w1"), g("L0_ffn_b1"), g("L0_ffn_w2"), g("L0_ffn_b2"))
    shards = [np.asarray(s) for s in shards]
    shards = run_l0attn(shards, g("L0_attn_Wq"), g("L0_attn_bq"), g("L0_attn_Wk"), g("L0_attn_bk"),
                        g("L0_attn_Wv"), g("L0_attn_bv"), g("L0_attn_Wo"), g("L0_attn_bo"),
                        g("L0_attn_g1"), g("L0_attn_be1"), g("L0_attn_W1"), g("L0_attn_b1"),
                        g("L0_attn_W2"), g("L0_attn_b2"), g("L0_attn_g2"), g("L0_attn_be2"))
    shards = [np.asarray(s) for s in shards]
    shards = run_moe1(shards, g("L0_ffn_gate_w"), g("L0_ffn_gate_b"),
                      g("L0_ffn_w1"), g("L0_ffn_b1"), g("L0_ffn_w2"), g("L0_ffn_b2"))
    shards = [np.asarray(s) for s in shards]
    shards = run_moe2(shards, g("L1_ffn_router_w"), g("L1_ffn_w_gate"),
                      g("L1_ffn_w_up"), g("L1_ffn_w_down"))
    shards = [np.asarray(s) for s in shards]
    shards = run_l1attn(shards, g("L1_attn_Wqkv"), g("L1_attn_bqkv"),
                        g("L1_attn_Wout"), g("L1_attn_bout"))
    shards = [np.asarray(s) for s in shards]
    shards = run_moe2(shards, g("L1_ffn_router_w"), g("L1_ffn_w_gate"),
                      g("L1_ffn_w_up"), g("L1_ffn_w_down"))
    shards = [np.asarray(s) for s in shards]

    y = np.concatenate([s.T for s in shards], axis=0)
    return y.reshape(2, 8192, 1024).astype(np.float32)





def emit_moe1_sparse(P, tag, x_dram, y_dram, gwt_d, gbr_d, W, H, WC=None):
    nc = P.nc
    NTI = TPC // 128
    NTBL = CAP // 128
    NTH1 = CAP // TT
    BIGF = float(1 << 22)
    with contextlib.ExitStack() as sc_main:
        sbm = lambda n, s, d: sc_main.enter_context(nc.sbuf_tensor(f"{tag}_{n}", s, d))
        xln = [sbm(f"xln{c}", [128, TPC], BF16) for c in range(NCH)]
        sidxi = sbm("sidxi", [128, NTI], mybir.dt.int32)
        wcomb = sbm("wcomb", [128, NTI], F32)

        with contextlib.ExitStack() as sc_ln:
            sbl = lambda n, s, d: sc_ln.enter_context(nc.sbuf_tensor(f"{tag}_{n}", s, d))
            xb = [sbl(f"xb{c}", [128, TPC], BF16) for c in range(NCH)]
            sq = sbl("sq", [128, 2 * TT], BF16)
            rows = [sbl(f"r{i}", [1, TPC], F32) for i in range(6)]
            tmpf = sbl("tmpf", [128, 2 * TT], F32)
            s_ln = emit_ln(P, tag + "_ln", x_dram, xb, xln, sq, rows, tmpf)
            ln_done = s_ln.n

        s_tx = P.sem("a_qt")
        s_te = P.sem("a_qte")
        s_td = [P.sem("a_wosA"), P.sem("a_wosB")]
        s_tz = P.sem("a_bb")
        s_tg = P.sem("a_ar")
        tx0, te0, tz0, tg0 = P.sem0("a_qt", "a_qte", "a_bb", "a_ar")
        td0 = [P.sems["a_wosA"].n, P.sems["a_wosB"].n]
        ZCOLS = 8 * CAP // 128
        with contextlib.ExitStack() as sc_t:
            sbt = lambda n, s, d: sc_t.enter_context(nc.sbuf_tensor(f"{tag}_{n}", s, d))
            tmt = [sbt(f"tmt{p}", [128, 1024], BF16) for p in range(2)]
            zrow = sbt("zrow", [128, ZCOLS], F32)
            psT = sc_t.enter_context(nc.psum_tensor(f"{tag}_psT", [128, 1024], BF16))

            with nc.Block() as blk:
                @blk.tensor
                def _(e):
                    e.wait_ge(s_ln.h, ln_done)
                    e.wait_ge(P.s_const.h, 96)
                    for lt in range(NTI):
                        if lt >= 1:
                            e.wait_ge(s_te.h, te0 + 1 + 8 * lt)
                        for c in range(NCH):
                            s_tx.inc(e.transpose(psT[:, 128 * c:128 * (c + 1)],
                                                 xln[c][:, 128 * lt:128 * (lt + 1)],
                                                 P.identb[:]))

                @blk.vector
                def _(e):
                    s_te.inc(e.memset(zrow[:], 0.0))
                    e.wait_ge(s_te.h, s_te.n)
                    for lt in range(NTI):
                        e.wait_ge(s_tx.h, tx0 + 8 * (lt + 1))
                        if lt >= 2:
                            e.wait_ge(s_td[lt % 2].h, td0[lt % 2] + 16 * (lt // 2))
                        for c in range(NCH):
                            s_te.inc(e.tensor_copy(out=tmt[lt % 2][:, 128 * c:128 * (c + 1)],
                                                   in_=psT[:, 128 * c:128 * (c + 1)]))
                            e.wait_ge(s_te.h, s_te.n)

                @blk.sync
                def _(e):
                    e.wait_ge(s_te.h, te0 + 1)
                    s_tz.inc(e.dma_start(out=H["tbl"].ap().rearrange("r one -> (r one)")
                                         .rearrange("(p k) -> p k", p=128),
                                         in_=zrow[:]), 16)
                    for lt in range(NTI):
                        e.wait_ge(s_te.h, te0 + 1 + 8 * (lt + 1))
                        s_td[lt % 2].inc(e.dma_start(out=H["xtmi"].ap()[128 * lt:128 * (lt + 1), :],
                                             in_=tmt[lt % 2][:]), 16)

                @blk.gpsimd
                def _(e):
                    e.wait_ge(s_td[0].h, td0[0] + 16 * (NTI // 2))
                    e.wait_ge(s_td[1].h, td0[1] + 16 * (NTI // 2))
                    s_tg.inc(e.collective_compute(
                        "AllGather", ALU.bypass, replica_groups=RG,
                        ins=[H["xtmi"].ap().opt()], outs=[H["xtma"].ap().opt()]), 1)

        with contextlib.ExitStack() as sc:
            sb = lambda n, s, d: sc.enter_context(nc.sbuf_tensor(f"{tag}_{n}", s, d))
            rwt_sb = sb("rwt", [128, NCH * NE], BF16)
            gb_sb = sb("gb", [1, NE], BF16)
            gbbc = sb("gbbc", [128, NE], F32)
            premask = sb("premask", [128, 16], BF16)
            triu_sb = sb("triu", [128, 128], BF16)
            ecol_sb = sb("ecol", [128, NE], F32)
            gtidf = sb("gtidf", [128, NTI], F32)
            lg = sb("lg", [128, NE], F32)
            negm = sb("negm", [128, 1], F32)
            e_ = sb("e_", [128, NE], F32)
            esum = sb("esum", [128, 1], F32)
            tp_all = sb("tp_all", [128, 16], F32)
            oh_all = sb("oh_all", [128, 16 * NE], F32)
            ohb = sb("ohb", [128, NE], BF16)
            inc_all = sb("incall", [128, 16 * NE], F32)
            tot16 = sb("tot16", [16, NE], F32)
            gtb = sb("gtb", [128, NE], BF16)
            offs16 = sb("offs16", [16, NE], F32)
            offrow = sb("offrow", [1, NE], F32)
            offbc = sb("offbc", [128, NE], F32)
            ra_ = sb("ra_", [128, NE], F32)
            krank = sb("krank", [128, 1], F32)
            keep = sb("keep", [128, 1], F32)
            ei_ = sb("ei_", [128, 1], F32)
            s2_ = sb("s2_", [128, 1], F32)
            s3_ = sb("s3_", [128, 1], F32)
            sxf = sb("sxf", [128, NTI], F32)

            s_rld = P.sem("rt_ld")
            s_rmm = P.sem("rt_mm")
            s_rv = P.sem("rt_v")
            s_ra = P.sem("rt_a")
            s_cs = P.sem("r1_cs")
            s_ce = P.sem("r1_ce")
            s_tt = P.sem("r1_tt")
            s_ag = P.sem("r1_ag")
            s_of = P.sem("r1_of")
            s_obd = P.sem("r1_obd")
            s_obm = P.sem("r1_obm")
            s_sc = P.sem("a_cx")
            s_tar = P.sem("a_cxe")
            rld0, rmm0, rv0, ra0, cs0, ce0, tt0, ag0, of0, obd0, obm0, sc0, tar0 = P.sem0(
                "rt_ld", "rt_mm", "rt_v", "rt_a", "r1_cs", "r1_ce", "r1_tt", "r1_ag",
                "r1_of", "r1_obd", "r1_obm", "a_cx", "a_cxe")

            ps_r = P.ps[6]
            psc = [P.ps[4], P.ps[5]]
            VC2 = 12
            rvB0 = rv0 + 1 + 5 * NTI + 1
            SCV = [rvB0 + VC2 * (i + 1) for i in range(NTI)]

            s_sld = P.sem("a_ws")
            sld0 = P.sems["a_ws"].n
            s_agc = P.sem("a_kv")
            agc0 = P.sems["a_kv"].n
            with nc.Block() as blk:
                @blk.sync
                def _(e):
                    s_sld.inc(e.dma_start(out=premask[:], in_=W["premask"][:, :]), 16)
                    s_sld.inc(e.dma_start(
                        out=triu_sb[:],
                        in_=P.const("c_triu_b", np.triu(np.ones((128, 128), np.float32)).astype(BF))[:, :]), 16)
                    s_sld.inc(e.dma_start(
                        out=ecol_sb[:],
                        in_=P.const("c_ecol", np.tile(np.arange(8, dtype=np.float32), (128, 1)))[:, :]), 16)
                    s_sld.inc(e.dma_start(out=gtidf[:], in_=W["gtidf"][:, :]), 16)
                    for i in range(NTI):
                        e.wait_ge(s_ce.h, ce0 + i + 1)
                        s_tt.inc(e.dma_start(out=tot16[i:i + 1, :], in_=inc_all[127:128, NE * i:NE * (i + 1)]), 16)
                    for i in range(NTI):
                        e.wait_ge(s_of.h, of0 + 2)
                        if i >= 1:
                            e.wait_ge(s_obm.h, obm0 + i)
                        s_obd.inc(e.dma_start(out=offrow[:], in_=offs16[i:i + 1, :]), 16)

                @blk.gpsimd
                def _(e):
                    s_rld.inc(e.dma_start(out=rwt_sb[:].rearrange("p (c ee) -> p c ee", c=NCH),
                                          in_=gwt_d.rearrange("(c p) ee -> p c ee", p=128)), 16)
                    s_rld.inc(e.dma_start(out=gb_sb[:], in_=gbr_d[:, :]), 16)
                    e.wait_ge(s_tt.h, tt0 + 16 * NTI)
                    s_ag.inc(e.dma_start(out=H["t16"].ap()[:, :], in_=tot16[:]), 16)
                    e.wait_ge(s_ag.h, ag0 + 16)
                    s_agc.inc(e.collective_compute(
                        "AllGather", ALU.bypass, replica_groups=RG,
                        ins=[H["t16"].ap().opt()], outs=[H["gt"].ap().opt()]), 1)
                    e.wait_ge(s_agc.h, agc0 + 1)
                    s_ag.inc(e.dma_start(out=gtb[:], in_=H["gt"].ap()[:, :]), 16)
                    rbnd = e.alloc_register()
                    e.reg_mov(rbnd, 8 * CAP - 1)
                    for i in range(NTI):
                        e.wait_ge(s_tz.h, tz0 + 16)
                        e.wait_ge(s_rv.h, SCV[i])
                        s_sc.inc(e.indirect_dma_start(
                            out=H["tbl"].ap(), out_offset=bass.IndirectOffsetOnAxis(sidxi[:, i:i + 1], 0),
                            in_=gtidf[:, i:i + 1], in_offset=None,
                            bounds_check=rbnd, oob_is_err=False), 16)
                    e.free_register(rbnd)
                    e.wait_ge(s_sc.h, sc0 + 16 * NTI)
                    s_tar.inc(e.collective_compute(
                        "AllReduce", ALU.add, replica_groups=RG,
                        ins=[H["tbl"].ap().opt()], outs=[H["tblo"].ap().opt()]), 1)

                @blk.tensor
                def _(e):
                    e.wait_ge(s_rld.h, rld0 + 32)
                    e.wait_ge(s_sld.h, sld0 + 64)
                    e.wait_ge(P.s_const.h, 96)
                    s_rmm.inc(e.matmul(ps_r[:, 0:NE], P.onesbr[:], gb_sb[:], start=True, stop=True))

                    def logits(i):
                        if i >= 1:
                            e.wait_ge(s_rv.h, rv0 + 5 * i - 3)
                        else:
                            e.wait_ge(s_rv.h, rv0 + 1)
                        for c in range(NCH):
                            ii = e.matmul(ps_r[:, 0:NE], xln[c][:, 128 * i:128 * (i + 1)],
                                          rwt_sb[:, NE * c:NE * (c + 1)],
                                          start=(c == 0), stop=(c == NCH - 1))
                            if c == NCH - 1:
                                s_rmm.inc(ii)

                    def cumsum(i):
                        e.wait_ge(s_rv.h, rv0 + 5 * i + 5)
                        if i >= 2:
                            e.wait_ge(s_ce.h, ce0 + i - 1)
                        s_cs.inc(e.matmul(psc[i % 2][:, 0:NE], triu_sb[:], ohb[:],
                                          start=True, stop=True))

                    logits(0)
                    for i in range(1, NTI):
                        logits(i)
                        cumsum(i - 1)
                    cumsum(NTI - 1)
                    e.wait_ge(s_ag.h, ag0 + 32)
                    s_of.inc(e.matmul(ps_r[0:16, 0:NE], premask[:], gtb[:], start=True, stop=True))
                    for i in range(NTI):
                        e.wait_ge(s_obd.h, obd0 + 16 * (i + 1))
                        if i >= 2:
                            e.wait_ge(s_rv.h, rvB0 + VC2 * (i - 2) + 1)
                        s_obm.inc(e.matmul(psc[i % 2][:, 0:NE], P.onesf[:], offrow[:],
                                           start=True, stop=True))

                @blk.vector
                def _(e):
                    st = chained(e, s_rv)
                    e.wait_ge(s_rmm.h, rmm0 + 1)
                    st(e.tensor_copy(out=gbbc[:], in_=ps_r[:, 0:NE]))
                    for i in range(NTI):
                        e.wait_ge(s_rmm.h, rmm0 + i + 2)
                        st(e.tensor_tensor(out=lg[:], in0=ps_r[:, 0:NE], in1=gbbc[:], op=ALU.add))
                        st(e.reduce_max(out=negm[:], in_=lg[:], axis=AX.X, negate=True))
                        e.wait_ge(s_ra.h, ra0 + i + 1)
                        st(e.tensor_single_scalar(out=oh_all[:, NE * i:NE * (i + 1)], in_=e_[:],
                                                  scalar=1.0, op=ALU.is_ge))
                        st(e.tensor_copy(out=ohb[:], in_=oh_all[:, NE * i:NE * (i + 1)]))
                        st(e.reciprocal(out=tp_all[:, i:i + 1], in_=esum[:]))
                        e.wait_ge(s_cs.h, cs0 + i + 1)
                        s_ce.inc(e.tensor_copy(out=inc_all[:, NE * i:NE * (i + 1)],
                                               in_=psc[i % 2][:, 0:NE]))
                        e.wait_ge(s_ce.h, s_ce.n)
                    e.wait_ge(s_of.h, of0 + 1)
                    st(e.tensor_copy(out=offs16[:], in_=ps_r[0:16, 0:NE]))
                    s_of.inc(e.engine_nop())
                    assert P.sems["rt_v"].n == rvB0, (P.sems["rt_v"].n, rvB0)
                    for i in range(NTI):
                        e.wait_ge(s_obm.h, obm0 + i + 1)
                        st(e.tensor_tensor(out=offbc[:], in0=inc_all[:, NE * i:NE * (i + 1)],
                                           in1=psc[i % 2][:, 0:NE], op=ALU.add))
                        st(e.tensor_tensor(out=ra_[:], in0=oh_all[:, NE * i:NE * (i + 1)],
                                           in1=offbc[:], op=ALU.mult))
                        st(e.reduce_sum(out=krank[:], in_=ra_[:], axis=AX.X))
                        st(e.tensor_single_scalar(out=keep[:], in_=krank[:], scalar=CAP + 0.5, op=ALU.is_lt))
                        st(e.tensor_tensor(out=wcomb[:, i:i + 1], in0=tp_all[:, i:i + 1],
                                           in1=keep[:], op=ALU.mult))
                        st(e.tensor_tensor(out=ra_[:], in0=oh_all[:, NE * i:NE * (i + 1)],
                                           in1=ecol_sb[:], op=ALU.mult))
                        st(e.reduce_sum(out=ei_[:], in_=ra_[:], axis=AX.X))
                        st(e.tensor_scalar_mul(out=s2_[:], in0=ei_[:], scalar1=float(CAP)))
                        st(e.tensor_tensor(out=s3_[:], in0=s2_[:], in1=krank[:], op=ALU.add))
                        st(e.tensor_scalar_add(out=s2_[:], in0=s3_[:], scalar1=BIGF - 1.0))
                        st(e.scalar_tensor_tensor(out=sxf[:, i:i + 1], in0=keep[:], scalar=-BIGF,
                                                  in1=s2_[:], op0=ALU.mult, op1=ALU.add))
                        st(e.tensor_copy(out=sidxi[:, i:i + 1], in_=sxf[:, i:i + 1]))
                        assert P.sems["rt_v"].n == SCV[i]

                @blk.scalar
                def _(e):
                    for i in range(NTI):
                        e.wait_ge(s_rv.h, rv0 + 5 * i + 3)
                        s_ra.inc(e.activation(out=e_[:], in_=lg[:], func=AF.Exp, bias=negm[:, 0:1],
                                              accum_out=esum[:, 0:1]))

        s_gi = P.sem("a_wo")
        s_gd = P.sem("a_wod")
        s_gx = P.sem("a_pk")
        s_ge = P.sem("a_pe")
        s_gp = [P.sem("t_slA"), P.sem("t_slB")]
        gi0, gd0, gx0, ge0 = P.sem0("a_wo", "a_wod", "a_pk", "a_pe")
        gp0 = [P.sems["t_slA"].n, P.sems["t_slB"].n]
        with contextlib.ExitStack() as sc_mlp:
            sbg = lambda n, s, d: sc_mlp.enter_context(nc.sbuf_tensor(f"{tag}_{n}", s, d))
            xbuf = [sbg(f"xbuf{c}", [128, CAP], BF16) for c in range(NCH)]
            eout_fm = [sbg(f"eofm{c}", [128, CAP], BF16) for c in range(NCH)]
            with contextlib.ExitStack() as sc_g:
                sbg2 = lambda n, s, d: sc_g.enter_context(nc.sbuf_tensor(f"{tag}_{n}", s, d))
                tbli = sbg2("tbli", [128, NTBL], mybir.dt.int32)
                idtf = sbg2("idtf", [128, NTBL], F32)
                idti = sbg2("idti", [128, NTBL], mybir.dt.int32)
                gdt = [sbg2(f"gdt{p}", [128, 1024], BF16) for p in range(2)]
                psT2 = sc_g.enter_context(nc.psum_tensor(f"{tag}_psT2", [128, 1024], BF16))

                with nc.Block() as blk:
                    @blk.gpsimd
                    def _(e):
                        s_gi.inc(e.dma_start(out=tbli[:], in_=W["tbli"][:, :]), 16)
                        e.wait_ge(s_gi.h, gi0 + 16)
                        e.wait_ge(s_tar.h, tar0 + 1)
                        e.wait_ge(s_tg.h, tg0 + 1)
                        flat = H["tblo"].ap().rearrange("r one -> (r one)").rearrange("(a b) -> a b", b=1)
                        for j in range(NTBL):
                            s_gi.inc(e.indirect_dma_start(
                                out=idtf[:, j:j + 1], out_offset=None,
                                in_=flat, in_offset=bass.IndirectOffsetOnAxis(tbli[:, j:j + 1], 0)), 16)
                        for j in range(NTBL):
                            e.wait_ge(s_gd.h, gd0 + j + 1)
                            if j >= 2:
                                e.wait_ge(s_gx.h, gx0 + 8 * (j - 1))
                            s_gp[j % 2].inc(e.indirect_dma_start(
                                out=gdt[j % 2][:], out_offset=None,
                                in_=H["xtma"].ap(), in_offset=bass.IndirectOffsetOnAxis(idti[:, j:j + 1], 0)), 16)

                    @blk.vector
                    def _(e):
                        e.wait_ge(s_gi.h, gi0 + 16 * (NTBL + 1))
                        for j in range(NTBL):
                            s_gd.inc(e.tensor_copy(out=idti[:, j:j + 1], in_=idtf[:, j:j + 1]))
                            e.wait_ge(s_gd.h, s_gd.n)
                        for j in range(NTBL):
                            e.wait_ge(s_gx.h, gx0 + 8 * (j + 1))
                            for c in range(NCH):
                                s_ge.inc(e.tensor_copy(out=xbuf[c][:, 128 * j:128 * (j + 1)],
                                                       in_=psT2[:, 128 * c:128 * (c + 1)]))
                                e.wait_ge(s_ge.h, s_ge.n)

                    @blk.tensor
                    def _(e):
                        for j in range(NTBL):
                            e.wait_ge(s_gp[j % 2].h, gp0[j % 2] + 16 * (j // 2 + 1))
                            if j >= 1:
                                e.wait_ge(s_ge.h, ge0 + 8 * j)
                            for c in range(NCH):
                                s_gx.inc(e.transpose(psT2[:, 128 * c:128 * (c + 1)],
                                                     gdt[j % 2][:, 128 * c:128 * (c + 1)],
                                                     P.identb[:]))

            s_w = [P.sem("m2_wA"), P.sem("m2_wB")]
            s_wd = [P.sem("m2_wdA"), P.sem("m2_wdB")]
            s_b = [P.sem("t_wl"), P.sem("t_ms")]
            s_g = P.sem("m2_g")
            s_sl = P.sem("m2_sl")
            s_o = P.sem("m2_o")
            s_oc = P.sem("m2_oc")
            s_eb = P.sem("a_av")
            s_et = P.sem("a_aa")
            s_ee = P.sem("a_kve")
            s_ed = [P.sem("t_xlA"), P.sem("t_xlB")]
            s_eg = P.sem("t_ag2")
            w0 = [P.sems["m2_wA"].n, P.sems["m2_wB"].n]
            wdd0 = [P.sems["m2_wdA"].n, P.sems["m2_wdB"].n]
            b0 = [P.sems["t_wl"].n, P.sems["t_ms"].n]
            g0, sl0, o0, oc0, eb0, et0, ee0, eg0 = P.sem0(
                "m2_g", "m2_sl", "m2_o", "m2_oc", "a_av", "a_aa", "a_kve", "t_ag2")
            ed0 = [P.sems["t_xlA"].n, P.sems["t_xlB"].n]
            NG1 = 16
            NGF1 = 2
            with contextlib.ExitStack() as sc_m:
                sbq = lambda n, s, d: sc_m.enter_context(nc.sbuf_tensor(f"{tag}_{n}", s, d))
                w1_b = [sbq(f"w1b{p}", [128, NCH * 256], BF16) for p in range(2)]
                wd_b = [sbq(f"wdq{p}", [128, 8 * 1024], BF16) for p in range(2)]
                h_sb = sbq("h", [128, 16 * TT], BF16)
                acc = [sbq(f"acc{c}", [128, TT], F32) for c in range(NCH)]
                b1_sb = sbq("b1", [128, NFC], F32)
                b2_sb = sbq("b2", [128, NCH], F32)
                psG = [P.ps[0], P.ps[1]]
                ps8 = sc_m.enter_context(nc.psum_tensor(f"{tag}_ps8", [128, TT], F32))
                PSO = P.ps[:7] + [ps8]

                mode = WC["mode"] if WC is not None else None
                if WC is not None:
                    s_wb = P.sem("m1_wb")
                    s_wdb = P.sem("m1_wdb")
                    wb0 = P.sems["m1_wb"].n
                    wdb0 = P.sems["m1_wdb"].n

                with nc.Block() as blk:
                    if mode == "fill":
                        @blk.sync
                        def _(e):
                            for g in range(NG1):
                                e.wait_ge(s_w[g % 2].h, w0[g % 2] + 16 * (g // 2 + 1))
                                s_wb.inc(e.dma_start(out=WC["w1bf"].ap()[:, 2048 * g:2048 * (g + 1)],
                                                     in_=w1_b[g % 2][:]), 16)
                            for q4 in range(4):
                                e.wait_ge(s_wd[q4 % 2].h, wdd0[q4 % 2] + 16 * (q4 // 2 + 1))
                                s_wdb.inc(e.dma_start(out=WC["w2bf"].ap()[:, 8192 * q4:8192 * (q4 + 1)],
                                                      in_=wd_b[q4 % 2][:]), 16)

                    @blk.gpsimd
                    def _(e):
                        s_b[0].inc(e.dma_start(out=b1_sb[:], in_=W["b1tc"][:, :]), 16)
                        s_b[0].inc(e.dma_start(out=b2_sb[:], in_=W["b2tc"][:, :]), 16)
                        if mode == "use":
                            e.wait_ge(s_wb.h, WC["wb_w1"])
                            e.wait_ge(s_wdb.h, WC["wb_wd"])
                        for th in range(NTH1):
                            for hp in range(2):
                                for g8 in range(NG1 // 2):
                                    g = hp * (NG1 // 2) + g8
                                    gi = th * NG1 + g
                                    if gi >= 2:
                                        e.wait_ge(s_g.h, g0 + (gi - 1) * NGF1)
                                    if mode == "fill" and th == 0 and gi >= 2:
                                        e.wait_ge(s_wb.h, wb0 + 16 * (gi - 1))
                                    if mode == "fill" and th == 1 and g8 == 0 and hp == 0:
                                        e.wait_ge(s_wb.h, wb0 + 16 * NG1)
                                    fsl = slice(NGF1 * 128 * g, NGF1 * 128 * (g + 1))
                                    if mode == "use" or (mode == "fill" and th >= 1):
                                        s_w[gi % 2].inc(e.dma_start(
                                            out=w1_b[gi % 2][:],
                                            in_=WC["w1bf"].ap()[:, 2048 * g:2048 * (g + 1)]), 16)
                                    else:
                                        s_w[gi % 2].inc(e.dma_start(
                                            out=w1_b[gi % 2][:].rearrange("p (c f) -> p c f", c=NCH),
                                            in_=W["w1c"].rearrange("(c p) f -> p c f", p=128)[:, :, fsl]), 16)
                                for q in range(2):
                                    qi = (th * 2 + hp) * 2 + q
                                    if qi >= 2:
                                        e.wait_ge(s_o.h, o0 + 8 * (qi - 1))
                                    if mode == "fill" and th == 0 and qi >= 2:
                                        e.wait_ge(s_wdb.h, wdb0 + 16 * (qi - 1))
                                    if mode == "fill" and th == 1 and hp == 0 and q == 0:
                                        e.wait_ge(s_wdb.h, wdb0 + 64)
                                    if mode == "use" or (mode == "fill" and th >= 1):
                                        q4 = hp * 2 + q
                                        s_wd[qi % 2].inc(e.dma_start(
                                            out=wd_b[qi % 2][:],
                                            in_=WC["w2bf"].ap()[:, 8192 * q4:8192 * (q4 + 1)]), 16)
                                    else:
                                        s_wd[qi % 2].inc(e.dma_start(
                                            out=wd_b[qi % 2][:].rearrange("p (fc o) -> p fc o", fc=8),
                                            in_=W["w2c"].rearrange("(fc p) o -> p fc o", p=128)[:, 16 * hp + 8 * q:16 * hp + 8 * (q + 1), :]), 16)
                        if mode == "fill":
                            WC["wb_w1"] = P.sems["m1_wb"].n
                            WC["wb_wd"] = P.sems["m1_wdb"].n

                    @blk.tensor
                    def _(e):
                        e.wait_ge(s_b[0].h, b0[0] + 32)
                        gstage = 0
                        dstage = 0
                        for th in range(NTH1):
                            col = TT * th
                            for hp in range(2):
                                if th * 2 + hp >= 1:
                                    e.wait_ge(s_oc.h, oc0 + 8 * dstage)
                                for g in range(NG1 // 2):
                                    gi = th * NG1 + hp * (NG1 // 2) + g
                                    e.wait_ge(s_w[gi % 2].h, w0[gi % 2] + 16 * (gi // 2 + 1))
                                    if gi == 0:
                                        e.wait_ge(s_ge.h, ge0 + 8 * NTBL)
                                    for f2 in range(NGF1):
                                        fc = (hp * (NG1 // 2) + g) * NGF1 + f2
                                        spar = gstage % 2
                                        if gstage >= 2:
                                            e.wait_ge(s_sl.h, sl0 + gstage - 1)
                                        for dk in range(NCH):
                                            ii = e.matmul(psG[spar][:, :],
                                                          w1_b[gi % 2][:, 256 * dk + 128 * f2:256 * dk + 128 * f2 + 128],
                                                          xbuf[dk][:, col:col + TT],
                                                          start=(dk == 0), stop=(dk == NCH - 1))
                                            if dk == NCH - 1:
                                                s_g.inc(ii)
                                        gstage += 1
                                e.wait_ge(s_sl.h, sl0 + gstage)
                                for q in range(2):
                                    qi = (th * 2 + hp) * 2 + q
                                    e.wait_ge(s_wd[qi % 2].h, wdd0[qi % 2] + 16 * (qi // 2 + 1))
                                    if dstage >= 1:
                                        e.wait_ge(s_oc.h, oc0 + 8 * dstage)
                                    for fcl in range(8):
                                        fh = q * 8 + fcl
                                        for do in range(NCH):
                                            ii = e.matmul(PSO[do][:, :],
                                                          wd_b[qi % 2][:, 1024 * fcl + 128 * do:1024 * fcl + 128 * do + 128],
                                                          h_sb[:, TT * fh:TT * (fh + 1)],
                                                          start=(fcl == 0), stop=(fcl == 7))
                                            if fcl == 7:
                                                s_o.inc(ii)
                                    dstage += 1

                    @blk.scalar
                    def _(e):
                        gstage = 0
                        for th in range(NTH1):
                            for fc in range(NFC):
                                e.wait_ge(s_g.h, g0 + gstage + 1)
                                if gstage % 16 == 0 and gstage >= 16:
                                    e.wait_ge(s_o.h, o0 + 16 * (gstage // 16))
                                spar = gstage % 2
                                fh = fc % 16
                                s_sl.inc(e.activation(out=h_sb[:, TT * fh:TT * (fh + 1)],
                                                      in_=psG[spar][:, :], func=AF.Relu,
                                                      bias=b1_sb[:, fc:fc + 1]))
                                gstage += 1

                    @blk.vector
                    def _(e):
                        dstage = 0
                        for th in range(NTH1):
                            for hp in range(2):
                                for q in range(2):
                                    for do in range(NCH):
                                        e.wait_ge(s_o.h, o0 + 8 * dstage + do + 1)
                                        if hp == 0 and q == 0:
                                            s_oc.inc(e.tensor_copy(out=acc[do][:], in_=PSO[do][:, :]))
                                        else:
                                            s_oc.inc(e.tensor_tensor(out=acc[do][:], in0=acc[do][:],
                                                                     in1=PSO[do][:, :], op=ALU.add))
                                        e.wait_ge(s_oc.h, s_oc.n)
                                    dstage += 1
                            for c in range(NCH):
                                s_eb.inc(e.tensor_scalar_add(out=eout_fm[c][:, TT * th:TT * (th + 1)],
                                                             in0=acc[c][:],
                                                             scalar1=b2_sb[:, c:c + 1]))
                                e.wait_ge(s_eb.h, s_eb.n)

            with contextlib.ExitStack() as sc_e:
                emt = [sc_e.enter_context(nc.sbuf_tensor(f"{tag}_emt{p}", [128, 1024], BF16))
                       for p in range(2)]
                psT3 = sc_e.enter_context(nc.psum_tensor(f"{tag}_psT3", [128, 1024], BF16))
                with nc.Block() as blk:
                    @blk.tensor
                    def _(e):
                        for k in range(4 * NTH1):
                            th = k // 4
                            it = k % 4
                            e.wait_ge(s_eb.h, eb0 + 8 * (th + 1))
                            if k >= 1:
                                e.wait_ge(s_ee.h, ee0 + 8 * k)
                            for c in range(NCH):
                                s_et.inc(e.transpose(psT3[:, 128 * c:128 * (c + 1)],
                                                     eout_fm[c][:, 128 * k:128 * (k + 1)],
                                                     P.identb[:]))

                    @blk.vector
                    def _(e):
                        for k in range(4 * NTH1):
                            e.wait_ge(s_et.h, et0 + 8 * (k + 1))
                            if k >= 2:
                                e.wait_ge(s_ed[k % 2].h, ed0[k % 2] + 16 * (k // 2))
                            for c in range(NCH):
                                s_ee.inc(e.tensor_copy(out=emt[k % 2][:, 128 * c:128 * (c + 1)],
                                                       in_=psT3[:, 128 * c:128 * (c + 1)]))
                                e.wait_ge(s_ee.h, s_ee.n)

                    @blk.sync
                    def _(e):
                        for k in range(4 * NTH1):
                            e.wait_ge(s_ee.h, ee0 + 8 * (k + 1))
                            s_ed[k % 2].inc(e.dma_start(out=H["eoi"].ap()[128 * k:128 * (k + 1), :],
                                                 in_=emt[k % 2][:]), 16)

                    @blk.gpsimd
                    def _(e):
                        e.wait_ge(s_ed[0].h, ed0[0] + 16 * 2 * NTH1)
                        e.wait_ge(s_ed[1].h, ed0[1] + 16 * 2 * NTH1)
                        s_eg.inc(e.collective_compute(
                            "AllGather", ALU.bypass, replica_groups=RG,
                            ins=[H["eoi"].ap().opt()], outs=[H["eoa"].ap().opt()]), 1)

        s_bk = [P.sem("t_wsA"), P.sem("t_wsB")]
        s_bv = P.sem("t_wv")
        s_bt = P.sem("t_nm")
        s_by = P.sem("t_wm")
        s_xl = [P.sem("fin_lA"), P.sem("fin_lB")]
        s_ys = [P.sem("fin_sA"), P.sem("fin_sB")]
        bv0, bt0, by0 = P.sem0("t_wv", "t_nm", "t_wm")
        bk0 = [P.sems["t_wsA"].n, P.sems["t_wsB"].n]
        xl0 = [P.sems["fin_lA"].n, P.sems["fin_lB"].n]
        with contextlib.ExitStack() as sc_b:
            sbb = lambda n, s, d: sc_b.enter_context(nc.sbuf_tensor(f"{tag}_{n}", s, d))
            xf = [sbb(f"xf{c}", [128, TPC], F32) for c in range(NCH)]
            yrow = [sbb(f"yr{c}", [128, TPC], F32) for c in range(NCH)]
            g1 = [sbb(f"g1{p}", [128, 1024], BF16) for p in range(2)]
            gs = [sbb(f"gs{p}", [128, 1024], BF16) for p in range(2)]
            psT4 = sc_b.enter_context(nc.psum_tensor(f"{tag}_psT4", [128, 1024], BF16))

            with nc.Block() as blk:
                @blk.sync
                def _(e):
                    for c in range(NCH):
                        s_xl[c % 2].inc(e.dma_start(out=xf[c][:], in_=x_dram[128 * c:128 * (c + 1), :]), 16)

                @blk.gpsimd
                def _(e):
                    e.wait_ge(s_eg.h, eg0 + 1)
                    rbnd2 = e.alloc_register()
                    e.reg_mov(rbnd2, 8 * CAP - 1)
                    for i in range(NTI):
                        if i >= 2:
                            e.wait_ge(s_bv.h, bv0 + 2 * (i - 1))
                        s_bk[i % 2].inc(e.dma_start(out=g1[i % 2][:], in_=H["eoa"].ap()[0:128, :]), 16)
                        e.wait_ge(s_bk[i % 2].h, P.sems["t_wsA" if i % 2 == 0 else "t_wsB"].n)
                        s_bk[i % 2].inc(e.indirect_dma_start(
                            out=g1[i % 2][:], out_offset=None,
                            in_=H["eoa"].ap(), in_offset=bass.IndirectOffsetOnAxis(sidxi[:, i:i + 1], 0),
                            bounds_check=rbnd2, oob_is_err=False), 16)
                    e.free_register(rbnd2)
                    e.wait_ge(s_by.h, by0 + NCH * NTI)
                    for c in range(NCH):
                        s_ys[c % 2].inc(e.dma_start(out=y_dram[128 * c:128 * (c + 1), :], in_=yrow[c][:]), 16)
                    e.wait_ge(s_ys[0].h, P.sems["fin_sA"].n)
                    e.wait_ge(s_ys[1].h, P.sems["fin_sB"].n)

                @blk.vector
                def _(e):
                    e.wait_ge(s_xl[0].h, xl0[0] + 64)
                    e.wait_ge(s_xl[1].h, xl0[1] + 64)
                    for i in range(NTI):
                        e.wait_ge(s_bk[i % 2].h, bk0[i % 2] + 32 * (i // 2 + 1))
                        s_bv.inc(e.tensor_scalar_mul(out=gs[i % 2][:], in0=g1[i % 2][:],
                                                     scalar1=wcomb[:, i:i + 1]))
                        e.wait_ge(s_bv.h, s_bv.n)
                        s_bv.inc(e.engine_nop())
                        e.wait_ge(s_bt.h, bt0 + 8 * (i + 1))
                        for c in range(NCH):
                            s_by.inc(e.scalar_tensor_tensor(
                                out=yrow[c][:, 128 * i:128 * (i + 1)], in0=psT4[:, 128 * c:128 * (c + 1)],
                                scalar=0.5, in1=xf[c][:, 128 * i:128 * (i + 1)],
                                op0=ALU.mult, op1=ALU.add))
                            e.wait_ge(s_by.h, s_by.n)

                @blk.tensor
                def _(e):
                    for i in range(NTI):
                        e.wait_ge(s_bv.h, bv0 + 2 * i + 1)
                        if i >= 1:
                            e.wait_ge(s_by.h, by0 + NCH * i)
                        for c in range(NCH):
                            s_bt.inc(e.transpose(psT4[:, 128 * c:128 * (c + 1)],
                                                 gs[i % 2][:, 128 * c:128 * (c + 1)],
                                                 P.identb[:]))

    return P.sem("fin_sA")




def build_moe1s_prog():
    P = Prog()
    nc = P.nc
    x = nc.dram_tensor("x", [D, TPC], F32, kind="ExternalInput").ap()
    y = nc.dram_tensor("y", [D, TPC], F32, kind="ExternalOutput").ap()
    gwt = nc.dram_tensor("gwt", [D, NE], F32, kind="ExternalInput").ap()
    gbr = nc.dram_tensor("gbr", [1, NE], F32, kind="ExternalInput").ap()
    W = {
        "premask": nc.dram_tensor("premask", [128, 16], BF16, kind="ExternalInput").ap(),
        "gtidf": nc.dram_tensor("gtidf", [128, 16], F32, kind="ExternalInput").ap(),
        "tbli": nc.dram_tensor("tbli", [128, CAP // 128], mybir.dt.int32, kind="ExternalInput").ap(),
        "w1c": nc.dram_tensor("w1c", [D, FF], F32, kind="ExternalInput").ap(),
        "b1tc": nc.dram_tensor("b1tc", [128, NFC], F32, kind="ExternalInput").ap(),
        "w2c": nc.dram_tensor("w2c", [FF, D], F32, kind="ExternalInput").ap(),
        "b2tc": nc.dram_tensor("b2tc", [128, NCH], F32, kind="ExternalInput").ap(),
    }
    H = {
        "xtmi": nc.dram_tensor("xtmi", [TPC, D], BF16, kind="Internal"),
        "xtma": nc.dram_tensor("xtma", [T, D], BF16, kind="Internal"),
        "tbl": nc.dram_tensor("tbl", [8 * CAP, 1], F32, kind="Internal"),
        "tblo": nc.dram_tensor("tblo", [8 * CAP, 1], F32, kind="Internal"),
        "eoi": nc.dram_tensor("eoi", [CAP, D], BF16, kind="Internal"),
        "eoa": nc.dram_tensor("eoa", [8 * CAP, D], BF16, kind="Internal"),
        "t16": nc.dram_tensor("t16", [16, NE], F32, kind="Internal"),
        "gt": nc.dram_tensor("gt", [128, NE], F32, kind="Internal"),
    }
    emit_moe1_sparse(P, "s1", x, y, gwt, gbr, W, H)
    return P


def moe1s_core_inputs(c, gate_w, gate_b, w1, b1, w2, b2):
    gtidf = np.zeros((128, 16), np.float32)
    tbli = np.zeros((128, CAP // 128), np.int32)
    for i in range(16):
        for p in range(128):
            gtidf[p, i] = TPC * c + 128 * i + p
    for j in range(CAP // 128):
        for p in range(128):
            tbli[p, j] = CAP * c + 128 * j + p
    return {
        "gwt": np.ascontiguousarray(gate_w.T.astype(np.float32)),
        "gbr": np.ascontiguousarray(gate_b.astype(np.float32).reshape(1, NE)),
        "premask": premask_for_core(c),
        "gtidf": gtidf, "tbli": tbli,
        "w1c": np.ascontiguousarray(w1[c].astype(np.float32)),
        "b1tc": np.ascontiguousarray(b1[c].reshape(NFC, 128).T.astype(np.float32)),
        "w2c": np.ascontiguousarray(w2[c].astype(np.float32)),
        "b2tc": np.ascontiguousarray(b2[c].reshape(NCH, 128).T.astype(np.float32)),
    }


def run_moe1s(x_fm_shards, gate_w, gate_b, w1, b1, w2, b2):
    if "moe1s" not in _CACHE:
        _CACHE["moe1s"] = build_moe1s_prog()
    P = _CACHE["moe1s"]
    in_maps = []
    for c, xs in enumerate(x_fm_shards):
        m = moe1s_core_inputs(c, gate_w, gate_b, w1, b1, w2, b2)
        m["x"] = np.ascontiguousarray(xs)
        in_maps.append(m)
    res = run_bass_kernel_spmd(P.nc, in_maps, core_ids=list(range(NCORES)))
    return [r["y"] for r in res.results]

def build_full_prog():
    P = Prog()
    nc = P.nc
    DT = lambda n, s, d=F32, k="ExternalInput": nc.dram_tensor(n, s, d, kind=k)
    x0 = DT("x", [D, TPC]).ap()
    y = DT("y", [D, TPC], F32, "ExternalOutput").ap()
    mids = [DT(f"mid{i}", [D, TPC], F32, "Internal").ap() for i in range(5)]

    gwt = DT("gwt", [D, NE]).ap()
    gbr = DT("gbr", [1, NE]).ap()
    pm = DT("premask", [128, 16], BF16).ap()
    A0 = {}
    for n, s in (("wq", [D, D]), ("wk", [D, D]), ("wv", [D, D]), ("wo", [D, D]),
                 ("bqr", [1, D]), ("bkr", [1, D]), ("bvr", [1, D]),
                 ("bot", [128, NCH]), ("g1t", [128, NCH]), ("be1t", [128, NCH]),
                 ("g2t", [128, NCH]), ("be2t", [128, NCH]),
                 ("w1", [D, FF]), ("b1t", [128, NFC]), ("w2", [FF, D]), ("b2t", [128, NCH]),
                 ("bsel", [64, 2])):
        A0[n] = DT("i0_" + n, s).ap()
    for n, s in (("pre1", [D, TPC]), ("pre2", [D, TPC]), ("kvi", [64, 2048]), ("kvo", [64, 2048])):
        A0[n] = DT("i0_" + n, s, F32, "Internal")
    rwt = DT("rwt", [D, NE]).ap()
    wg = DT("wg", [NE, D, FF]).ap()
    wu = DT("wu", [NE, D, FF]).ap()
    wd = DT("wd", [NE, FF, D]).ap()
    T1 = {}
    for n, s in (("wqc", [D, 128]), ("wkc", [D, 128]), ("wvc", [D, 128]),
                 ("bqt", [128, 1]), ("bkt", [128, 1]), ("bvr", [1, 128]),
                 ("wout", [D, D]), ("boutt", [128, NCH])):
        T1[n] = DT("i1_" + n, s).ap()
    T1["ctxidx"] = DT("i1_ctxidx", [128, 8], mybir.dt.int32).ap()
    T1["xagi"] = DT("i1_xagi", [D, TPC], BF16, "Internal")
    T1["xago"] = DT("i1_xago", [NCORES * D, TPC], BF16, "Internal")
    T1["cagi"] = DT("i1_cagi", [128, T], BF16, "Internal")
    T1["cago"] = DT("i1_cago", [D, T], BF16, "Internal")
    WS = {
        "premask": pm,
        "gtidf": DT("gtidf", [128, 16]).ap(),
        "tbli": DT("tbli", [128, CAP // 128], mybir.dt.int32).ap(),
        "w1c": DT("w1c", [D, FF]).ap(),
        "b1tc": DT("b1tc", [128, NFC]).ap(),
        "w2c": DT("w2c", [FF, D]).ap(),
        "b2tc": DT("b2tc", [128, NCH]).ap(),
    }
    def mkH(u):
        return {
            "xtmi": DT(f"xtmi{u}", [TPC, D], BF16, "Internal"),
            "xtma": DT(f"xtma{u}", [T, D], BF16, "Internal"),
            "tbl": DT(f"tbl{u}", [8 * CAP, 1], F32, "Internal"),
            "tblo": DT(f"tblo{u}", [8 * CAP, 1], F32, "Internal"),
            "eoi": DT(f"eoi{u}", [CAP, D], BF16, "Internal"),
            "eoa": DT(f"eoa{u}", [8 * CAP, D], BF16, "Internal"),
            "t16": DT(f"t16{u}", [16, NE], F32, "Internal"),
            "gt": DT(f"gt{u}", [128, NE], F32, "Internal"),
        }

    WC = {"mode": "fill",
          "w1bf": DT("w1bfc", [128, 16 * 2048], BF16, "Internal"),
          "w2bf": DT("w2bfc", [128, 4 * 8192], BF16, "Internal")}
    emit_moe1_sparse(P, "m1a", x0, mids[0], gwt, gbr, WS, mkH("a"), WC=WC)
    WC["mode"] = "use"
    emit_l0attn(P, "a0", mids[0], mids[1], A0)
    emit_moe1_sparse(P, "m1b", mids[1], mids[2], gwt, gbr, WS, mkH("b"), WC=WC)
    emit_moe2(P, "m2a", mids[2], mids[3], rwt, wg, wu, wd)
    emit_l1attn(P, "t1", mids[3], mids[4], T1)
    emit_moe2(P, "m2b", mids[4], y, rwt, wg, wu, wd)
    return P


def full_inmaps(inputs, shards):
    g = lambda n: np.asarray(inputs[n], np.float32)
    t8 = lambda v: np.ascontiguousarray(v.reshape(NCH, 128).T.astype(np.float32))
    b1 = g("L0_ffn_b1")
    Wqkv = g("L1_attn_Wqkv")
    bqkv = g("L1_attn_bqkv")
    base = {
        "gwt": np.ascontiguousarray(g("L0_ffn_gate_w").T),
        "gbr": np.ascontiguousarray(g("L0_ffn_gate_b").reshape(1, NE)),
        "i0_wq": g("L0_attn_Wq"), "i0_wk": g("L0_attn_Wk"),
        "i0_wv": g("L0_attn_Wv"), "i0_wo": g("L0_attn_Wo"),
        "i0_bqr": g("L0_attn_bq").reshape(1, D),
        "i0_bkr": g("L0_attn_bk").reshape(1, D),
        "i0_bvr": g("L0_attn_bv").reshape(1, D),
        "i0_bot": t8(g("L0_attn_bo")),
        "i0_g1t": t8(g("L0_attn_g1")), "i0_be1t": t8(g("L0_attn_be1")),
        "i0_g2t": t8(g("L0_attn_g2")), "i0_be2t": t8(g("L0_attn_be2")),
        "i0_w1": g("L0_attn_W1"),
        "i0_b1t": np.ascontiguousarray(g("L0_attn_b1").reshape(NFC, 128).T),
        "i0_w2": g("L0_attn_W2"),
        "i0_b2t": t8(g("L0_attn_b2")),
        "rwt": np.ascontiguousarray(g("L1_ffn_router_w").T),
        "wg": g("L1_ffn_w_gate"), "wu": g("L1_ffn_w_up"), "wd": g("L1_ffn_w_down"),
        "i1_wout": g("L1_attn_Wout"),
        "i1_boutt": t8(g("L1_attn_bout")),
    }
    maps = []
    for c in range(NCORES):
        b = c // 4
        bsel = np.zeros((64, 2), np.float32)
        bsel[:, b] = 1.0
        idx = np.zeros((128, 8), np.int32)
        for rc in range(8):
            for p in range(128):
                idx[p, rc] = (128 * rc + p) * 8 + c
        m = dict(base)
        m["x"] = np.ascontiguousarray(shards[c])
        m["premask"] = premask_for_core(c)
        mc = moe1s_core_inputs(c, g("L0_ffn_gate_w"), g("L0_ffn_gate_b"),
                               g("L0_ffn_w1"), b1, g("L0_ffn_w2"), g("L0_ffn_b2"))
        for kk in ("gtidf", "tbli", "w1c", "b1tc", "w2c", "b2tc"):
            m[kk] = mc[kk]
        m["i0_bsel"] = bsel
        m["i1_wqc"] = np.ascontiguousarray(Wqkv[:, 128 * c:128 * (c + 1)])
        m["i1_wkc"] = np.ascontiguousarray(Wqkv[:, 1024 + 128 * c:1024 + 128 * (c + 1)])
        m["i1_wvc"] = np.ascontiguousarray(Wqkv[:, 2048 + 128 * c:2048 + 128 * (c + 1)])
        m["i1_bqt"] = np.ascontiguousarray(bqkv[128 * c:128 * (c + 1)].reshape(128, 1))
        m["i1_bkt"] = np.ascontiguousarray(bqkv[1024 + 128 * c:1024 + 128 * (c + 1)].reshape(128, 1))
        m["i1_bvr"] = np.ascontiguousarray(bqkv[2048 + 128 * c:2048 + 128 * (c + 1)].reshape(1, 128))
        m["i1_ctxidx"] = idx
        maps.append(m)
    return maps


def kernel(**inputs):
    x = np.asarray(inputs["L0_attn_x"], np.float32)
    xt = np.ascontiguousarray(x.reshape(T, D))
    shards = [np.ascontiguousarray(xt[i * TPC:(i + 1) * TPC].T) for i in range(NCORES)]
    if "full" not in _CACHE:
        _CACHE["full"] = build_full_prog()
    P = _CACHE["full"]
    res = run_bass_kernel_spmd(P.nc, full_inmaps(inputs, shards), core_ids=list(range(NCORES)))
    y = np.concatenate([np.asarray(r["y"]).T for r in res.results], axis=0)
    return y.reshape(2, 8192, 1024).astype(np.float32)
